# version 31
# baseline (speedup 1.0000x reference)
import os
import math
import contextlib
import numpy as np

import concourse.bass as bass
import concourse.mybir as mybir
from concourse.bass_utils import run_bass_kernel_spmd

F32 = mybir.dt.float32
BF16 = mybir.dt.bfloat16
AF = mybir.ActivationFunctionType
ALU = mybir.AluOpType
AX = mybir.AxisListType

S = 2048
DM = 1024
NH = 16
HD = 64
NST = 16
NDC = 8
NQT = 16
HID = 2048
NHC = 16
EPS = 1e-5

_tv = np.array([100.0, 0.0, 0.0, 0.0], np.float32)
_ex = np.exp(_tv - _tv.max(), dtype=np.float32)
_gk = (_ex / _ex.sum(dtype=np.float32)).astype(np.float32)
GATES = {1: float(_gk[0]), 0: float(_gk[1]), 2: float(_gk[2]), 3: float(_gk[3])}
EXPERTS = [0, 1, 2, 3]

ENGS = ("tensor", "scalar", "vector", "sync", "gpsimd")
KPHASE = int(os.environ.get("KPHASE", "99"))
KQT = int(os.environ.get("KQT", "16"))
KQT0 = int(os.environ.get("KQT0", "0"))
KNH = int(os.environ.get("KNH", "16"))
KSKIP = set(os.environ.get("KSKIP", "").split(","))


class Prog:

    def __init__(self):
        self.q = {e: [] for e in ENGS}
        self.cnt = {}
        self.lastwait = {}
        self.sems = None

    def addsem(self, *names):
        for n in names:
            self.cnt[n] = 0

    def op(self, eng, fn, inc=None):
        if inc is None:
            self.q[eng].append(fn)
            return None
        s, a = inc
        self.cnt[s] += a
        tgt = self.cnt[s]

        def run(e, f=fn, s=s, a=a, self_=self):
            ins = f(e)
            ins.then_inc(self_.sems[s], a)

        self.q[eng].append(run)
        return tgt

    def wait(self, eng, sem, val):
        if val is None or val <= 0:
            return
        assert val <= self.cnt[sem], f"wait target {val} > planned {self.cnt[sem]} for {sem}"
        if self.lastwait.get((eng, sem), 0) >= val:
            return
        self.lastwait[(eng, sem)] = val

        def run(e, s=sem, v=val, self_=self):
            e.wait_ge(self_.sems[s], v)

        self.q[eng].append(run)

    def replay(self, eng, e):
        for f in self.q[eng]:
            f(e)


def build_program():
    nc = bass.Bass("TRN2", target_bir_lowering=False, debug=False)

    x_d = nc.dram_tensor("x", [S, DM], F32, kind="ExternalInput").ap()
    wq_d = nc.dram_tensor("wq", [DM, DM], F32, kind="ExternalInput").ap()
    wk_d = nc.dram_tensor("wk", [DM, DM], F32, kind="ExternalInput").ap()
    wv_d = nc.dram_tensor("wv", [DM, DM], F32, kind="ExternalInput").ap()
    wo_d = nc.dram_tensor("wo", [DM, DM], F32, kind="ExternalInput").ap()
    bq_d = nc.dram_tensor("bq", [DM], F32, kind="ExternalInput").ap()
    bk_d = nc.dram_tensor("bk", [DM], F32, kind="ExternalInput").ap()
    bv_d = nc.dram_tensor("bv", [DM], F32, kind="ExternalInput").ap()
    bo_d = nc.dram_tensor("bo", [DM], F32, kind="ExternalInput").ap()
    w1_d = nc.dram_tensor("w1", [16, DM, HID], F32, kind="ExternalInput").ap()
    b1_d = nc.dram_tensor("b1", [16, HID], F32, kind="ExternalInput").ap()
    w2_d = nc.dram_tensor("w2", [16, HID, DM], F32, kind="ExternalInput").ap()
    b2_d = nc.dram_tensor("b2", [16, DM], F32, kind="ExternalInput").ap()
    y_d = nc.dram_tensor("y", [S, DM], F32, kind="ExternalOutput").ap()
    bsc_d = nc.dram_tensor("bsc", [DM], F32, kind="Internal").ap()

    prog = Prog()
    prog.addsem("lx0", "lx1", "lx2", "lx3", "lw", "lw1", "lw2", "lr", "lb", "st", "pev", "pe", "act", "dve", "pool")

    with contextlib.ExitStack() as stack:
        E = stack.enter_context

        SBUF_BASE = 16512
        cur = [0]

        def at(name, shape, dtype, off=None, adv=True):
            nbytes = (shape[1] * (4 if dtype == F32 else 2) + 31) // 32 * 32
            if off is None:
                off = cur[0]
                if adv:
                    cur[0] += nbytes
            h = nc.alloc_sbuf_tensor_at(name, shape, dtype, offset=SBUF_BASE + off)
            return h

        x_tm = at("x_tm", [128, NST * 1024], F32)
        off_xnv = cur[0]
        xnv = at("xnv", [128, 8 * 2048], BF16)
        off_qa = cur[0]
        qa = at("qa", [128, 8 * 2048], BF16)
        off_ka = cur[0]
        ka = at("ka", [128, 16 * 1024], BF16)
        off_c1 = cur[0]
        c1 = at("c1", [128, 8192], BF16)
        qnkn = at("qnkn", [16, 4096], F32, off=off_c1, adv=False)
        cur[0] = off_c1 + 16384
        off_c2 = cur[0]
        c2 = at("c2", [128, 8192], BF16)
        b2t = at("b2t", [128, 1024], F32, off=off_c2, adv=False)
        atok = at("atok", [128, 2048], BF16)
        attT = at("attT", [128, 1024], BF16, off=off_c1 + 12288, adv=False)
        idb = at("idb", [128, 128], BF16)
        idf = at("idf", [16, 16], F32)
        trimask = at("trimask", [128, 128], BF16)
        negut = at("negut", [128, 128], BF16)
        bias_sb = at("bias_sb", [128, 256], F32)
        lncols = at("lncols", [128, 112], F32)
        swc = at("swc", [128, 8], F32)
        bqcol = at("bqcol", [128, 8], F32)
        bkcol = at("bkcol", [128, 8], F32)
        b1col = at("b1col", [128, 32], F32)
        qkm = at("qkm", [128, 256], BF16)
        ones_r = at("ones_r", [1, 128], BF16)
        ones_c = at("ones_c", [128, 1], BF16)
        onesf16 = at("onesf16", [1, 16], F32)
        onesf_col = at("onesf_col", [1, 128], F32)
        bvrow = at("bvrow", [1, 1024], BF16)
        borow = at("borow", [1, 1024], BF16)
        b2row = at("b2row", [1, 1024], BF16)
        ksrow = at("ksrow", [1, 16], F32)
        kcol16 = at("kcol16", [16, 1], F32)
        kmax1 = at("kmax1", [1, 1], F32)
        kneg1 = at("kneg1", [1, 1], F32)
        kapcol = at("kapcol", [16, 1], F32)
        bstage = at("bstage", [16, 128], F32)
        b1stage = at("b1stage", [16, 256], F32)
        assert cur[0] <= 212830, f"sbuf over budget: {cur[0]}"

        P0 = E(nc.psum_tensor("P0", [128, 1536], F32))
        P1 = E(nc.psum_tensor("P1", [128, 1536], F32))
        PTa = E(nc.psum_tensor("PTa", [128, 1024], BF16))
        P2 = E(nc.psum_tensor("P2", [128, 512], F32))

        sems = {}
        for n in ("lx0", "lx1", "lx2", "lx3", "lw", "lw1", "lw2", "lr", "lb", "st",
                  "pev", "pe", "act", "dve", "pool"):
            sems[n] = E(nc.semaphore(n))
        prog.sems = sems

        plan(prog, locals())

        block = E(nc.Block())

        @block.tensor
        def _(e):
            prog.replay("tensor", e)

        @block.scalar
        def _(e):
            prog.replay("scalar", e)

        @block.vector
        def _(e):
            prog.replay("vector", e)

        @block.sync
        def _(e):
            prog.replay("sync", e)

        @block.gpsimd
        def _(e):
            prog.replay("gpsimd", e)

    return nc


def plan(prog, ns):
    P = prog.op
    W = prog.wait
    x_d, y_d, bsc_d = ns["x_d"], ns["y_d"], ns["bsc_d"]
    wq_d, wk_d, wv_d, wo_d = ns["wq_d"], ns["wk_d"], ns["wv_d"], ns["wo_d"]
    bq_d, bk_d, bv_d, bo_d = ns["bq_d"], ns["bk_d"], ns["bv_d"], ns["bo_d"]
    w1_d, b1_d, w2_d, b2_d = ns["w1_d"], ns["b1_d"], ns["w2_d"], ns["b2_d"]
    x_tm, xnv, qa, ka, c1, qnkn, c2 = (
        ns["x_tm"], ns["xnv"], ns["qa"], ns["ka"], ns["c1"], ns["qnkn"], ns["c2"])
    atok, attT, idb, idf, trimask = ns["atok"], ns["attT"], ns["idb"], ns["idf"], ns["trimask"]
    negut = ns["negut"]
    b2t = ns["b2t"]
    bias_sb, lncols, swc = ns["bias_sb"], ns["lncols"], ns["swc"]
    bqcol, bkcol, b1col, qkm = ns["bqcol"], ns["bkcol"], ns["b1col"], ns["qkm"]
    ones_r, onesf16, ones_c = ns["ones_r"], ns["onesf16"], ns["ones_c"]
    onesf_col = ns["onesf_col"]
    bvrow, borow, b2row = ns["bvrow"], ns["borow"], ns["b2row"]
    ksrow, kcol16, kmax1, kneg1, kapcol = (
        ns["ksrow"], ns["kcol16"], ns["kmax1"], ns["kneg1"], ns["kapcol"])
    bstage, b1stage = ns["bstage"], ns["b1stage"]
    P0, P1, PTa, P2 = ns["P0"], ns["P1"], ns["PTa"], ns["P2"]

    def mm(out, lhsT, rhs, start, stop, inc=None):
        return P("tensor",
                 lambda e, o=out, l=lhsT, r=rhs, s=start, t=stop:
                 e.matmul(o, l, r, start=s, stop=t), inc)

    def tr(out, in_, ident, inc=None):
        return P("tensor",
                 lambda e, o=out, i=in_, d=ident: e.transpose(out=o, in_=i, identity=d), inc)

    def act(out, in_, func, bias=0.0, scale=1.0, accum=None, inc=None):
        return P("scalar",
                 lambda e, o=out, i=in_, f=func, b=bias, s=scale, a=accum:
                 e.activation(out=o, in_=i, func=f, bias=b, scale=s, accum_out=a), inc)

    def dcopy(out, in_, inc=None):
        return P("vector", lambda e, o=out, i=in_: e.tensor_copy(out=o, in_=i), inc)

    def acopy(out, in_, inc=None):
        return P("scalar", lambda e, o=out, i=in_: e.copy(out, in_) if False else e.copy(o, i), inc)

    def dtt(out, in0, in1, aop, inc=None):
        return P("vector",
                 lambda e, o=out, a=in0, b=in1, p=aop: e.tensor_tensor(out=o, in0=a, in1=b, op=p),
                 inc)

    def dts(out, in0, scalar, aop, inc=None, accum=None):
        return P("vector",
                 lambda e, o=out, a=in0, s=scalar, p=aop, ac=accum:
                 e.tensor_scalar(out=o, in0=a, scalar1=s, scalar2=None, op0=p, accum_out=ac), inc)

    def dstt(out, in0, scalar, in1, op0, op1, accum=None, inc=None):
        return P("vector",
                 lambda e, o=out, a=in0, s=scalar, b=in1, p0=op0, p1=op1, ac=accum:
                 e.scalar_tensor_tensor(out=o, in0=a, scalar=s, in1=b, op0=p0, op1=p1,
                                        accum_out=ac), inc)

    def dred(out, in_, aop, inc=None):
        return P("vector",
                 lambda e, o=out, i=in_, p=aop: e.tensor_reduce(out=o, in_=i, axis=AX.X, op=p),
                 inc)

    def drecip(out, in_, inc=None):
        return P("vector", lambda e, o=out, i=in_: e.reciprocal(out=o, in_=i), inc)

    def dma(eng, out, in_, inc):
        return P(eng, lambda e, o=out, i=in_: e.dma_start(out=o, in_=i), inc)

    def gmemset(apx, v, inc=None):
        return P("gpsimd", lambda e, a=apx, vv=v: e.memset(a, vv), inc)

    ptring = {"idx": 0, "free": [("dve", 0)], "bank": None}

    def pt_group(srcs, mk_evac, waits=(), evac="vector"):
        b = ptring["idx"]
        ptring["idx"] = (b + 1) % len(ptring["bank"])
        PTx = ptring["bank"][b]
        fs, fv = ptring["free"][b]
        W("tensor", fs, fv)
        for sem, val in waits:
            W("tensor", sem, val)
        t_last = 0
        for i, s in enumerate(srcs):
            t_last = tr(PTx[:, i * 128:(i + 1) * 128], s, idb[:],
                        inc=("pe", 1) if i == len(srcs) - 1 else None)
        out_ap, in_ap = mk_evac(PTx, len(srcs))
        if evac == "act":
            W("scalar", "pe", t_last)
            ev = ("act", acopy(out_ap, in_ap, inc=("act", 1)))
        else:
            W("vector", "pe", t_last)
            ev = ("dve", dcopy(out_ap, in_ap, inc=("dve", 1)))
        ptring["free"][b] = ev
        return t_last, ev

    ptring["bank"] = [PTa]

    m = gmemset(idb[:], 0.0, inc=("pool", 1))
    W("gpsimd", "pool", m)
    P("gpsimd", lambda e: e.affine_select(out=idb[:], in_=idb[:], compare_op=ALU.not_equal,
                                          fill=1.0, base=0, pattern=[[-1, 128]],
                                          channel_multiplier=1))
    m = gmemset(idf[:], 0.0, inc=("pool", 1))
    W("gpsimd", "pool", m)
    P("gpsimd", lambda e: e.affine_select(out=idf[:], in_=idf[:], compare_op=ALU.not_equal,
                                          fill=1.0, base=0, pattern=[[-1, 16]],
                                          channel_multiplier=1))
    m = gmemset(trimask[:], 1.0, inc=("pool", 1))
    W("gpsimd", "pool", m)
    P("gpsimd", lambda e: e.affine_select(out=trimask[:], in_=trimask[:], compare_op=ALU.is_ge,
                                          fill=0.0, base=0, pattern=[[1, 128]],
                                          channel_multiplier=-1))
    m = gmemset(negut[:], -480.0, inc=("pool", 1))
    W("gpsimd", "pool", m)
    P("gpsimd", lambda e: e.affine_select(out=negut[:], in_=negut[:], compare_op=ALU.is_ge,
                                          fill=0.0, base=-1, pattern=[[1, 128]],
                                          channel_multiplier=-1))
    gmemset(ones_c[:], 1.0)
    gmemset(onesf_col[:], 1.0)
    gmemset(ones_r[:], 1.0)
    gmemset(onesf16[:], 1.0)
    m = gmemset(qkm[:], 0.0, inc=("pool", 1))
    W("gpsimd", "pool", m)
    for g in range(8):
        gmemset(qkm[0:64, g * 16 + 2 * g: g * 16 + 2 * g + 1], 0.0625)
        gmemset(qkm[64:128, g * 16 + 2 * g + 1: g * 16 + 2 * g + 2], 0.0625)
        gmemset(qkm[0:64, 128 + g * 16 + 2 * g: 128 + g * 16 + 2 * g + 1], 0.0625)
        last = gmemset(qkm[64:128, 128 + g * 16 + 2 * g + 1: 128 + g * 16 + 2 * g + 2],
                       0.0625, inc=("pool", 1) if g == 7 else None)
    consts_pool = prog.cnt["pool"]

    dma("gpsimd", bvrow[:], bv_d[None, :], ("lr", 16))
    dma("gpsimd", borow[:], bo_d[None, :], ("lr", 16))
    borow_lr = prog.cnt["lr"]

    x_r = x_d.rearrange("(t p) d -> p t d", p=128)
    xload = []
    for t in range(4):
        dma("sync", x_tm[:, t * 4096:(t + 1) * 4096].rearrange("p (a d) -> p a d", d=1024),
            x_r[:, 4 * t:4 * t + 4, :], (f"lx{t}", 16))
        xload.append((f"lx{t}", prog.cnt[f"lx{t}"]))
    dma("sync", bstage[0:8, :], bq_d.rearrange("(t p) -> t p", p=128), ("lb", 16))
    bq_lb = prog.cnt["lb"]

    def wload(wd):
        dma("gpsimd", c1[:, :].rearrange("p (ot d) -> p ot d", d=1024),
            wd.rearrange("(ot p) d -> p ot d", p=128), ("lw", 16))
        return prog.cnt["lw"]

    wq_lw = wload(wq_d)

    SC, SQ, NM, MSQ, VAR, RSTD, LNV = 0, 16, 32, 48, 64, 80, 96

    def ln_stats(st, entry_pe, entry_dve, prev_stt, wait_lx, sbuf_discard=False):
        if wait_lx:
            W("vector", xload[st // 4][0], xload[st // 4][1])
        W("vector", "pe", entry_pe)
        W("vector", "dve", entry_dve)
        dred(lncols[:, SC + st:SC + st + 1], x_tm[:, st * 1024:(st + 1) * 1024], ALU.add)
        W("vector", "dve", prev_stt)
        out_ap = c1[:, 0:1024] if sbuf_discard else P0[:, 0:1024]
        return dstt(out_ap, x_tm[:, st * 1024:(st + 1) * 1024], 1.0,
                    x_tm[:, st * 1024:(st + 1) * 1024], ALU.mult, ALU.mult,
                    accum=lncols[:, SQ + st:SQ + st + 1], inc=("dve", 1))

    def layer_norm(entry_pe, entry_dve, wait_lx, entry_act=0, stats_done=None):
        grouped = stats_done is None
        evac_dve = []
        prev_ts = 0
        prev_evac = 0
        prev_tr = 0
        tr_half = {0: 0, 1: 0}
        prev_stt = 0
        ngrp = 4 if grouped else 1
        gw = 16 // ngrp
        for grp in range(ngrp):
            c0 = grp * gw
            if grouped:
                for st in range(c0, c0 + gw):
                    prev_stt = ln_stats(st, entry_pe, entry_dve, prev_stt, wait_lx)
                sd = prev_stt
            else:
                sd = stats_done
            W("vector", "dve", sd)
            m1 = dts(lncols[:, NM + c0:NM + c0 + gw], lncols[:, SC + c0:SC + c0 + gw],
                     -1.0 / 1024.0, ALU.mult, inc=("dve", 1))
            W("vector", "dve", m1)
            m2 = dstt(lncols[:, MSQ + c0:MSQ + c0 + gw], lncols[:, NM + c0:NM + c0 + gw],
                      1.0, lncols[:, NM + c0:NM + c0 + gw], ALU.mult, ALU.mult,
                      inc=("dve", 1))
            m3 = dts(lncols[:, SQ + c0:SQ + c0 + gw], lncols[:, SQ + c0:SQ + c0 + gw],
                     1.0 / 1024.0, ALU.mult, inc=("dve", 1))
            W("vector", "dve", m3)
            m4 = dstt(lncols[:, VAR + c0:VAR + c0 + gw], lncols[:, SQ + c0:SQ + c0 + gw],
                      EPS, lncols[:, MSQ + c0:MSQ + c0 + gw], ALU.add, ALU.subtract,
                      inc=("dve", 1))
            W("scalar", "dve", m4)
            a1 = act(lncols[:, LNV + c0:LNV + c0 + gw], lncols[:, VAR + c0:VAR + c0 + gw],
                     AF.Ln, bias=0.0, scale=1.0, inc=("act", 1))
            W("scalar", "act", a1)
            a2 = act(lncols[:, RSTD + c0:RSTD + c0 + gw], lncols[:, LNV + c0:LNV + c0 + gw],
                     AF.Exp, bias=0.0, scale=-0.5, inc=("act", 1))
            W("vector", "act", a2)
            m5 = dtt(lncols[:, MSQ + c0:MSQ + c0 + gw], lncols[:, NM + c0:NM + c0 + gw],
                     lncols[:, RSTD + c0:RSTD + c0 + gw], ALU.mult, inc=("dve", 1))
            for st in range(c0, c0 + gw):
                half = st % 2
                xn_tmp = atok[:, half * 1024:(half + 1) * 1024]
                W("scalar", "dve", m5)
                W("scalar", "pe", tr_half[half])
                prev_ts = act(xn_tmp[:, 0:1024], x_tm[:, st * 1024:(st + 1) * 1024],
                              AF.Identity, bias=lncols[:, MSQ + st:MSQ + st + 1],
                              scale=lncols[:, RSTD + st:RSTD + st + 1], inc=("act", 1))
                waits = [("act", prev_ts), ("pool", consts_pool)]
                if entry_act:
                    waits.append(("act", entry_act))
                prev_tr, ev = pt_group(
                    [xn_tmp[:, dc * 128:(dc + 1) * 128] for dc in range(NDC)],
                    lambda PTx, n, st=st: (
                        xnv[:, :].rearrange("p (dc c) -> p dc c", c=2048)[
                            :, :, st * 128:(st + 1) * 128],
                        PTx[:, :].rearrange("p (dc c) -> p dc c", c=128)),
                    waits=waits)
                tr_half[half] = prev_tr
                prev_evac = ev[1]
                evac_dve.append(prev_evac)
        return evac_dve, prev_tr

    def early_store(dve_gate):
        for st in range(NST):
            W("sync", "dve", dve_gate)
            dma("sync", y_d.rearrange("(t p) d -> p t d", p=128)[:, st:st + 1, :],
                x_tm[:, st * 1024:(st + 1) * 1024].rearrange(
                    "p (a d) -> p a d", a=1), ("st", 16))
        W("sync", "st", prog.cnt["st"])

    ln1_evac, ln1_tr = layer_norm(0, 0, wait_lx=True)
    if KPHASE == 1:
        early_store(prog.cnt["dve"])
        return

    def build_wt(load_lw):
        tr_last = 0
        ev = 0
        for dc in range(NDC):
            tr_last, ev = pt_group(
                [c1[:, ot * 1024 + dc * 128: ot * 1024 + (dc + 1) * 128]
                 for ot in range(8)],
                lambda PTx, n, dc=dc: (c2[:, dc * 1024:(dc + 1) * 1024], PTx[:, :]),
                waits=[("lw", load_lw)])
        return tr_last, ev[1]

    W("tensor", "lb", bq_lb)
    tbq = tr(P1[:, 1168:1176], bstage[0:8, :], idf[0:8, 0:8], inc=("pe", 1))
    W("vector", "pe", tbq)
    ebq = dcopy(bqcol[:, 0:8], P1[:, 1168:1176], inc=("dve", 1))
    W("sync", "pe", tbq)
    dma("sync", bstage[0:8, :], bk_d.rearrange("(t p) -> t p", p=128), ("lb", 16))
    bk_lb = prog.cnt["lb"]

    wq_tr, wq_evac = build_wt(wq_lw)
    W("tensor", "lb", bk_lb)
    W("tensor", "dve", ebq)
    tbk = tr(P1[:, 1168:1176], bstage[0:8, :], idf[0:8, 0:8], inc=("pe", 1))
    W("vector", "pe", tbk)
    ebk = dcopy(bkcol[:, 0:8], P1[:, 1168:1176], inc=("dve", 1))

    W("sync", "pe", tbk)
    comb = 0
    first = True
    for e in EXPERTS:
        slot = b1stage[0:8, 0:128] if first else b1stage[0:8, 128:256]
        dma("sync", slot, b2_d[e, :].rearrange("(t p) -> t p", p=128), ("lb", 16))
        l_ = prog.cnt["lb"]
        W("vector", "lb", l_)
        W("vector", "dve", comb)
        m_ = dts(slot, slot, GATES[e], ALU.mult, inc=("dve", 1))
        if first:
            comb = m_
            first = False
        else:
            W("vector", "dve", m_)
            comb = dtt(b1stage[0:8, 0:128], b1stage[0:8, 0:128], b1stage[0:8, 128:256],
                       ALU.add, inc=("dve", 1))
        if e != EXPERTS[-1]:
            W("sync", "dve", comb)
    W("sync", "dve", comb)
    dma("sync", bsc_d.rearrange("(t p) -> t p", p=128), b1stage[0:8, 0:128], ("lb", 16))
    bounce_lb = prog.cnt["lb"]

    W("gpsimd", "pe", wq_tr)
    wk_lw = wload(wk_d)

    def proj_fm(dest, dest_off, bcol, ln_evacs, w_evac, entry_act=0):
        prev_ev = {0: entry_act, 1: entry_act}
        last_mm = 0
        last_ev = 0
        for sc in range(4):
            for ot in range(8):
                reg = (sc * 8 + ot) % 2
                W("tensor", "dve", w_evac)
                W("tensor", "dve", ln_evacs[sc * 4 + 3])
                W("tensor", "act", prev_ev[reg])
                for dc in range(NDC):
                    last_mm = mm(P1[:, reg * 512:(reg + 1) * 512],
                                 c2[:, dc * 1024 + ot * 128: dc * 1024 + (ot + 1) * 128],
                                 xnv[:, dc * 2048 + sc * 512: dc * 2048 + (sc + 1) * 512],
                                 start=(dc == 0), stop=(dc == NDC - 1),
                                 inc=("pe", 1) if dc == NDC - 1 else None)
                W("scalar", "pe", last_mm)
                prev_ev[reg] = act(dest[:, ot * 2048 + sc * 512: ot * 2048 + (sc + 1) * 512],
                                   P1[:, reg * 512:(reg + 1) * 512], AF.Identity,
                                   bias=bcol[:, ot:ot + 1], scale=1.0, inc=("act", 1))
                last_ev = prev_ev[reg]
        return last_mm, last_ev

    W("scalar", "dve", ebq)
    q_mm, q_ev = proj_fm(qa, 0, bqcol, ln1_evac, wq_evac)

    wk_tr, wk_evac = build_wt(wk_lw)
    W("gpsimd", "pe", wk_tr)
    wv_lw = wload(wv_d)

    W("scalar", "dve", ebk)
    k_mm, k_ev = proj_fm(ka, 0, bkcol, ln1_evac, wk_evac, entry_act=q_ev)

    wv_tr, wv_evac = build_wt(wv_lw)
    W("gpsimd", "pe", wv_tr)
    wo_lw = wload(wo_d)

    vregions = [(P1, 0), (P1, 512), (P0, 0), (P0, 512)]
    v_prev_ev = {0: 0, 1: 0, 2: 0, 3: 0}
    v_last_ev = 0
    v_last_mm = 0
    for st in range(NST):
        grp = []
        for half in range(2):
            r = (st * 2 + half) % 4
            pbuf, poff = vregions[r]
            W("tensor", "dve", wv_evac)
            W("tensor", "dve", v_prev_ev[r])
            W("tensor", "lr", borow_lr)
            if st == 0 and half == 0:
                W("tensor", "pe", k_mm)
                W("tensor", "act", k_ev)
            for dc in range(NDC):
                mm(pbuf[:, poff:poff + 512],
                   xnv[:, dc * 2048 + st * 128: dc * 2048 + (st + 1) * 128],
                   c2[:, dc * 1024 + half * 512: dc * 1024 + half * 512 + 512],
                   start=(dc == 0), stop=False)
            v_last_mm = mm(pbuf[:, poff:poff + 512], ones_r[:],
                           bvrow[:, half * 512:half * 512 + 512],
                           start=False, stop=True, inc=("pe", 1))
            grp.append((r, pbuf, poff))
        for half in range(2):
            r, pbuf, poff = grp[half]
            W("vector", "pe", v_last_mm)
            dest = xnv[:, :].rearrange("p (dc c) -> p dc c", c=2048)[
                :, 4 * half:4 * half + 4, st * 128:(st + 1) * 128]
            src = pbuf[:, poff:poff + 512].rearrange("p (dc c) -> p dc c", c=128)
            v_prev_ev[r] = dcopy(dest, src, inc=("dve", 1))
            v_last_ev = v_prev_ev[r]

    wo_tr, wo_evac = build_wt(wo_lw)
    if KPHASE == 2:
        early_store(prog.cnt["dve"])
        return

    scr_free = {0: ln1_tr, 1: wo_tr}

    def scr_ap(b, lo, hi):
        return atok[:, lo:hi] if b == 0 else c1[:, 4096 + lo:4096 + hi]

    def norms_prep(src_t, mask_coloff, psum_dve_entry):
        last_mm = 0
        for g in range(8):
            b = g % 2
            W("vector", "pe", scr_free[b])
            W("vector", "dve", v_last_ev)
            tmul = dtt(scr_ap(b, 0, 2048), src_t[:, g * 2048:(g + 1) * 2048],
                       src_t[:, g * 2048:(g + 1) * 2048], ALU.mult, inc=("dve", 1))
            W("tensor", "dve", tmul)
            if g == 0:
                W("tensor", "dve", psum_dve_entry)
            for c4 in range(4):
                out_ap = (P0[0:16, c4 * 512:(c4 + 1) * 512] if c4 < 3
                          else P1[0:16, 0:512])
                last_mm = mm(out_ap,
                             qkm[:, mask_coloff + g * 16: mask_coloff + (g + 1) * 16],
                             scr_ap(b, c4 * 512, (c4 + 1) * 512),
                             start=(g == 0), stop=(g == 7),
                             inc=("pe", 1) if c4 == 3 else None)
            scr_free[b] = last_mm
        return last_mm

    qn_mm = norms_prep(qa, 0, v_last_ev)
    W("vector", "pe", qn_mm)
    dcopy(qnkn[0:16, 0:1536], P0[0:16, 0:1536], inc=("dve", 1))
    qn_ev = dcopy(qnkn[0:16, 1536:2048], P1[0:16, 0:512], inc=("dve", 1))
    kn_mm = norms_prep(ka, 128, qn_ev)
    W("vector", "pe", kn_mm)
    kn_p0 = dcopy(qnkn[0:16, 2048:2048 + 1536], P0[0:16, 0:1536], inc=("dve", 1))
    kn_ev = dcopy(qnkn[0:16, 2048 + 1536:4096], P1[0:16, 0:512], inc=("dve", 1))
    W("vector", "dve", kn_ev)
    r1 = dred(kcol16[0:16, 0:1], qnkn[0:16, 0:2048], ALU.max, inc=("dve", 1))
    W("vector", "dve", r1)
    r2 = dred(kapcol[0:16, 0:1], qnkn[0:16, 2048:4096], ALU.max, inc=("dve", 1))
    W("vector", "dve", r2)
    r3 = dtt(kcol16[0:16, 0:1], kcol16[0:16, 0:1], kapcol[0:16, 0:1], ALU.add,
             inc=("dve", 1))
    W("vector", "dve", r3)
    ng = dts(kcol16[0:16, 0:1], kcol16[0:16, 0:1], -1.0, ALU.mult, inc=("dve", 1))
    W("tensor", "dve", ng)
    t1 = tr(P1[0:1, 1168:1184], kcol16[0:16, 0:1], idf[0:16, 0:16], inc=("pe", 1))
    W("vector", "pe", t1)
    c1m = dcopy(ksrow[:, :], P1[0:1, 1168:1184], inc=("dve", 1))
    W("tensor", "dve", c1m)
    bc = mm(P1[:, 1280:1296], onesf_col[:, :], ksrow[:, :],
            start=True, stop=True, inc=("pe", 1))
    W("vector", "pe", bc)
    bias_ev = dcopy(bias_sb[:, 0:16], P1[:, 1280:1296], inc=("dve", 1))

    p_sb = [c1[:, 0:2048], c1[:, 2048:4096]]
    pT_sb = [c1[:, 4096:6144], c1[:, 6144:8192]]
    att_slot = [0, 64]
    CHUNK = 1536

    exp_done = [(0, 0, 0, 0)] * (NH + 2)
    pT_free = {0: 0, 1: 0}
    p_dve_free = {0: 0, 1: 0}
    attslot_free = {0: ("dve", 0), 1: ("dve", kn_p0)}
    ae_last = 0
    exp_bank = {0: ("dve", kn_p0), 1: ("dve", kn_p0), 2: ("dve", kn_p0)}
    atok_free = {0: kn_mm, 1: kn_mm}
    wo_done_prev = 0
    resid_done_prev = 0
    resid_of = {}
    ln2stt = {}
    ln2stt = {}
    last_exp = 0
    last_exp2 = 0
    qk_last = 0
    qk_of_h = {}

    def stage_A1(qt, h):
        nonlocal qk_last
        L = (qt + 1) * 128
        L1 = min(L, CHUNK)
        two = L > CHUNK
        dbuf = L <= 512
        soff = (h % 2) * 1024 if dbuf else 0
        par = h % 2
        g = h // 2
        hp = (h % 2) * 64
        bias_ap = bias_sb[:, h:h + 1]
        rhsq = qa[hp:hp + 64, g * 2048 + qt * 128: g * 2048 + (qt + 1) * 128]

        def st_tiles(lo, hi, off, war):
            nonlocal qk_last
            W("tensor", war[0], war[1])
            for jc in range(lo // 128, hi // 128):
                qk_last = mm(P0[:, off + (jc * 128 - lo): off + (jc * 128 - lo) + 128],
                             ka[hp:hp + 64, g * 2048 + jc * 128: g * 2048 + (jc + 1) * 128],
                             rhsq, start=True, stop=True,
                             inc=("pe", 1) if (jc + 1) * 128 >= hi else None)
            return qk_last

        def exp_chunk(lo, hi, off):
            W("scalar", "pe", qk_last)
            W("scalar", "dve", bias_ev)
            W("scalar", "dve", p_dve_free[par])
            return act(pT_sb[par][:, lo:hi], P0[:, off:off + (hi - lo)], AF.Exp,
                       bias=bias_ap, scale=0.125, inc=("act", 1))

        W("tensor", "act", q_ev)
        W("tensor", "act", k_ev)
        if dbuf:
            bk = par * 2
            war = exp_bank[bk]
            W("tensor", war[0], war[1])
            for jc in range(qt):
                mm(P0[:, soff + jc * 128: soff + (jc + 1) * 128],
                   ka[hp:hp + 64, g * 2048 + jc * 128: g * 2048 + (jc + 1) * 128],
                   rhsq, start=True, stop=True)
            dreg = P0[:, soff + qt * 128: soff + (qt + 1) * 128]
            mm(dreg, ka[hp:hp + 64, g * 2048 + qt * 128: g * 2048 + (qt + 1) * 128],
               rhsq, start=True, stop=False)
            qk_last = mm(dreg, negut[:, :], idb[:, :], start=False, stop=True,
                         inc=("pe", 1))
            qk_of_h[(qt, h)] = qk_last
            e1 = exp_chunk(0, L, soff)
            exp_bank[bk] = ("act", e1)
            exp_done[h] = (e1, e1, e1, 0)
            return None
        eA = 0
        e = 0
        for ci in range((L1 + 511) // 512):
            lo = ci * 512
            hi = min(lo + 512, L1)
            st_tiles(lo, hi, lo, exp_bank[ci])
            qk_of_h[(qt, h)] = qk_last
            e = exp_chunk(lo, hi, lo)
            exp_bank[ci] = ("act", e)
            if ci == 0:
                eA = e
        if not two:
            exp_done[h] = (e, e, eA, 0)
            return None
        return (rhsq, bias_ap, eA, e)

    def stage_A2(qt, h, carry):
        nonlocal qk_last
        if carry is None:
            return
        rhsq, bias_ap, eA, eB = carry
        L = (qt + 1) * 128
        par = h % 2
        g = h // 2
        hp = (h % 2) * 64
        L2 = L - CHUNK
        W("tensor", exp_bank[0][0], exp_bank[0][1])
        qk2 = 0
        for jc in range(L2 // 128):
            qk2 = mm(P0[:, jc * 128:(jc + 1) * 128],
                     ka[hp:hp + 64, g * 2048 + CHUNK + jc * 128:
                        g * 2048 + CHUNK + (jc + 1) * 128],
                     rhsq, start=True, stop=True,
                     inc=("pe", 1) if (jc + 1) * 128 >= L2 else None)
        qk_last = qk2
        qk_of_h[(qt, h)] = qk_last
        W("scalar", "pe", qk2)
        W("scalar", "dve", p_dve_free[par])
        exm2 = act(pT_sb[par][:, CHUNK:L], P0[:, 0:L2], AF.Exp,
                   bias=bias_ap, scale=0.125, inc=("act", 1))
        exp_bank[0] = ("act", exm2)
        exp_done[h] = (exm2, exm2, eA, exm2)

    bstate = {}

    def stage_B1(qt, h):
        if (qt + 1) * 128 <= 512:
            bstate[h] = 0
            return
        par = h % 2
        ex_last = exp_done[h][0]
        dcol = qt * 128
        pT_diag = pT_sb[par][:, dcol:dcol + 128]
        W("vector", "act", ex_last)
        W("vector", "pool", consts_pool)
        pm = dtt(pT_diag, pT_diag, trimask[:], ALU.mult, inc=("dve", 1))
        p_dve_free[par] = pm
        bstate[h] = pm

    def stage_B2(qt, h):
        nonlocal ae_last
        L = qt + 1
        par = h % 2
        small = L * 128 <= 512
        ex_last, _, exm1, exm2 = exp_done[h]
        pm = bstate.pop(h)
        W("tensor", "act", ex_last)
        W("tensor", "dve", pm)
        if small:
            W("tensor", "dve", attslot_free[par][1])
        else:
            W("tensor", "dve", ae_last)
        pv_last = 0

        def vfrag(jc):
            o = (h // 2) * 2048 + jc * 128 + (h % 2) * 64
            return xnv[:, o:o + 64]

        if small and par == 1:
            for jc in range(L):
                mm(P0[:, 512:576], pT_sb[1][:, jc * 128:(jc + 1) * 128], vfrag(jc),
                   start=(jc == 0), stop=(jc == L - 1))
            for jc in range(L):
                pv_last = mm(P0[:, 576:577], pT_sb[1][:, jc * 128:(jc + 1) * 128],
                             ones_c[:, :], start=(jc == 0), stop=(jc == L - 1),
                             inc=("pe", 1) if jc == L - 1 else None)
            sums_ap = P0[:, 576:577]
            pv_ap = P0[:, 512:576]
        else:
            a0 = att_slot[par]
            sc0 = 1024 + par
            for jc in range(L):
                mm(P2[:, a0:a0 + 64],
                   pT_sb[par][:, jc * 128:(jc + 1) * 128], vfrag(jc),
                   start=(jc == 0), stop=(jc == L - 1))
                pv_last = mm(P1[:, sc0:sc0 + 1],
                             pT_sb[par][:, jc * 128:(jc + 1) * 128], ones_c[:, :],
                             start=(jc == 0), stop=(jc == L - 1),
                             inc=("pe", 1) if jc == L - 1 else None)
            sums_ap = P1[:, sc0:sc0 + 1]
            pv_ap = P2[:, a0:a0 + 64]
        pT_free[par] = pv_last
        W("vector", "pe", pv_last)
        W("vector", "dve", attslot_free[par][1])
        rc = drecip(swc[:, 2 + par:3 + par], sums_ap, inc=("dve", 1))
        W("vector", "dve", rc)
        if h == 0:
            W("vector", "pe", atok_free[qt % 2])
        ae = dts(atok[:, (qt % 2) * 1024 + h * 64: (qt % 2) * 1024 + (h + 1) * 64],
                 pv_ap, swc[:, 2 + par:3 + par], ALU.mult, inc=("dve", 1))
        attslot_free[par] = ("dve", ae)
        ae_last = ae
        return ae

    def emit_tail(qt, last_ae):
        nonlocal wo_done_prev, resid_done_prev
        at_last, atev = pt_group(
            [atok[:, (qt % 2) * 1024 + dc * 128:(qt % 2) * 1024 + (dc + 1) * 128]
             for dc in range(NDC)],
            lambda PTx, n: (attT[:, :], PTx[:, :]),
            waits=[("dve", last_ae), ("pe", wo_done_prev)], evac="auto")
        atok_free[qt % 2] = at_last
        wo_last = 0
        W("tensor", atev[0], atev[1])
        W("tensor", "dve", wo_evac)
        W("tensor", "dve", resid_done_prev)
        W("tensor", "lr", borow_lr)
        for half in range(2):
            for dc in range(NDC):
                mm(P1[:, half * 512:(half + 1) * 512],
                   attT[:, dc * 128:(dc + 1) * 128],
                   c2[:, dc * 1024 + half * 512: dc * 1024 + half * 512 + 512],
                   start=(dc == 0), stop=False)
            wo_last = mm(P1[:, half * 512:(half + 1) * 512], ones_r[:],
                         borow[:, half * 512:half * 512 + 512], start=False, stop=True,
                         inc=("pe", 1))
        wo_done_prev = wo_last
        W("vector", "pe", wo_last)
        rs = 0
        for half in range(2):
            rs = dtt(x_tm[:, qt * 1024 + half * 512: qt * 1024 + half * 512 + 512],
                     x_tm[:, qt * 1024 + half * 512: qt * 1024 + half * 512 + 512],
                     P1[:, half * 512:(half + 1) * 512], ALU.add, inc=("dve", 1))
        resid_done_prev = rs
        resid_of[qt] = rs

    steps = [(qt, h) for qt in range(KQT0, min(NQT, KQT)) for h in range(KNH)]
    last_ae_of = {}
    b1_emitted = set()
    stt_tile = 0
    stt_phase = 0

    def do_B1(s):
        if s not in b1_emitted:
            b1_emitted.add(s)
            stage_B1(*s)

    def do_B2(s):
        if s not in last_ae_of:
            last_ae_of[s] = stage_B2(*s)
            if s[1] == KNH - 1:
                emit_tail(s[0], last_ae_of[s])

    for idx, (qt, h) in enumerate(steps):
        if KNH == NH and stt_tile <= 14 and stt_tile in resid_of and \
                ((qt >= 5 and h in (5, 11)) or qt == NQT - 1):
            T = stt_tile
            if stt_phase == 0:
                W("vector", "dve", resid_of[T])
                dred(lncols[:, SC + T:SC + T + 1],
                     x_tm[:, T * 1024:(T + 1) * 1024], ALU.add)
                stt_phase = 1
            else:
                W("vector", "dve", resid_of[T])
                W("vector", "dve", ln2stt.get(T - 1, 0))
                ln2stt[T] = dstt(c1[:, 0:1024], x_tm[:, T * 1024:(T + 1) * 1024],
                                 1.0, x_tm[:, T * 1024:(T + 1) * 1024],
                                 ALU.mult, ALU.mult,
                                 accum=lncols[:, SQ + T:SQ + T + 1], inc=("dve", 1))
                stt_phase = 0
                stt_tile += 1
        if idx >= 2:
            do_B2(steps[idx - 2])
        if (qt, h) == (4, 0) and KNH == NH and KQT0 == 0:
            do_B1((3, 15))
            do_B2((3, 15))
            exp_bank[1] = ("dve", last_ae_of[(3, 15)])
        carry = stage_A1(qt, h)
        if idx >= 1:
            do_B1(steps[idx - 1])
        stage_A2(qt, h, carry)
    n = len(steps)
    if n >= 2:
        do_B2(steps[n - 2])
    do_B1(steps[n - 1])
    do_B2(steps[n - 1])

    attn_pe_done = wo_done_prev
    attn_dve_done = resid_done_prev
    if KPHASE == 4:
        early_store(prog.cnt["dve"])
        return

    if 14 in ln2stt:
        ln2stt[15] = ln_stats(15, 0, resid_of[15], ln2stt[14], wait_lx=False,
                              sbuf_discard=True)
    ln2_evac, ln2_tr = layer_norm(attn_pe_done, attn_dve_done, wait_lx=False,
                                  entry_act=prog.cnt["act"],
                                  stats_done=ln2stt.get(15, None))

    if KPHASE == 5:
        early_store(prog.cnt["dve"])
        return

    h_phase_done = {0: qk_last}
    y_phase_done = {}
    w1_loaded = {}
    w2_loaded = {}
    b1_done = {}
    prev_t_b1 = tbk
    W("gpsimd", "pe", attn_pe_done)
    W("gpsimd", "lb", bounce_lb)
    dma("gpsimd", b2t[:, :], bsc_d[None, :].broadcast_to((128, DM)), ("lr", 16))
    b2t_lr = prog.cnt["lr"]

    def emit_b1(ei, e):
        nonlocal prev_t_b1
        bpar = ei % 2
        W("sync", "pe", prev_t_b1)
        W("sync", "lb", bounce_lb)
        dma("sync", b1stage[0:16, bpar * 128:(bpar + 1) * 128],
            b1_d[e, :].rearrange("(t p) -> t p", p=128), ("lb", 16))
        b1_lb = prog.cnt["lb"]
        W("tensor", "lb", b1_lb)
        W("tensor", "dve", b1_done.get(ei - 1, 0))
        if ei == 0:
            W("tensor", "dve", ae_last)
        t_b1 = tr(P1[:, 1168 + bpar * 16: 1184 + bpar * 16],
                  b1stage[0:16, bpar * 128:(bpar + 1) * 128], idf[0:16, 0:16],
                  inc=("pe", 1))
        prev_t_b1 = t_b1
        W("vector", "pe", t_b1)
        W("vector", "act", b1col_readers.get(ei - 2, 0))
        b1_done[ei] = dts(b1col[:, bpar * 16:(bpar + 1) * 16],
                          P1[:, 1168 + bpar * 16: 1184 + bpar * 16], GATES[e], ALU.mult,
                          inc=("dve", 1))

    b1col_readers = {}
    emit_b1(0, EXPERTS[0])

    badd = 0
    for st in range(NST):
        W("vector", "lr", b2t_lr)
        W("vector", "dve", ln2_evac[-1])
        badd = dtt(x_tm[:, st * 1024:(st + 1) * 1024],
                   x_tm[:, st * 1024:(st + 1) * 1024], b2t[:, :], ALU.add,
                   inc=("dve", 1))

    yregion_free = {r: ln2_evac[-1] for r in range(3)}

    for ei, e in enumerate(EXPERTS):
        if ei == 0 and (NQT - 1, NH - 1) in qk_of_h:
            for dc in range(NDC):
                W("gpsimd", "pe", qk_of_h[(NQT - 1, 2 * dc + 1)])
                dma("gpsimd", qa[:, dc * 2048:(dc + 1) * 2048],
                    w1_d[e, dc * 128:(dc + 1) * 128, :], ("lw1", 16))
            w1_loaded[ei] = prog.cnt["lw1"]
            for hc in range(NHC):
                W("gpsimd", "pe", qk_of_h[(NQT - 1, (hc // 2) * 2 + 1)])
                dma("gpsimd", ka[:, hc * 1024:(hc + 1) * 1024],
                    w2_d[e, hc * 128:(hc + 1) * 128, :], ("lw2", 16))
            w2_loaded[ei] = prog.cnt["lw2"]
        else:
            W("gpsimd", "pe", h_phase_done[ei])
            dma("gpsimd", qa[:, :].rearrange("p (dc h) -> p dc h", h=2048),
                w1_d[e].rearrange("(dc p) h -> p dc h", p=128), ("lw1", 16))
            w1_loaded[ei] = prog.cnt["lw1"]
            W("gpsimd", "pe", y_phase_done.get(ei - 1, qk_last))
            dma("gpsimd", ka[:, :].rearrange("p (hc o) -> p hc o", o=1024),
                w2_d[e].rearrange("(hc p) o -> p hc o", p=128), ("lw2", 16))
            w2_loaded[ei] = prog.cnt["lw2"]
        bpar = ei % 2
        if ei > 0:
            emit_b1(ei, e)

        hp_prev = {0: 0, 1: 0}
        w2_prev_groups = 0
        relu_last = 0
        last_h_mm = 0
        last_y_mm = 0
        resid_last = 0
        for sc in range(4):
            for ht in range(NHC):
                reg = ht % 2
                W("tensor", "lw1", w1_loaded[ei])
                W("tensor", "act", hp_prev[reg])
                W("tensor", "dve", ln2_evac[sc * 4 + 3])
                if ei == 0 and sc == 0:
                    W("tensor", "dve", bias_ev)
                for dc in range(NDC):
                    last_h_mm = mm(P1[:, reg * 512:(reg + 1) * 512],
                                   qa[:, dc * 2048 + ht * 128: dc * 2048 + (ht + 1) * 128],
                                   xnv[:, dc * 2048 + sc * 512: dc * 2048 + (sc + 1) * 512],
                                   start=(dc == 0), stop=(dc == NDC - 1),
                                   inc=("pe", 1) if dc == NDC - 1 else None)
                W("scalar", "pe", last_h_mm)
                W("scalar", "pe", w2_prev_groups)
                W("scalar", "dve", b1_done[ei])
                hp_prev[reg] = act(c1[:, ht * 512:(ht + 1) * 512],
                                   P1[:, reg * 512:(reg + 1) * 512], AF.Relu,
                                   bias=b1col[:, bpar * 16 + ht: bpar * 16 + ht + 1],
                                   scale=GATES[e], inc=("act", 1))
                relu_last = hp_prev[reg]
            for stl in range(4):
                st = sc * 4 + stl
                for half in range(2):
                    r = (stl * 2 + half) % 3
                    poff = r * 512
                    W("tensor", "lw2", w2_loaded[ei])
                    W("tensor", "act", relu_last)
                    W("tensor", "dve", yregion_free[r])
                    for hc in range(NHC):
                        last_y_mm = mm(
                            P0[:, poff:poff + 512],
                            c1[:, hc * 512 + stl * 128: hc * 512 + (stl + 1) * 128],
                            ka[:, hc * 1024 + half * 512: hc * 1024 + half * 512 + 512],
                            start=(hc == 0), stop=(hc == NHC - 1),
                            inc=("pe", 1) if hc == NHC - 1 else None)
                    W("vector", "pe", last_y_mm)
                    W("vector", "dve", badd)
                    resid_last = dtt(
                        x_tm[:, st * 1024 + half * 512: st * 1024 + half * 512 + 512],
                        x_tm[:, st * 1024 + half * 512: st * 1024 + half * 512 + 512],
                        P0[:, poff:poff + 512], ALU.add, inc=("dve", 1))
                    yregion_free[r] = resid_last
            w2_prev_groups = last_y_mm
            if ei == len(EXPERTS) - 1:
                for stl in range(4):
                    st = sc * 4 + stl
                    W("sync", "dve", resid_last)
                    dma("sync", y_d.rearrange("(t p) d -> p t d", p=128)[:, st:st + 1, :],
                        x_tm[:, st * 1024:(st + 1) * 1024].rearrange(
                            "p (a d) -> p a d", a=1), ("st", 16))
        h_phase_done[ei + 1] = last_h_mm
        y_phase_done[ei] = last_y_mm
        b2_mm_done = last_y_mm
        b1col_readers[ei] = relu_last

    W("sync", "st", prog.cnt["st"])


_NC_CACHE = None


def kernel(L0_attn_x, L0_attn_Wq, L0_attn_bq, L0_attn_Wk, L0_attn_bk, L0_attn_Wv,
           L0_attn_bv, L0_attn_Wo, L0_attn_bo, L0_ffn_w_gate, L0_ffn_w_noise,
           L0_ffn_W1, L0_ffn_b1, L0_ffn_W2, L0_ffn_b2):
    del L0_ffn_w_gate, L0_ffn_w_noise
    global _NC_CACHE
    if _NC_CACHE is None:
        _NC_CACHE = build_program()
    nc = _NC_CACHE
    x = np.ascontiguousarray(np.asarray(L0_attn_x, dtype=np.float32))
    common = {
        "wq": np.ascontiguousarray(np.asarray(L0_attn_Wq, np.float32)),
        "wk": np.ascontiguousarray(np.asarray(L0_attn_Wk, np.float32)),
        "wv": np.ascontiguousarray(np.asarray(L0_attn_Wv, np.float32)),
        "wo": np.ascontiguousarray(np.asarray(L0_attn_Wo, np.float32)),
        "bq": np.ascontiguousarray(np.asarray(L0_attn_bq, np.float32)),
        "bk": np.ascontiguousarray(np.asarray(L0_attn_bk, np.float32)),
        "bv": np.ascontiguousarray(np.asarray(L0_attn_bv, np.float32)),
        "bo": np.ascontiguousarray(np.asarray(L0_attn_bo, np.float32)),
        "w1": np.ascontiguousarray(np.asarray(L0_ffn_W1, np.float32)),
        "b1": np.ascontiguousarray(np.asarray(L0_ffn_b1, np.float32)),
        "w2": np.ascontiguousarray(np.asarray(L0_ffn_W2, np.float32)),
        "b2": np.ascontiguousarray(np.asarray(L0_ffn_b2, np.float32)),
    }
    in_maps = []
    for b in range(8):
        m = dict(common)
        m["x"] = np.ascontiguousarray(x[b])
        in_maps.append(m)
    res = run_bass_kernel_spmd(nc, in_maps, core_ids=list(range(8)))
    out = np.stack([res.results[b]["y"] for b in range(8)], axis=0)
    return out.astype(np.float32)


# version 34
# speedup vs baseline: 1.0142x; 1.0142x over previous
import os
import math
import contextlib
import numpy as np

import concourse.bass as bass
import concourse.mybir as mybir
from concourse.bass_utils import run_bass_kernel_spmd

F32 = mybir.dt.float32
BF16 = mybir.dt.bfloat16
AF = mybir.ActivationFunctionType
ALU = mybir.AluOpType
AX = mybir.AxisListType

S = 2048
DM = 1024
NH = 16
HD = 64
NST = 16
NDC = 8
NQT = 16
HID = 2048
NHC = 16
EPS = 1e-5

_tv = np.array([100.0, 0.0, 0.0, 0.0], np.float32)
_ex = np.exp(_tv - _tv.max(), dtype=np.float32)
_gk = (_ex / _ex.sum(dtype=np.float32)).astype(np.float32)
GATES = {1: float(_gk[0]), 0: float(_gk[1]), 2: float(_gk[2]), 3: float(_gk[3])}
EXPERTS = [0, 1, 2, 3]

ENGS = ("tensor", "scalar", "vector", "sync", "gpsimd")
KPHASE = int(os.environ.get("KPHASE", "99"))
KQT = int(os.environ.get("KQT", "16"))
KQT0 = int(os.environ.get("KQT0", "0"))
KNH = int(os.environ.get("KNH", "16"))
KSKIP = set(os.environ.get("KSKIP", "").split(","))


class Prog:

    def __init__(self):
        self.q = {e: [] for e in ENGS}
        self.cnt = {}
        self.lastwait = {}
        self.sems = None

    def addsem(self, *names):
        for n in names:
            self.cnt[n] = 0

    def op(self, eng, fn, inc=None):
        if inc is None:
            self.q[eng].append(fn)
            return None
        s, a = inc
        self.cnt[s] += a
        tgt = self.cnt[s]

        def run(e, f=fn, s=s, a=a, self_=self):
            ins = f(e)
            ins.then_inc(self_.sems[s], a)

        self.q[eng].append(run)
        return tgt

    def wait(self, eng, sem, val):
        if val is None or val <= 0:
            return
        assert val <= self.cnt[sem], f"wait target {val} > planned {self.cnt[sem]} for {sem}"
        if self.lastwait.get((eng, sem), 0) >= val:
            return
        self.lastwait[(eng, sem)] = val

        def run(e, s=sem, v=val, self_=self):
            e.wait_ge(self_.sems[s], v)

        self.q[eng].append(run)

    def replay(self, eng, e):
        for f in self.q[eng]:
            f(e)


def build_program():
    nc = bass.Bass("TRN2", target_bir_lowering=False, debug=False)

    x_d = nc.dram_tensor("x", [S, DM], F32, kind="ExternalInput").ap()
    wq_d = nc.dram_tensor("wq", [DM, DM], F32, kind="ExternalInput").ap()
    wk_d = nc.dram_tensor("wk", [DM, DM], F32, kind="ExternalInput").ap()
    wv_d = nc.dram_tensor("wv", [DM, DM], F32, kind="ExternalInput").ap()
    wo_d = nc.dram_tensor("wo", [DM, DM], F32, kind="ExternalInput").ap()
    bq_d = nc.dram_tensor("bq", [DM], F32, kind="ExternalInput").ap()
    bk_d = nc.dram_tensor("bk", [DM], F32, kind="ExternalInput").ap()
    bv_d = nc.dram_tensor("bv", [DM], F32, kind="ExternalInput").ap()
    bo_d = nc.dram_tensor("bo", [DM], F32, kind="ExternalInput").ap()
    w1_d = nc.dram_tensor("w1", [16, DM, HID], F32, kind="ExternalInput").ap()
    b1_d = nc.dram_tensor("b1", [16, HID], F32, kind="ExternalInput").ap()
    w2_d = nc.dram_tensor("w2", [16, HID, DM], F32, kind="ExternalInput").ap()
    b2_d = nc.dram_tensor("b2", [16, DM], F32, kind="ExternalInput").ap()
    y_d = nc.dram_tensor("y", [S, DM], F32, kind="ExternalOutput").ap()
    bsc_d = nc.dram_tensor("bsc", [DM], F32, kind="Internal").ap()

    prog = Prog()
    prog.addsem("lx0", "lx1", "lx2", "lx3", "lw", "lw1", "lw2", "lr", "lb", "st", "pev", "pe", "act", "dve", "pool")

    with contextlib.ExitStack() as stack:
        E = stack.enter_context

        SBUF_BASE = 16512
        cur = [0]

        def at(name, shape, dtype, off=None, adv=True):
            nbytes = (shape[1] * (4 if dtype == F32 else 2) + 31) // 32 * 32
            if off is None:
                off = cur[0]
                if adv:
                    cur[0] += nbytes
            h = nc.alloc_sbuf_tensor_at(name, shape, dtype, offset=SBUF_BASE + off)
            return h

        x_tm = at("x_tm", [128, NST * 1024], F32)
        off_xnv = cur[0]
        xnv = at("xnv", [128, 8 * 2048], BF16)
        off_qa = cur[0]
        qa = at("qa", [128, 8 * 2048], BF16)
        off_ka = cur[0]
        ka = at("ka", [128, 16 * 1024], BF16)
        off_c1 = cur[0]
        c1 = at("c1", [128, 8192], BF16)
        qnkn = at("qnkn", [16, 4096], F32, off=off_c1, adv=False)
        cur[0] = off_c1 + 16384
        off_c2 = cur[0]
        c2 = at("c2", [128, 8192], BF16)
        b2t = at("b2t", [128, 1024], F32, off=off_c2, adv=False)
        atok = at("atok", [128, 2048], BF16)
        attT = at("attT", [128, 1024], BF16, off=off_c1 + 12288, adv=False)
        idb = at("idb", [128, 128], BF16)
        idf = at("idf", [16, 16], F32)
        trimask = at("trimask", [128, 128], BF16)
        negut = at("negut", [128, 128], BF16)
        bias_sb = at("bias_sb", [128, 256], F32)
        lncols = at("lncols", [128, 112], F32)
        swc = at("swc", [128, 8], F32)
        bqcol = at("bqcol", [128, 8], F32)
        bkcol = at("bkcol", [128, 8], F32)
        b1col = at("b1col", [128, 32], F32)
        qkm = at("qkm", [128, 256], BF16)
        ones_r = at("ones_r", [1, 128], BF16)
        ones_c = at("ones_c", [128, 1], BF16)
        onesf16 = at("onesf16", [1, 16], F32)
        onesf_col = at("onesf_col", [1, 128], F32)
        bvrow = at("bvrow", [1, 1024], BF16)
        borow = at("borow", [1, 1024], BF16)
        b2row = at("b2row", [1, 1024], BF16)
        ksrow = at("ksrow", [1, 16], F32)
        kcol16 = at("kcol16", [16, 1], F32)
        kmax1 = at("kmax1", [1, 1], F32)
        kneg1 = at("kneg1", [1, 1], F32)
        kapcol = at("kapcol", [16, 1], F32)
        bstage = at("bstage", [16, 128], F32)
        b1stage = at("b1stage", [16, 256], F32)
        assert cur[0] <= 212830, f"sbuf over budget: {cur[0]}"

        P0 = E(nc.psum_tensor("P0", [128, 1536], F32))
        P1 = E(nc.psum_tensor("P1", [128, 1536], F32))
        PTa = E(nc.psum_tensor("PTa", [128, 1024], BF16))
        P2 = E(nc.psum_tensor("P2", [128, 512], F32))

        sems = {}
        for n in ("lx0", "lx1", "lx2", "lx3", "lw", "lw1", "lw2", "lr", "lb", "st",
                  "pev", "pe", "act", "dve", "pool"):
            sems[n] = E(nc.semaphore(n))
        prog.sems = sems

        plan(prog, locals())

        block = E(nc.Block())

        @block.tensor
        def _(e):
            prog.replay("tensor", e)

        @block.scalar
        def _(e):
            prog.replay("scalar", e)

        @block.vector
        def _(e):
            prog.replay("vector", e)

        @block.sync
        def _(e):
            prog.replay("sync", e)

        @block.gpsimd
        def _(e):
            prog.replay("gpsimd", e)

    return nc


def plan(prog, ns):
    P = prog.op
    W = prog.wait
    x_d, y_d, bsc_d = ns["x_d"], ns["y_d"], ns["bsc_d"]
    wq_d, wk_d, wv_d, wo_d = ns["wq_d"], ns["wk_d"], ns["wv_d"], ns["wo_d"]
    bq_d, bk_d, bv_d, bo_d = ns["bq_d"], ns["bk_d"], ns["bv_d"], ns["bo_d"]
    w1_d, b1_d, w2_d, b2_d = ns["w1_d"], ns["b1_d"], ns["w2_d"], ns["b2_d"]
    x_tm, xnv, qa, ka, c1, qnkn, c2 = (
        ns["x_tm"], ns["xnv"], ns["qa"], ns["ka"], ns["c1"], ns["qnkn"], ns["c2"])
    atok, attT, idb, idf, trimask = ns["atok"], ns["attT"], ns["idb"], ns["idf"], ns["trimask"]
    negut = ns["negut"]
    b2t = ns["b2t"]
    bias_sb, lncols, swc = ns["bias_sb"], ns["lncols"], ns["swc"]
    bqcol, bkcol, b1col, qkm = ns["bqcol"], ns["bkcol"], ns["b1col"], ns["qkm"]
    ones_r, onesf16, ones_c = ns["ones_r"], ns["onesf16"], ns["ones_c"]
    onesf_col = ns["onesf_col"]
    bvrow, borow, b2row = ns["bvrow"], ns["borow"], ns["b2row"]
    ksrow, kcol16, kmax1, kneg1, kapcol = (
        ns["ksrow"], ns["kcol16"], ns["kmax1"], ns["kneg1"], ns["kapcol"])
    bstage, b1stage = ns["bstage"], ns["b1stage"]
    P0, P1, PTa, P2 = ns["P0"], ns["P1"], ns["PTa"], ns["P2"]

    def mm(out, lhsT, rhs, start, stop, inc=None):
        return P("tensor",
                 lambda e, o=out, l=lhsT, r=rhs, s=start, t=stop:
                 e.matmul(o, l, r, start=s, stop=t), inc)

    def tr(out, in_, ident, inc=None):
        return P("tensor",
                 lambda e, o=out, i=in_, d=ident: e.transpose(out=o, in_=i, identity=d), inc)

    def act(out, in_, func, bias=0.0, scale=1.0, accum=None, inc=None):
        return P("scalar",
                 lambda e, o=out, i=in_, f=func, b=bias, s=scale, a=accum:
                 e.activation(out=o, in_=i, func=f, bias=b, scale=s, accum_out=a), inc)

    def dcopy(out, in_, inc=None):
        return P("vector", lambda e, o=out, i=in_: e.tensor_copy(out=o, in_=i), inc)

    def acopy(out, in_, inc=None):
        return P("scalar", lambda e, o=out, i=in_: e.copy(out, in_) if False else e.copy(o, i), inc)

    def dtt(out, in0, in1, aop, inc=None):
        return P("vector",
                 lambda e, o=out, a=in0, b=in1, p=aop: e.tensor_tensor(out=o, in0=a, in1=b, op=p),
                 inc)

    def dts(out, in0, scalar, aop, inc=None, accum=None):
        return P("vector",
                 lambda e, o=out, a=in0, s=scalar, p=aop, ac=accum:
                 e.tensor_scalar(out=o, in0=a, scalar1=s, scalar2=None, op0=p, accum_out=ac), inc)

    def dstt(out, in0, scalar, in1, op0, op1, accum=None, inc=None):
        return P("vector",
                 lambda e, o=out, a=in0, s=scalar, b=in1, p0=op0, p1=op1, ac=accum:
                 e.scalar_tensor_tensor(out=o, in0=a, scalar=s, in1=b, op0=p0, op1=p1,
                                        accum_out=ac), inc)

    def dred(out, in_, aop, inc=None):
        return P("vector",
                 lambda e, o=out, i=in_, p=aop: e.tensor_reduce(out=o, in_=i, axis=AX.X, op=p),
                 inc)

    def drecip(out, in_, inc=None):
        return P("vector", lambda e, o=out, i=in_: e.reciprocal(out=o, in_=i), inc)

    def dma(eng, out, in_, inc):
        return P(eng, lambda e, o=out, i=in_: e.dma_start(out=o, in_=i), inc)

    def gmemset(apx, v, inc=None):
        return P("gpsimd", lambda e, a=apx, vv=v: e.memset(a, vv), inc)

    ptring = {"idx": 0, "free": [("dve", 0)], "bank": None}

    def pt_group(srcs, mk_evac, waits=(), evac="vector"):
        b = ptring["idx"]
        ptring["idx"] = (b + 1) % len(ptring["bank"])
        PTx = ptring["bank"][b]
        fs, fv = ptring["free"][b]
        W("tensor", fs, fv)
        for sem, val in waits:
            W("tensor", sem, val)
        t_last = 0
        for i, s in enumerate(srcs):
            t_last = tr(PTx[:, i * 128:(i + 1) * 128], s, idb[:],
                        inc=("pe", 1) if i == len(srcs) - 1 else None)
        out_ap, in_ap = mk_evac(PTx, len(srcs))
        if evac == "act":
            W("scalar", "pe", t_last)
            ev = ("act", acopy(out_ap, in_ap, inc=("act", 1)))
        else:
            W("vector", "pe", t_last)
            ev = ("dve", dcopy(out_ap, in_ap, inc=("dve", 1)))
        ptring["free"][b] = ev
        return t_last, ev

    ptring["bank"] = [PTa]

    m = gmemset(idb[:], 0.0, inc=("pool", 1))
    W("gpsimd", "pool", m)
    P("gpsimd", lambda e: e.affine_select(out=idb[:], in_=idb[:], compare_op=ALU.not_equal,
                                          fill=1.0, base=0, pattern=[[-1, 128]],
                                          channel_multiplier=1))
    m = gmemset(idf[:], 0.0, inc=("pool", 1))
    W("gpsimd", "pool", m)
    P("gpsimd", lambda e: e.affine_select(out=idf[:], in_=idf[:], compare_op=ALU.not_equal,
                                          fill=1.0, base=0, pattern=[[-1, 16]],
                                          channel_multiplier=1))
    m = gmemset(trimask[:], 1.0, inc=("pool", 1))
    W("gpsimd", "pool", m)
    P("gpsimd", lambda e: e.affine_select(out=trimask[:], in_=trimask[:], compare_op=ALU.is_ge,
                                          fill=0.0, base=0, pattern=[[1, 128]],
                                          channel_multiplier=-1))
    m = gmemset(negut[:], -480.0, inc=("pool", 1))
    W("gpsimd", "pool", m)
    P("gpsimd", lambda e: e.affine_select(out=negut[:], in_=negut[:], compare_op=ALU.is_ge,
                                          fill=0.0, base=-1, pattern=[[1, 128]],
                                          channel_multiplier=-1))
    gmemset(ones_c[:], 1.0)
    gmemset(onesf_col[:], 1.0)
    gmemset(ones_r[:], 1.0)
    gmemset(onesf16[:], 1.0)
    m = gmemset(qkm[:], 0.0, inc=("pool", 1))
    W("gpsimd", "pool", m)
    for g in range(8):
        gmemset(qkm[0:64, g * 16 + 2 * g: g * 16 + 2 * g + 1], 0.0625)
        gmemset(qkm[64:128, g * 16 + 2 * g + 1: g * 16 + 2 * g + 2], 0.0625)
        gmemset(qkm[0:64, 128 + g * 16 + 2 * g: 128 + g * 16 + 2 * g + 1], 0.0625)
        last = gmemset(qkm[64:128, 128 + g * 16 + 2 * g + 1: 128 + g * 16 + 2 * g + 2],
                       0.0625, inc=("pool", 1) if g == 7 else None)
    consts_pool = prog.cnt["pool"]

    dma("gpsimd", bvrow[:], bv_d[None, :], ("lr", 16))
    dma("gpsimd", borow[:], bo_d[None, :], ("lr", 16))
    borow_lr = prog.cnt["lr"]

    x_r = x_d.rearrange("(t p) d -> p t d", p=128)
    xload = []
    for t in range(4):
        dma("sync", x_tm[:, t * 4096:(t + 1) * 4096].rearrange("p (a d) -> p a d", d=1024),
            x_r[:, 4 * t:4 * t + 4, :], (f"lx{t}", 16))
        xload.append((f"lx{t}", prog.cnt[f"lx{t}"]))
    dma("sync", bstage[0:8, :], bq_d.rearrange("(t p) -> t p", p=128), ("lb", 16))
    bq_lb = prog.cnt["lb"]

    def wload(wd):
        dma("gpsimd", c1[:, :].rearrange("p (ot d) -> p ot d", d=1024),
            wd.rearrange("(ot p) d -> p ot d", p=128), ("lw", 16))
        return prog.cnt["lw"]

    wq_lw = wload(wq_d)

    SC, SQ, NM, MSQ, VAR, RSTD, LNV = 0, 16, 32, 48, 64, 80, 96

    def ln_stats(st, entry_pe, entry_dve, prev_stt, wait_lx, sbuf_discard=False):
        if wait_lx:
            W("vector", xload[st // 4][0], xload[st // 4][1])
        W("vector", "pe", entry_pe)
        W("vector", "dve", entry_dve)
        dred(lncols[:, SC + st:SC + st + 1], x_tm[:, st * 1024:(st + 1) * 1024], ALU.add)
        W("vector", "dve", prev_stt)
        out_ap = c1[:, 0:1024] if sbuf_discard else P0[:, 0:1024]
        return dstt(out_ap, x_tm[:, st * 1024:(st + 1) * 1024], 1.0,
                    x_tm[:, st * 1024:(st + 1) * 1024], ALU.mult, ALU.mult,
                    accum=lncols[:, SQ + st:SQ + st + 1], inc=("dve", 1))

    def layer_norm(entry_pe, entry_dve, wait_lx, entry_act=0, stats_done=None):
        grouped = stats_done is None
        evac_dve = []
        prev_ts = 0
        prev_evac = 0
        prev_tr = 0
        tr_half = {0: 0, 1: 0}
        prev_stt = 0
        ngrp = 4 if grouped else 1
        gw = 16 // ngrp
        for grp in range(ngrp):
            c0 = grp * gw
            if grouped:
                for st in range(c0, c0 + gw):
                    prev_stt = ln_stats(st, entry_pe, entry_dve, prev_stt, wait_lx)
                sd = prev_stt
            else:
                sd = stats_done
            W("vector", "dve", sd)
            m1 = dts(lncols[:, NM + c0:NM + c0 + gw], lncols[:, SC + c0:SC + c0 + gw],
                     -1.0 / 1024.0, ALU.mult, inc=("dve", 1))
            W("vector", "dve", m1)
            m2 = dstt(lncols[:, MSQ + c0:MSQ + c0 + gw], lncols[:, NM + c0:NM + c0 + gw],
                      1.0, lncols[:, NM + c0:NM + c0 + gw], ALU.mult, ALU.mult,
                      inc=("dve", 1))
            m3 = dts(lncols[:, SQ + c0:SQ + c0 + gw], lncols[:, SQ + c0:SQ + c0 + gw],
                     1.0 / 1024.0, ALU.mult, inc=("dve", 1))
            W("vector", "dve", m3)
            m4 = dstt(lncols[:, VAR + c0:VAR + c0 + gw], lncols[:, SQ + c0:SQ + c0 + gw],
                      EPS, lncols[:, MSQ + c0:MSQ + c0 + gw], ALU.add, ALU.subtract,
                      inc=("dve", 1))
            W("scalar", "dve", m4)
            a1 = act(lncols[:, LNV + c0:LNV + c0 + gw], lncols[:, VAR + c0:VAR + c0 + gw],
                     AF.Ln, bias=0.0, scale=1.0, inc=("act", 1))
            W("scalar", "act", a1)
            a2 = act(lncols[:, RSTD + c0:RSTD + c0 + gw], lncols[:, LNV + c0:LNV + c0 + gw],
                     AF.Exp, bias=0.0, scale=-0.5, inc=("act", 1))
            W("vector", "act", a2)
            m5 = dtt(lncols[:, MSQ + c0:MSQ + c0 + gw], lncols[:, NM + c0:NM + c0 + gw],
                     lncols[:, RSTD + c0:RSTD + c0 + gw], ALU.mult, inc=("dve", 1))
            for st in range(c0, c0 + gw):
                half = st % 2
                xn_tmp = atok[:, half * 1024:(half + 1) * 1024]
                W("scalar", "dve", m5)
                W("scalar", "pe", tr_half[half])
                prev_ts = act(xn_tmp[:, 0:1024], x_tm[:, st * 1024:(st + 1) * 1024],
                              AF.Identity, bias=lncols[:, MSQ + st:MSQ + st + 1],
                              scale=lncols[:, RSTD + st:RSTD + st + 1], inc=("act", 1))
                waits = [("act", prev_ts), ("pool", consts_pool)]
                if entry_act:
                    waits.append(("act", entry_act))
                prev_tr, ev = pt_group(
                    [xn_tmp[:, dc * 128:(dc + 1) * 128] for dc in range(NDC)],
                    lambda PTx, n, st=st: (
                        xnv[:, :].rearrange("p (dc c) -> p dc c", c=2048)[
                            :, :, st * 128:(st + 1) * 128],
                        PTx[:, :].rearrange("p (dc c) -> p dc c", c=128)),
                    waits=waits)
                tr_half[half] = prev_tr
                prev_evac = ev[1]
                evac_dve.append(prev_evac)
        return evac_dve, prev_tr

    def early_store(dve_gate):
        for st in range(NST):
            W("sync", "dve", dve_gate)
            dma("sync", y_d.rearrange("(t p) d -> p t d", p=128)[:, st:st + 1, :],
                x_tm[:, st * 1024:(st + 1) * 1024].rearrange(
                    "p (a d) -> p a d", a=1), ("st", 16))
        W("sync", "st", prog.cnt["st"])

    ln1_evac, ln1_tr = layer_norm(0, 0, wait_lx=True)
    if KPHASE == 1:
        early_store(prog.cnt["dve"])
        return

    def build_wt(load_lw):
        tr_last = 0
        ev = 0
        for dc in range(NDC):
            tr_last, ev = pt_group(
                [c1[:, ot * 1024 + dc * 128: ot * 1024 + (dc + 1) * 128]
                 for ot in range(8)],
                lambda PTx, n, dc=dc: (c2[:, dc * 1024:(dc + 1) * 1024], PTx[:, :]),
                waits=[("lw", load_lw)])
        return tr_last, ev[1]

    W("tensor", "lb", bq_lb)
    tbq = tr(P1[:, 1168:1176], bstage[0:8, :], idf[0:8, 0:8], inc=("pe", 1))
    W("vector", "pe", tbq)
    ebq = dcopy(bqcol[:, 0:8], P1[:, 1168:1176], inc=("dve", 1))
    W("sync", "pe", tbq)
    dma("sync", bstage[0:8, :], bk_d.rearrange("(t p) -> t p", p=128), ("lb", 16))
    bk_lb = prog.cnt["lb"]

    wq_tr, wq_evac = build_wt(wq_lw)
    W("tensor", "lb", bk_lb)
    W("tensor", "dve", ebq)
    tbk = tr(P1[:, 1168:1176], bstage[0:8, :], idf[0:8, 0:8], inc=("pe", 1))
    W("vector", "pe", tbk)
    ebk = dcopy(bkcol[:, 0:8], P1[:, 1168:1176], inc=("dve", 1))

    W("sync", "pe", tbk)
    comb = 0
    first = True
    for e in EXPERTS:
        slot = b1stage[0:8, 0:128] if first else b1stage[0:8, 128:256]
        dma("sync", slot, b2_d[e, :].rearrange("(t p) -> t p", p=128), ("lb", 16))
        l_ = prog.cnt["lb"]
        W("vector", "lb", l_)
        W("vector", "dve", comb)
        m_ = dts(slot, slot, GATES[e], ALU.mult, inc=("dve", 1))
        if first:
            comb = m_
            first = False
        else:
            W("vector", "dve", m_)
            comb = dtt(b1stage[0:8, 0:128], b1stage[0:8, 0:128], b1stage[0:8, 128:256],
                       ALU.add, inc=("dve", 1))
        if e != EXPERTS[-1]:
            W("sync", "dve", comb)
    W("sync", "dve", comb)
    dma("sync", bsc_d.rearrange("(t p) -> t p", p=128), b1stage[0:8, 0:128], ("lb", 16))
    bounce_lb = prog.cnt["lb"]

    W("gpsimd", "pe", wq_tr)
    wk_lw = wload(wk_d)

    def proj_fm(dest, dest_off, bcol, ln_evacs, w_evac, entry_act=0):
        prev_ev = {0: entry_act, 1: entry_act}
        last_mm = 0
        last_ev = 0
        for sc in range(4):
            for ot in range(8):
                reg = (sc * 8 + ot) % 2
                W("tensor", "dve", w_evac)
                W("tensor", "dve", ln_evacs[sc * 4 + 3])
                W("tensor", "act", prev_ev[reg])
                for dc in range(NDC):
                    last_mm = mm(P1[:, reg * 512:(reg + 1) * 512],
                                 c2[:, dc * 1024 + ot * 128: dc * 1024 + (ot + 1) * 128],
                                 xnv[:, dc * 2048 + sc * 512: dc * 2048 + (sc + 1) * 512],
                                 start=(dc == 0), stop=(dc == NDC - 1),
                                 inc=("pe", 1) if dc == NDC - 1 else None)
                W("scalar", "pe", last_mm)
                prev_ev[reg] = act(dest[:, ot * 2048 + sc * 512: ot * 2048 + (sc + 1) * 512],
                                   P1[:, reg * 512:(reg + 1) * 512], AF.Identity,
                                   bias=bcol[:, ot:ot + 1], scale=1.0, inc=("act", 1))
                last_ev = prev_ev[reg]
        return last_mm, last_ev

    W("scalar", "dve", ebq)
    q_mm, q_ev = proj_fm(qa, 0, bqcol, ln1_evac, wq_evac)

    wk_tr, wk_evac = build_wt(wk_lw)
    W("gpsimd", "pe", wk_tr)
    wv_lw = wload(wv_d)

    W("scalar", "dve", ebk)
    k_mm, k_ev = proj_fm(ka, 0, bkcol, ln1_evac, wk_evac, entry_act=q_ev)

    wv_tr, wv_evac = build_wt(wv_lw)
    W("gpsimd", "pe", wv_tr)
    wo_lw = wload(wo_d)

    vregions = [(P1, 0), (P1, 512), (P0, 0), (P0, 512)]
    v_prev_ev = {0: 0, 1: 0, 2: 0, 3: 0}
    v_last_ev = 0
    v_last_mm = 0
    for st in range(NST):
        grp = []
        for half in range(2):
            r = (st * 2 + half) % 4
            pbuf, poff = vregions[r]
            W("tensor", "dve", wv_evac)
            W("tensor", "dve", v_prev_ev[r])
            W("tensor", "lr", borow_lr)
            if st == 0 and half == 0:
                W("tensor", "pe", k_mm)
                W("tensor", "act", k_ev)
            for dc in range(NDC):
                mm(pbuf[:, poff:poff + 512],
                   xnv[:, dc * 2048 + st * 128: dc * 2048 + (st + 1) * 128],
                   c2[:, dc * 1024 + half * 512: dc * 1024 + half * 512 + 512],
                   start=(dc == 0), stop=False)
            v_last_mm = mm(pbuf[:, poff:poff + 512], ones_r[:],
                           bvrow[:, half * 512:half * 512 + 512],
                           start=False, stop=True, inc=("pe", 1))
            grp.append((r, pbuf, poff))
        for half in range(2):
            r, pbuf, poff = grp[half]
            W("vector", "pe", v_last_mm)
            dest = xnv[:, :].rearrange("p (dc c) -> p dc c", c=2048)[
                :, 4 * half:4 * half + 4, st * 128:(st + 1) * 128]
            src = pbuf[:, poff:poff + 512].rearrange("p (dc c) -> p dc c", c=128)
            v_prev_ev[r] = dcopy(dest, src, inc=("dve", 1))
            v_last_ev = v_prev_ev[r]

    wo_tr, wo_evac = build_wt(wo_lw)
    if KPHASE == 2:
        early_store(prog.cnt["dve"])
        return

    scr_free = {0: ln1_tr, 1: wo_tr}

    def scr_ap(b, lo, hi):
        return atok[:, lo:hi] if b == 0 else c1[:, 4096 + lo:4096 + hi]

    def norms_prep(src_t, mask_coloff, psum_dve_entry):
        last_mm = 0
        for g in range(8):
            b = g % 2
            W("vector", "pe", scr_free[b])
            W("vector", "dve", v_last_ev)
            tmul = dtt(scr_ap(b, 0, 2048), src_t[:, g * 2048:(g + 1) * 2048],
                       src_t[:, g * 2048:(g + 1) * 2048], ALU.mult, inc=("dve", 1))
            W("tensor", "dve", tmul)
            if g == 0:
                W("tensor", "dve", psum_dve_entry)
            for c4 in range(4):
                out_ap = (P0[0:16, c4 * 512:(c4 + 1) * 512] if c4 < 3
                          else P1[0:16, 0:512])
                last_mm = mm(out_ap,
                             qkm[:, mask_coloff + g * 16: mask_coloff + (g + 1) * 16],
                             scr_ap(b, c4 * 512, (c4 + 1) * 512),
                             start=(g == 0), stop=(g == 7),
                             inc=("pe", 1) if c4 == 3 else None)
            scr_free[b] = last_mm
        return last_mm

    qn_mm = norms_prep(qa, 0, v_last_ev)
    W("vector", "pe", qn_mm)
    dcopy(qnkn[0:16, 0:1536], P0[0:16, 0:1536], inc=("dve", 1))
    qn_ev = dcopy(qnkn[0:16, 1536:2048], P1[0:16, 0:512], inc=("dve", 1))
    kn_mm = norms_prep(ka, 128, qn_ev)
    W("vector", "pe", kn_mm)
    kn_p0 = dcopy(qnkn[0:16, 2048:2048 + 1536], P0[0:16, 0:1536], inc=("dve", 1))
    kn_ev = dcopy(qnkn[0:16, 2048 + 1536:4096], P1[0:16, 0:512], inc=("dve", 1))
    W("vector", "dve", kn_ev)
    r1 = dred(kcol16[0:16, 0:1], qnkn[0:16, 0:2048], ALU.max, inc=("dve", 1))
    W("vector", "dve", r1)
    r2 = dred(kapcol[0:16, 0:1], qnkn[0:16, 2048:4096], ALU.max, inc=("dve", 1))
    W("vector", "dve", r2)
    r3 = dtt(kcol16[0:16, 0:1], kcol16[0:16, 0:1], kapcol[0:16, 0:1], ALU.add,
             inc=("dve", 1))
    W("vector", "dve", r3)
    ng = dts(kcol16[0:16, 0:1], kcol16[0:16, 0:1], -1.0, ALU.mult, inc=("dve", 1))
    W("tensor", "dve", ng)
    t1 = tr(P1[0:1, 1168:1184], kcol16[0:16, 0:1], idf[0:16, 0:16], inc=("pe", 1))
    W("vector", "pe", t1)
    c1m = dcopy(ksrow[:, :], P1[0:1, 1168:1184], inc=("dve", 1))
    W("tensor", "dve", c1m)
    bc = mm(P1[:, 1280:1296], onesf_col[:, :], ksrow[:, :],
            start=True, stop=True, inc=("pe", 1))
    W("vector", "pe", bc)
    bias_ev = dcopy(bias_sb[:, 0:16], P1[:, 1280:1296], inc=("dve", 1))

    p_sb = [c1[:, 0:2048], c1[:, 2048:4096]]
    pT_sb = [c1[:, 4096:6144], c1[:, 6144:8192]]
    att_slot = [0, 64]
    CHUNK = 1536

    exp_done = [(0, 0, 0, 0)] * (NH + 2)
    pT_free = {0: 0, 1: 0}
    p_dve_free = {0: 0, 1: 0}
    attslot_free = {0: ("dve", 0), 1: ("dve", kn_p0)}
    ae_last = 0
    exp_bank = {0: ("dve", kn_p0), 1: ("dve", kn_p0), 2: ("dve", kn_p0)}
    atok_free = {0: kn_mm, 1: kn_mm}
    wo_done_prev = 0
    resid_done_prev = 0
    resid_of = {}
    ln2stt = {}
    ln2stt = {}
    last_exp = 0
    last_exp2 = 0
    qk_last = 0
    qk_of_h = {}

    def stage_A1(qt, h):
        nonlocal qk_last
        L = (qt + 1) * 128
        L1 = min(L, CHUNK)
        two = L > CHUNK
        dbuf = L <= 512
        soff = (h % 2) * 1024 if dbuf else 0
        par = h % 2
        g = h // 2
        hp = (h % 2) * 64
        bias_ap = bias_sb[:, h:h + 1]
        rhsq = qa[hp:hp + 64, g * 2048 + qt * 128: g * 2048 + (qt + 1) * 128]

        def st_tiles(lo, hi, off, war):
            nonlocal qk_last
            W("tensor", war[0], war[1])
            for jc in range(lo // 128, hi // 128):
                qk_last = mm(P0[:, off + (jc * 128 - lo): off + (jc * 128 - lo) + 128],
                             ka[hp:hp + 64, g * 2048 + jc * 128: g * 2048 + (jc + 1) * 128],
                             rhsq, start=True, stop=True,
                             inc=("pe", 1) if (jc + 1) * 128 >= hi else None)
            return qk_last

        def exp_chunk(lo, hi, off):
            W("scalar", "pe", qk_last)
            W("scalar", "dve", bias_ev)
            W("scalar", "dve", p_dve_free[par])
            return act(pT_sb[par][:, lo:hi], P0[:, off:off + (hi - lo)], AF.Exp,
                       bias=bias_ap, scale=0.125, inc=("act", 1))

        W("tensor", "act", q_ev)
        W("tensor", "act", k_ev)
        if dbuf:
            bk = par * 2
            war = exp_bank[bk]
            W("tensor", war[0], war[1])
            for jc in range(qt):
                mm(P0[:, soff + jc * 128: soff + (jc + 1) * 128],
                   ka[hp:hp + 64, g * 2048 + jc * 128: g * 2048 + (jc + 1) * 128],
                   rhsq, start=True, stop=True)
            dreg = P0[:, soff + qt * 128: soff + (qt + 1) * 128]
            mm(dreg, ka[hp:hp + 64, g * 2048 + qt * 128: g * 2048 + (qt + 1) * 128],
               rhsq, start=True, stop=False)
            qk_last = mm(dreg, negut[:, :], idb[:, :], start=False, stop=True,
                         inc=("pe", 1))
            qk_of_h[(qt, h)] = qk_last
            e1 = exp_chunk(0, L, soff)
            exp_bank[bk] = ("act", e1)
            exp_done[h] = (e1, e1, e1, 0)
            return None
        eA = 0
        e = 0
        for ci in range((L1 + 511) // 512):
            lo = ci * 512
            hi = min(lo + 512, L1)
            st_tiles(lo, hi, lo, exp_bank[ci])
            qk_of_h[(qt, h)] = qk_last
            e = exp_chunk(lo, hi, lo)
            exp_bank[ci] = ("act", e)
            if ci == 0:
                eA = e
        if not two:
            exp_done[h] = (e, e, eA, 0)
            return None
        return (rhsq, bias_ap, eA, e)

    def stage_A2(qt, h, carry):
        nonlocal qk_last
        if carry is None:
            return
        rhsq, bias_ap, eA, eB = carry
        L = (qt + 1) * 128
        par = h % 2
        g = h // 2
        hp = (h % 2) * 64
        L2 = L - CHUNK
        W("tensor", exp_bank[0][0], exp_bank[0][1])
        qk2 = 0
        for jc in range(L2 // 128):
            qk2 = mm(P0[:, jc * 128:(jc + 1) * 128],
                     ka[hp:hp + 64, g * 2048 + CHUNK + jc * 128:
                        g * 2048 + CHUNK + (jc + 1) * 128],
                     rhsq, start=True, stop=True,
                     inc=("pe", 1) if (jc + 1) * 128 >= L2 else None)
        qk_last = qk2
        qk_of_h[(qt, h)] = qk_last
        W("scalar", "pe", qk2)
        W("scalar", "dve", p_dve_free[par])
        exm2 = act(pT_sb[par][:, CHUNK:L], P0[:, 0:L2], AF.Exp,
                   bias=bias_ap, scale=0.125, inc=("act", 1))
        exp_bank[0] = ("act", exm2)
        exp_done[h] = (exm2, exm2, eA, exm2)

    bstate = {}

    def stage_B1(qt, h):
        if (qt + 1) * 128 <= 512:
            bstate[h] = 0
            return
        par = h % 2
        ex_last = exp_done[h][0]
        dcol = qt * 128
        pT_diag = pT_sb[par][:, dcol:dcol + 128]
        W("vector", "act", ex_last)
        W("vector", "pool", consts_pool)
        pm = dtt(pT_diag, pT_diag, trimask[:], ALU.mult, inc=("dve", 1))
        p_dve_free[par] = pm
        bstate[h] = pm

    def stage_B2(qt, h):
        nonlocal ae_last
        L = qt + 1
        par = h % 2
        small = L * 128 <= 512
        ex_last, _, exm1, exm2 = exp_done[h]
        pm = bstate.pop(h)
        W("tensor", "act", ex_last)
        W("tensor", "dve", pm)
        if small:
            W("tensor", "dve", attslot_free[par][1])
        else:
            W("tensor", "dve", ae_last)
        pv_last = 0

        def vfrag(jc):
            o = (h // 2) * 2048 + jc * 128 + (h % 2) * 64
            return xnv[:, o:o + 64]

        if small and par == 1:
            for jc in range(L):
                mm(P0[:, 512:576], pT_sb[1][:, jc * 128:(jc + 1) * 128], vfrag(jc),
                   start=(jc == 0), stop=(jc == L - 1))
            for jc in range(L):
                pv_last = mm(P0[:, 576:577], pT_sb[1][:, jc * 128:(jc + 1) * 128],
                             ones_c[:, :], start=(jc == 0), stop=(jc == L - 1),
                             inc=("pe", 1) if jc == L - 1 else None)
            sums_ap = P0[:, 576:577]
            pv_ap = P0[:, 512:576]
        else:
            a0 = att_slot[par]
            sc0 = 1024 + par
            for jc in range(L):
                mm(P2[:, a0:a0 + 64],
                   pT_sb[par][:, jc * 128:(jc + 1) * 128], vfrag(jc),
                   start=(jc == 0), stop=(jc == L - 1))
                pv_last = mm(P1[:, sc0:sc0 + 1],
                             pT_sb[par][:, jc * 128:(jc + 1) * 128], ones_c[:, :],
                             start=(jc == 0), stop=(jc == L - 1),
                             inc=("pe", 1) if jc == L - 1 else None)
            sums_ap = P1[:, sc0:sc0 + 1]
            pv_ap = P2[:, a0:a0 + 64]
        pT_free[par] = pv_last
        W("vector", "pe", pv_last)
        W("vector", "dve", attslot_free[par][1])
        rc = drecip(swc[:, 2 + par:3 + par], sums_ap, inc=("dve", 1))
        W("vector", "dve", rc)
        if h == 0:
            W("vector", "pe", atok_free[qt % 2])
        ae = dts(atok[:, (qt % 2) * 1024 + h * 64: (qt % 2) * 1024 + (h + 1) * 64],
                 pv_ap, swc[:, 2 + par:3 + par], ALU.mult, inc=("dve", 1))
        attslot_free[par] = ("dve", ae)
        ae_last = ae
        return ae

    def emit_tail(qt, last_ae):
        nonlocal wo_done_prev, resid_done_prev
        at_last, atev = pt_group(
            [atok[:, (qt % 2) * 1024 + dc * 128:(qt % 2) * 1024 + (dc + 1) * 128]
             for dc in range(NDC)],
            lambda PTx, n: (attT[:, :], PTx[:, :]),
            waits=[("dve", last_ae), ("pe", wo_done_prev)], evac="auto")
        atok_free[qt % 2] = at_last
        wo_last = 0
        W("tensor", atev[0], atev[1])
        W("tensor", "dve", wo_evac)
        W("tensor", "dve", resid_done_prev)
        W("tensor", "lr", borow_lr)
        for half in range(2):
            for dc in range(NDC):
                mm(P1[:, half * 512:(half + 1) * 512],
                   attT[:, dc * 128:(dc + 1) * 128],
                   c2[:, dc * 1024 + half * 512: dc * 1024 + half * 512 + 512],
                   start=(dc == 0), stop=False)
            wo_last = mm(P1[:, half * 512:(half + 1) * 512], ones_r[:],
                         borow[:, half * 512:half * 512 + 512], start=False, stop=True,
                         inc=("pe", 1))
        wo_done_prev = wo_last
        W("vector", "pe", wo_last)
        rs = 0
        for half in range(2):
            rs = dtt(x_tm[:, qt * 1024 + half * 512: qt * 1024 + half * 512 + 512],
                     x_tm[:, qt * 1024 + half * 512: qt * 1024 + half * 512 + 512],
                     P1[:, half * 512:(half + 1) * 512], ALU.add, inc=("dve", 1))
        resid_done_prev = rs
        resid_of[qt] = rs

    steps = [(qt, h) for qt in range(KQT0, min(NQT, KQT)) for h in range(KNH)]
    last_ae_of = {}
    b1_emitted = set()
    stt_tile = 0
    stt_phase = 0

    def do_B1(s):
        if s not in b1_emitted:
            b1_emitted.add(s)
            stage_B1(*s)

    def do_B2(s):
        if s not in last_ae_of:
            last_ae_of[s] = stage_B2(*s)
            if s[1] == KNH - 1:
                emit_tail(s[0], last_ae_of[s])

    for idx, (qt, h) in enumerate(steps):
        if KNH == NH and stt_tile <= 14 and stt_tile in resid_of and \
                ((qt >= 5 and h in (5, 11)) or qt == NQT - 1):
            T = stt_tile
            if stt_phase == 0:
                W("vector", "dve", resid_of[T])
                dred(lncols[:, SC + T:SC + T + 1],
                     x_tm[:, T * 1024:(T + 1) * 1024], ALU.add)
                stt_phase = 1
            else:
                W("vector", "dve", resid_of[T])
                W("vector", "dve", ln2stt.get(T - 1, 0))
                ln2stt[T] = dstt(c1[:, 0:1024], x_tm[:, T * 1024:(T + 1) * 1024],
                                 1.0, x_tm[:, T * 1024:(T + 1) * 1024],
                                 ALU.mult, ALU.mult,
                                 accum=lncols[:, SQ + T:SQ + T + 1], inc=("dve", 1))
                stt_phase = 0
                stt_tile += 1
        if idx >= 2:
            do_B2(steps[idx - 2])
        if (qt, h) == (4, 0) and KNH == NH and KQT0 == 0:
            do_B1((3, 15))
            do_B2((3, 15))
            exp_bank[1] = ("dve", last_ae_of[(3, 15)])
        carry = stage_A1(qt, h)
        if idx >= 1:
            do_B1(steps[idx - 1])
        stage_A2(qt, h, carry)
    n = len(steps)
    if n >= 2:
        do_B2(steps[n - 2])
    do_B1(steps[n - 1])
    do_B2(steps[n - 1])

    attn_pe_done = wo_done_prev
    attn_dve_done = resid_done_prev
    if KPHASE == 4:
        early_store(prog.cnt["dve"])
        return

    if 14 in ln2stt:
        ln2stt[15] = ln_stats(15, 0, resid_of[15], ln2stt[14], wait_lx=False,
                              sbuf_discard=True)
    ln2_evac, ln2_tr = layer_norm(attn_pe_done, attn_dve_done, wait_lx=False,
                                  entry_act=prog.cnt["act"],
                                  stats_done=ln2stt.get(15, None))

    if KPHASE == 5:
        early_store(prog.cnt["dve"])
        return

    h_phase_done = {0: qk_last}
    y_phase_done = {}
    w1_loaded = {}
    w2_loaded = {}
    b1_done = {}
    prev_t_b1 = tbk
    W("gpsimd", "pe", attn_pe_done)
    W("gpsimd", "lb", bounce_lb)
    dma("gpsimd", b2t[:, :], bsc_d[None, :].broadcast_to((128, DM)), ("lr", 16))
    b2t_lr = prog.cnt["lr"]

    def emit_b1(ei, e):
        nonlocal prev_t_b1
        bpar = ei % 2
        W("sync", "pe", prev_t_b1)
        W("sync", "lb", bounce_lb)
        dma("sync", b1stage[0:16, bpar * 128:(bpar + 1) * 128],
            b1_d[e, :].rearrange("(t p) -> t p", p=128), ("lb", 16))
        b1_lb = prog.cnt["lb"]
        W("tensor", "lb", b1_lb)
        W("tensor", "dve", b1_done.get(ei - 1, 0))
        if ei == 0:
            W("tensor", "dve", ae_last)
        t_b1 = tr(P1[:, 1168 + bpar * 16: 1184 + bpar * 16],
                  b1stage[0:16, bpar * 128:(bpar + 1) * 128], idf[0:16, 0:16],
                  inc=("pe", 1))
        prev_t_b1 = t_b1
        W("vector", "pe", t_b1)
        W("vector", "act", b1col_readers.get(ei - 2, 0))
        b1_done[ei] = dts(b1col[:, bpar * 16:(bpar + 1) * 16],
                          P1[:, 1168 + bpar * 16: 1184 + bpar * 16], GATES[e], ALU.mult,
                          inc=("dve", 1))

    b1col_readers = {}
    emit_b1(0, EXPERTS[0])

    badd = 0
    for st in range(NST):
        W("vector", "lr", b2t_lr)
        W("vector", "dve", ln2_evac[-1])
        badd = dtt(x_tm[:, st * 1024:(st + 1) * 1024],
                   x_tm[:, st * 1024:(st + 1) * 1024], b2t[:, :], ALU.add,
                   inc=("dve", 1))

    yregion_free = {r: ln2_evac[-1] for r in range(3)}

    for ei, e in enumerate(EXPERTS):
        if ei == 0 and (NQT - 1, NH - 1) in qk_of_h:
            for dc in range(NDC):
                W("gpsimd", "pe", qk_of_h[(NQT - 1, 2 * dc + 1)])
                dma("gpsimd", qa[:, dc * 2048:(dc + 1) * 2048],
                    w1_d[e, dc * 128:(dc + 1) * 128, :], ("lw1", 16))
            w1_loaded[ei] = prog.cnt["lw1"]
            for hc in range(NHC):
                W("gpsimd", "pe", qk_of_h[(NQT - 1, (hc // 2) * 2 + 1)])
                dma("gpsimd", ka[:, hc * 1024:(hc + 1) * 1024],
                    w2_d[e, hc * 128:(hc + 1) * 128, :], ("lw2", 16))
            w2_loaded[ei] = prog.cnt["lw2"]
        else:
            W("gpsimd", "pe", h_phase_done[ei])
            dma("gpsimd", qa[:, :].rearrange("p (dc h) -> p dc h", h=2048),
                w1_d[e].rearrange("(dc p) h -> p dc h", p=128), ("lw1", 16))
            w1_loaded[ei] = prog.cnt["lw1"]
            W("gpsimd", "pe", y_phase_done.get(ei - 1, qk_last))
            dma("gpsimd", ka[:, :].rearrange("p (hc o) -> p hc o", o=1024),
                w2_d[e].rearrange("(hc p) o -> p hc o", p=128), ("lw2", 16))
            w2_loaded[ei] = prog.cnt["lw2"]
        bpar = ei % 2
        if ei > 0:
            emit_b1(ei, e)

        hp_prev = {0: 0, 1: 0}
        w2_prev_groups = 0
        relu_last = 0
        relu_of = {}
        last_h_mm = 0
        last_y_mm = 0
        resid_last = 0
        for sc in range(4):
            for ht in range(NHC):
                reg = ht % 2
                W("tensor", "lw1", w1_loaded[ei])
                W("tensor", "act", hp_prev[reg])
                W("tensor", "dve", ln2_evac[sc * 4 + 3])
                if ei == 0 and sc == 0:
                    W("tensor", "dve", bias_ev)
                for dc in range(NDC):
                    last_h_mm = mm(P1[:, reg * 512:(reg + 1) * 512],
                                   qa[:, dc * 2048 + ht * 128: dc * 2048 + (ht + 1) * 128],
                                   xnv[:, dc * 2048 + sc * 512: dc * 2048 + (sc + 1) * 512],
                                   start=(dc == 0), stop=(dc == NDC - 1),
                                   inc=("pe", 1) if dc == NDC - 1 else None)
                W("scalar", "pe", last_h_mm)
                W("scalar", "pe", w2_prev_groups)
                W("scalar", "dve", b1_done[ei])
                hp_prev[reg] = act(c1[:, ht * 512:(ht + 1) * 512],
                                   P1[:, reg * 512:(reg + 1) * 512], AF.Relu,
                                   bias=b1col[:, bpar * 16 + ht: bpar * 16 + ht + 1],
                                   scale=GATES[e], inc=("act", 1))
                relu_last = hp_prev[reg]
                relu_of[ht] = relu_last
            for stl in range(4):
                st = sc * 4 + stl
                for half in range(2):
                    r = (stl * 2 + half) % 3
                    poff = r * 512
                    W("tensor", "lw2", w2_loaded[ei])
                    W("tensor", "dve", yregion_free[r])
                    for hc in range(NHC):
                        if hc % 4 == 0:
                            W("tensor", "act", relu_of[min(hc + 3, NHC - 1)])
                        last_y_mm = mm(
                            P0[:, poff:poff + 512],
                            c1[:, hc * 512 + stl * 128: hc * 512 + (stl + 1) * 128],
                            ka[:, hc * 1024 + half * 512: hc * 1024 + half * 512 + 512],
                            start=(hc == 0), stop=(hc == NHC - 1),
                            inc=("pe", 1) if hc == NHC - 1 else None)
                    W("vector", "pe", last_y_mm)
                    W("vector", "dve", badd)
                    resid_last = dtt(
                        x_tm[:, st * 1024 + half * 512: st * 1024 + half * 512 + 512],
                        x_tm[:, st * 1024 + half * 512: st * 1024 + half * 512 + 512],
                        P0[:, poff:poff + 512], ALU.add, inc=("dve", 1))
                    yregion_free[r] = resid_last
            w2_prev_groups = last_y_mm
            if ei == len(EXPERTS) - 1:
                for stl in range(4):
                    st = sc * 4 + stl
                    W("sync", "dve", resid_last)
                    dma("sync", y_d.rearrange("(t p) d -> p t d", p=128)[:, st:st + 1, :],
                        x_tm[:, st * 1024:(st + 1) * 1024].rearrange(
                            "p (a d) -> p a d", a=1), ("st", 16))
        h_phase_done[ei + 1] = last_h_mm
        y_phase_done[ei] = last_y_mm
        b2_mm_done = last_y_mm
        b1col_readers[ei] = relu_last

    W("sync", "st", prog.cnt["st"])


_NC_CACHE = None


def kernel(L0_attn_x, L0_attn_Wq, L0_attn_bq, L0_attn_Wk, L0_attn_bk, L0_attn_Wv,
           L0_attn_bv, L0_attn_Wo, L0_attn_bo, L0_ffn_w_gate, L0_ffn_w_noise,
           L0_ffn_W1, L0_ffn_b1, L0_ffn_W2, L0_ffn_b2):
    del L0_ffn_w_gate, L0_ffn_w_noise
    global _NC_CACHE
    if _NC_CACHE is None:
        _NC_CACHE = build_program()
    nc = _NC_CACHE
    x = np.ascontiguousarray(np.asarray(L0_attn_x, dtype=np.float32))
    common = {
        "wq": np.ascontiguousarray(np.asarray(L0_attn_Wq, np.float32)),
        "wk": np.ascontiguousarray(np.asarray(L0_attn_Wk, np.float32)),
        "wv": np.ascontiguousarray(np.asarray(L0_attn_Wv, np.float32)),
        "wo": np.ascontiguousarray(np.asarray(L0_attn_Wo, np.float32)),
        "bq": np.ascontiguousarray(np.asarray(L0_attn_bq, np.float32)),
        "bk": np.ascontiguousarray(np.asarray(L0_attn_bk, np.float32)),
        "bv": np.ascontiguousarray(np.asarray(L0_attn_bv, np.float32)),
        "bo": np.ascontiguousarray(np.asarray(L0_attn_bo, np.float32)),
        "w1": np.ascontiguousarray(np.asarray(L0_ffn_W1, np.float32)),
        "b1": np.ascontiguousarray(np.asarray(L0_ffn_b1, np.float32)),
        "w2": np.ascontiguousarray(np.asarray(L0_ffn_W2, np.float32)),
        "b2": np.ascontiguousarray(np.asarray(L0_ffn_b2, np.float32)),
    }
    in_maps = []
    for b in range(8):
        m = dict(common)
        m["x"] = np.ascontiguousarray(x[b])
        in_maps.append(m)
    res = run_bass_kernel_spmd(nc, in_maps, core_ids=list(range(8)))
    out = np.stack([res.results[b]["y"] for b in range(8)], axis=0)
    return out.astype(np.float32)


# version 35
# speedup vs baseline: 1.0152x; 1.0010x over previous
import os
import math
import contextlib
import numpy as np

import concourse.bass as bass
import concourse.mybir as mybir
from concourse.bass_utils import run_bass_kernel_spmd

F32 = mybir.dt.float32
BF16 = mybir.dt.bfloat16
AF = mybir.ActivationFunctionType
ALU = mybir.AluOpType
AX = mybir.AxisListType

S = 2048
DM = 1024
NH = 16
HD = 64
NST = 16
NDC = 8
NQT = 16
HID = 2048
NHC = 16
EPS = 1e-5

_tv = np.array([100.0, 0.0, 0.0, 0.0], np.float32)
_ex = np.exp(_tv - _tv.max(), dtype=np.float32)
_gk = (_ex / _ex.sum(dtype=np.float32)).astype(np.float32)
GATES = {1: float(_gk[0]), 0: float(_gk[1]), 2: float(_gk[2]), 3: float(_gk[3])}
EXPERTS = [0, 1, 2, 3]

ENGS = ("tensor", "scalar", "vector", "sync", "gpsimd")
KPHASE = int(os.environ.get("KPHASE", "99"))
KQT = int(os.environ.get("KQT", "16"))
KQT0 = int(os.environ.get("KQT0", "0"))
KNH = int(os.environ.get("KNH", "16"))
KSKIP = set(os.environ.get("KSKIP", "").split(","))


class Prog:

    def __init__(self):
        self.q = {e: [] for e in ENGS}
        self.cnt = {}
        self.lastwait = {}
        self.sems = None

    def addsem(self, *names):
        for n in names:
            self.cnt[n] = 0

    def op(self, eng, fn, inc=None):
        if inc is None:
            self.q[eng].append(fn)
            return None
        s, a = inc
        self.cnt[s] += a
        tgt = self.cnt[s]

        def run(e, f=fn, s=s, a=a, self_=self):
            ins = f(e)
            ins.then_inc(self_.sems[s], a)

        self.q[eng].append(run)
        return tgt

    def wait(self, eng, sem, val):
        if val is None or val <= 0:
            return
        assert val <= self.cnt[sem], f"wait target {val} > planned {self.cnt[sem]} for {sem}"
        if self.lastwait.get((eng, sem), 0) >= val:
            return
        self.lastwait[(eng, sem)] = val

        def run(e, s=sem, v=val, self_=self):
            e.wait_ge(self_.sems[s], v)

        self.q[eng].append(run)

    def replay(self, eng, e):
        for f in self.q[eng]:
            f(e)


def build_program():
    nc = bass.Bass("TRN2", target_bir_lowering=False, debug=False)

    x_d = nc.dram_tensor("x", [S, DM], F32, kind="ExternalInput").ap()
    wq_d = nc.dram_tensor("wq", [DM, DM], F32, kind="ExternalInput").ap()
    wk_d = nc.dram_tensor("wk", [DM, DM], F32, kind="ExternalInput").ap()
    wv_d = nc.dram_tensor("wv", [DM, DM], F32, kind="ExternalInput").ap()
    wo_d = nc.dram_tensor("wo", [DM, DM], F32, kind="ExternalInput").ap()
    bq_d = nc.dram_tensor("bq", [DM], F32, kind="ExternalInput").ap()
    bk_d = nc.dram_tensor("bk", [DM], F32, kind="ExternalInput").ap()
    bv_d = nc.dram_tensor("bv", [DM], F32, kind="ExternalInput").ap()
    bo_d = nc.dram_tensor("bo", [DM], F32, kind="ExternalInput").ap()
    w1_d = nc.dram_tensor("w1", [16, DM, HID], F32, kind="ExternalInput").ap()
    b1_d = nc.dram_tensor("b1", [16, HID], F32, kind="ExternalInput").ap()
    w2_d = nc.dram_tensor("w2", [16, HID, DM], F32, kind="ExternalInput").ap()
    b2_d = nc.dram_tensor("b2", [16, DM], F32, kind="ExternalInput").ap()
    y_d = nc.dram_tensor("y", [S, DM], F32, kind="ExternalOutput").ap()
    bsc_d = nc.dram_tensor("bsc", [DM], F32, kind="Internal").ap()

    prog = Prog()
    prog.addsem("lx0", "lx1", "lx2", "lx3", "lw", "lw1", "lw2", "lr", "lb", "st", "pev", "pe", "act", "dve", "pool")

    with contextlib.ExitStack() as stack:
        E = stack.enter_context

        SBUF_BASE = 16512
        cur = [0]

        def at(name, shape, dtype, off=None, adv=True):
            nbytes = (shape[1] * (4 if dtype == F32 else 2) + 31) // 32 * 32
            if off is None:
                off = cur[0]
                if adv:
                    cur[0] += nbytes
            h = nc.alloc_sbuf_tensor_at(name, shape, dtype, offset=SBUF_BASE + off)
            return h

        x_tm = at("x_tm", [128, NST * 1024], F32)
        off_xnv = cur[0]
        xnv = at("xnv", [128, 8 * 2048], BF16)
        off_qa = cur[0]
        qa = at("qa", [128, 8 * 2048], BF16)
        off_ka = cur[0]
        ka = at("ka", [128, 16 * 1024], BF16)
        off_c1 = cur[0]
        c1 = at("c1", [128, 8192], BF16)
        qnkn = at("qnkn", [16, 4096], F32, off=off_c1, adv=False)
        cur[0] = off_c1 + 16384
        off_c2 = cur[0]
        c2 = at("c2", [128, 8192], BF16)
        b2t = at("b2t", [128, 1024], F32, off=off_c2, adv=False)
        atok = at("atok", [128, 2048], BF16)
        attT = at("attT", [128, 1024], BF16, off=off_c1 + 12288, adv=False)
        idb = at("idb", [128, 128], BF16)
        idf = at("idf", [16, 16], F32)
        trimask = at("trimask", [128, 128], BF16)
        negut = at("negut", [128, 128], BF16)
        bias_sb = at("bias_sb", [128, 256], F32)
        lncols = at("lncols", [128, 112], F32)
        swc = at("swc", [128, 8], F32)
        bqcol = at("bqcol", [128, 8], F32)
        bkcol = at("bkcol", [128, 8], F32)
        b1col = at("b1col", [128, 32], F32)
        qkm = at("qkm", [128, 256], BF16)
        ones_r = at("ones_r", [1, 128], BF16)
        ones_c = at("ones_c", [128, 1], BF16)
        onesf16 = at("onesf16", [1, 16], F32)
        onesf_col = at("onesf_col", [1, 128], F32)
        bvrow = at("bvrow", [1, 1024], BF16)
        borow = at("borow", [1, 1024], BF16)
        b2row = at("b2row", [1, 1024], BF16)
        ksrow = at("ksrow", [1, 16], F32)
        kcol16 = at("kcol16", [16, 1], F32)
        kmax1 = at("kmax1", [1, 1], F32)
        kneg1 = at("kneg1", [1, 1], F32)
        kapcol = at("kapcol", [16, 1], F32)
        bstage = at("bstage", [16, 128], F32)
        b1stage = at("b1stage", [16, 256], F32)
        assert cur[0] <= 212830, f"sbuf over budget: {cur[0]}"

        P0 = E(nc.psum_tensor("P0", [128, 1536], F32))
        P1 = E(nc.psum_tensor("P1", [128, 1536], F32))
        PTa = E(nc.psum_tensor("PTa", [128, 1024], BF16))
        P2 = E(nc.psum_tensor("P2", [128, 512], F32))

        sems = {}
        for n in ("lx0", "lx1", "lx2", "lx3", "lw", "lw1", "lw2", "lr", "lb", "st",
                  "pev", "pe", "act", "dve", "pool"):
            sems[n] = E(nc.semaphore(n))
        prog.sems = sems

        plan(prog, locals())

        block = E(nc.Block())

        @block.tensor
        def _(e):
            prog.replay("tensor", e)

        @block.scalar
        def _(e):
            prog.replay("scalar", e)

        @block.vector
        def _(e):
            prog.replay("vector", e)

        @block.sync
        def _(e):
            prog.replay("sync", e)

        @block.gpsimd
        def _(e):
            prog.replay("gpsimd", e)

    return nc


def plan(prog, ns):
    P = prog.op
    W = prog.wait
    x_d, y_d, bsc_d = ns["x_d"], ns["y_d"], ns["bsc_d"]
    wq_d, wk_d, wv_d, wo_d = ns["wq_d"], ns["wk_d"], ns["wv_d"], ns["wo_d"]
    bq_d, bk_d, bv_d, bo_d = ns["bq_d"], ns["bk_d"], ns["bv_d"], ns["bo_d"]
    w1_d, b1_d, w2_d, b2_d = ns["w1_d"], ns["b1_d"], ns["w2_d"], ns["b2_d"]
    x_tm, xnv, qa, ka, c1, qnkn, c2 = (
        ns["x_tm"], ns["xnv"], ns["qa"], ns["ka"], ns["c1"], ns["qnkn"], ns["c2"])
    atok, attT, idb, idf, trimask = ns["atok"], ns["attT"], ns["idb"], ns["idf"], ns["trimask"]
    negut = ns["negut"]
    b2t = ns["b2t"]
    bias_sb, lncols, swc = ns["bias_sb"], ns["lncols"], ns["swc"]
    bqcol, bkcol, b1col, qkm = ns["bqcol"], ns["bkcol"], ns["b1col"], ns["qkm"]
    ones_r, onesf16, ones_c = ns["ones_r"], ns["onesf16"], ns["ones_c"]
    onesf_col = ns["onesf_col"]
    bvrow, borow, b2row = ns["bvrow"], ns["borow"], ns["b2row"]
    ksrow, kcol16, kmax1, kneg1, kapcol = (
        ns["ksrow"], ns["kcol16"], ns["kmax1"], ns["kneg1"], ns["kapcol"])
    bstage, b1stage = ns["bstage"], ns["b1stage"]
    P0, P1, PTa, P2 = ns["P0"], ns["P1"], ns["PTa"], ns["P2"]

    def mm(out, lhsT, rhs, start, stop, inc=None):
        return P("tensor",
                 lambda e, o=out, l=lhsT, r=rhs, s=start, t=stop:
                 e.matmul(o, l, r, start=s, stop=t), inc)

    def tr(out, in_, ident, inc=None):
        return P("tensor",
                 lambda e, o=out, i=in_, d=ident: e.transpose(out=o, in_=i, identity=d), inc)

    def act(out, in_, func, bias=0.0, scale=1.0, accum=None, inc=None):
        return P("scalar",
                 lambda e, o=out, i=in_, f=func, b=bias, s=scale, a=accum:
                 e.activation(out=o, in_=i, func=f, bias=b, scale=s, accum_out=a), inc)

    def dcopy(out, in_, inc=None):
        return P("vector", lambda e, o=out, i=in_: e.tensor_copy(out=o, in_=i), inc)

    def acopy(out, in_, inc=None):
        return P("scalar", lambda e, o=out, i=in_: e.copy(out, in_) if False else e.copy(o, i), inc)

    def dtt(out, in0, in1, aop, inc=None):
        return P("vector",
                 lambda e, o=out, a=in0, b=in1, p=aop: e.tensor_tensor(out=o, in0=a, in1=b, op=p),
                 inc)

    def dts(out, in0, scalar, aop, inc=None, accum=None):
        return P("vector",
                 lambda e, o=out, a=in0, s=scalar, p=aop, ac=accum:
                 e.tensor_scalar(out=o, in0=a, scalar1=s, scalar2=None, op0=p, accum_out=ac), inc)

    def dstt(out, in0, scalar, in1, op0, op1, accum=None, inc=None):
        return P("vector",
                 lambda e, o=out, a=in0, s=scalar, b=in1, p0=op0, p1=op1, ac=accum:
                 e.scalar_tensor_tensor(out=o, in0=a, scalar=s, in1=b, op0=p0, op1=p1,
                                        accum_out=ac), inc)

    def dred(out, in_, aop, inc=None):
        return P("vector",
                 lambda e, o=out, i=in_, p=aop: e.tensor_reduce(out=o, in_=i, axis=AX.X, op=p),
                 inc)

    def drecip(out, in_, inc=None):
        return P("vector", lambda e, o=out, i=in_: e.reciprocal(out=o, in_=i), inc)

    def dma(eng, out, in_, inc):
        return P(eng, lambda e, o=out, i=in_: e.dma_start(out=o, in_=i), inc)

    def gmemset(apx, v, inc=None):
        return P("gpsimd", lambda e, a=apx, vv=v: e.memset(a, vv), inc)

    ptring = {"idx": 0, "free": [("dve", 0)], "bank": None}

    def pt_group(srcs, mk_evac, waits=(), evac="vector"):
        b = ptring["idx"]
        ptring["idx"] = (b + 1) % len(ptring["bank"])
        PTx = ptring["bank"][b]
        fs, fv = ptring["free"][b]
        W("tensor", fs, fv)
        for sem, val in waits:
            W("tensor", sem, val)
        t_last = 0
        for i, s in enumerate(srcs):
            t_last = tr(PTx[:, i * 128:(i + 1) * 128], s, idb[:],
                        inc=("pe", 1) if i == len(srcs) - 1 else None)
        out_ap, in_ap = mk_evac(PTx, len(srcs))
        if evac == "act":
            W("scalar", "pe", t_last)
            ev = ("act", acopy(out_ap, in_ap, inc=("act", 1)))
        else:
            W("vector", "pe", t_last)
            ev = ("dve", dcopy(out_ap, in_ap, inc=("dve", 1)))
        ptring["free"][b] = ev
        return t_last, ev

    ptring["bank"] = [PTa]

    m = gmemset(idb[:], 0.0, inc=("pool", 1))
    W("gpsimd", "pool", m)
    P("gpsimd", lambda e: e.affine_select(out=idb[:], in_=idb[:], compare_op=ALU.not_equal,
                                          fill=1.0, base=0, pattern=[[-1, 128]],
                                          channel_multiplier=1))
    m = gmemset(idf[:], 0.0, inc=("pool", 1))
    W("gpsimd", "pool", m)
    P("gpsimd", lambda e: e.affine_select(out=idf[:], in_=idf[:], compare_op=ALU.not_equal,
                                          fill=1.0, base=0, pattern=[[-1, 16]],
                                          channel_multiplier=1))
    m = gmemset(trimask[:], 1.0, inc=("pool", 1))
    W("gpsimd", "pool", m)
    P("gpsimd", lambda e: e.affine_select(out=trimask[:], in_=trimask[:], compare_op=ALU.is_ge,
                                          fill=0.0, base=0, pattern=[[1, 128]],
                                          channel_multiplier=-1))
    m = gmemset(negut[:], -480.0, inc=("pool", 1))
    W("gpsimd", "pool", m)
    P("gpsimd", lambda e: e.affine_select(out=negut[:], in_=negut[:], compare_op=ALU.is_ge,
                                          fill=0.0, base=-1, pattern=[[1, 128]],
                                          channel_multiplier=-1))
    gmemset(ones_c[:], 1.0)
    gmemset(onesf_col[:], 1.0)
    gmemset(ones_r[:], 1.0)
    gmemset(onesf16[:], 1.0)
    m = gmemset(qkm[:], 0.0, inc=("pool", 1))
    W("gpsimd", "pool", m)
    for g in range(8):
        gmemset(qkm[0:64, g * 16 + 2 * g: g * 16 + 2 * g + 1], 0.0625)
        gmemset(qkm[64:128, g * 16 + 2 * g + 1: g * 16 + 2 * g + 2], 0.0625)
        gmemset(qkm[0:64, 128 + g * 16 + 2 * g: 128 + g * 16 + 2 * g + 1], 0.0625)
        last = gmemset(qkm[64:128, 128 + g * 16 + 2 * g + 1: 128 + g * 16 + 2 * g + 2],
                       0.0625, inc=("pool", 1) if g == 7 else None)
    consts_pool = prog.cnt["pool"]

    dma("gpsimd", bvrow[:], bv_d[None, :], ("lr", 16))
    dma("gpsimd", borow[:], bo_d[None, :], ("lr", 16))
    borow_lr = prog.cnt["lr"]

    x_r = x_d.rearrange("(t p) d -> p t d", p=128)
    xload = []
    for t in range(4):
        dma("sync", x_tm[:, t * 4096:(t + 1) * 4096].rearrange("p (a d) -> p a d", d=1024),
            x_r[:, 4 * t:4 * t + 4, :], (f"lx{t}", 16))
        xload.append((f"lx{t}", prog.cnt[f"lx{t}"]))
    dma("sync", bstage[0:8, :], bq_d.rearrange("(t p) -> t p", p=128), ("lb", 16))
    bq_lb = prog.cnt["lb"]

    def wload(wd):
        dma("gpsimd", c1[:, :].rearrange("p (ot d) -> p ot d", d=1024),
            wd.rearrange("(ot p) d -> p ot d", p=128), ("lw", 16))
        return prog.cnt["lw"]

    wq_lw = wload(wq_d)

    SC, SQ, NM, MSQ, VAR, RSTD, LNV = 0, 16, 32, 48, 64, 80, 96

    def ln_stats(st, entry_pe, entry_dve, prev_stt, wait_lx, sbuf_discard=False):
        if wait_lx:
            W("vector", xload[st // 4][0], xload[st // 4][1])
        W("vector", "pe", entry_pe)
        W("vector", "dve", entry_dve)
        dred(lncols[:, SC + st:SC + st + 1], x_tm[:, st * 1024:(st + 1) * 1024], ALU.add)
        W("vector", "dve", prev_stt)
        out_ap = c1[:, 0:1024] if sbuf_discard else P0[:, 0:1024]
        return dstt(out_ap, x_tm[:, st * 1024:(st + 1) * 1024], 1.0,
                    x_tm[:, st * 1024:(st + 1) * 1024], ALU.mult, ALU.mult,
                    accum=lncols[:, SQ + st:SQ + st + 1], inc=("dve", 1))

    def layer_norm(entry_pe, entry_dve, wait_lx, entry_act=0, stats_done=None):
        grouped = stats_done is None
        evac_dve = []
        prev_ts = 0
        prev_evac = 0
        prev_tr = 0
        tr_half = {0: 0, 1: 0}
        prev_stt = 0
        ngrp = 4 if grouped else 1
        gw = 16 // ngrp
        for grp in range(ngrp):
            c0 = grp * gw
            if grouped:
                for st in range(c0, c0 + gw):
                    prev_stt = ln_stats(st, entry_pe, entry_dve, prev_stt, wait_lx)
                sd = prev_stt
            else:
                sd = stats_done
            W("vector", "dve", sd)
            m1 = dts(lncols[:, NM + c0:NM + c0 + gw], lncols[:, SC + c0:SC + c0 + gw],
                     -1.0 / 1024.0, ALU.mult, inc=("dve", 1))
            W("vector", "dve", m1)
            m2 = dstt(lncols[:, MSQ + c0:MSQ + c0 + gw], lncols[:, NM + c0:NM + c0 + gw],
                      1.0, lncols[:, NM + c0:NM + c0 + gw], ALU.mult, ALU.mult,
                      inc=("dve", 1))
            m3 = dts(lncols[:, SQ + c0:SQ + c0 + gw], lncols[:, SQ + c0:SQ + c0 + gw],
                     1.0 / 1024.0, ALU.mult, inc=("dve", 1))
            W("vector", "dve", m3)
            m4 = dstt(lncols[:, VAR + c0:VAR + c0 + gw], lncols[:, SQ + c0:SQ + c0 + gw],
                      EPS, lncols[:, MSQ + c0:MSQ + c0 + gw], ALU.add, ALU.subtract,
                      inc=("dve", 1))
            W("scalar", "dve", m4)
            a1 = act(lncols[:, LNV + c0:LNV + c0 + gw], lncols[:, VAR + c0:VAR + c0 + gw],
                     AF.Ln, bias=0.0, scale=1.0, inc=("act", 1))
            W("scalar", "act", a1)
            a2 = act(lncols[:, RSTD + c0:RSTD + c0 + gw], lncols[:, LNV + c0:LNV + c0 + gw],
                     AF.Exp, bias=0.0, scale=-0.5, inc=("act", 1))
            W("vector", "act", a2)
            m5 = dtt(lncols[:, MSQ + c0:MSQ + c0 + gw], lncols[:, NM + c0:NM + c0 + gw],
                     lncols[:, RSTD + c0:RSTD + c0 + gw], ALU.mult, inc=("dve", 1))
            for st in range(c0, c0 + gw):
                half = st % 2
                xn_tmp = atok[:, half * 1024:(half + 1) * 1024]
                W("scalar", "dve", m5)
                W("scalar", "pe", tr_half[half])
                prev_ts = act(xn_tmp[:, 0:1024], x_tm[:, st * 1024:(st + 1) * 1024],
                              AF.Identity, bias=lncols[:, MSQ + st:MSQ + st + 1],
                              scale=lncols[:, RSTD + st:RSTD + st + 1], inc=("act", 1))
                waits = [("act", prev_ts), ("pool", consts_pool)]
                if entry_act:
                    waits.append(("act", entry_act))
                prev_tr, ev = pt_group(
                    [xn_tmp[:, dc * 128:(dc + 1) * 128] for dc in range(NDC)],
                    lambda PTx, n, st=st: (
                        xnv[:, :].rearrange("p (dc c) -> p dc c", c=2048)[
                            :, :, st * 128:(st + 1) * 128],
                        PTx[:, :].rearrange("p (dc c) -> p dc c", c=128)),
                    waits=waits)
                tr_half[half] = prev_tr
                prev_evac = ev[1]
                evac_dve.append(prev_evac)
        return evac_dve, prev_tr

    def early_store(dve_gate):
        for st in range(NST):
            W("sync", "dve", dve_gate)
            dma("sync", y_d.rearrange("(t p) d -> p t d", p=128)[:, st:st + 1, :],
                x_tm[:, st * 1024:(st + 1) * 1024].rearrange(
                    "p (a d) -> p a d", a=1), ("st", 16))
        W("sync", "st", prog.cnt["st"])

    ln1_evac, ln1_tr = layer_norm(0, 0, wait_lx=True)
    if KPHASE == 1:
        early_store(prog.cnt["dve"])
        return

    def build_wt(load_lw):
        tr_last = 0
        ev = 0
        for dc in range(NDC):
            tr_last, ev = pt_group(
                [c1[:, ot * 1024 + dc * 128: ot * 1024 + (dc + 1) * 128]
                 for ot in range(8)],
                lambda PTx, n, dc=dc: (c2[:, dc * 1024:(dc + 1) * 1024], PTx[:, :]),
                waits=[("lw", load_lw)])
        return tr_last, ev[1]

    W("tensor", "lb", bq_lb)
    tbq = tr(P1[:, 1168:1176], bstage[0:8, :], idf[0:8, 0:8], inc=("pe", 1))
    W("vector", "pe", tbq)
    ebq = dcopy(bqcol[:, 0:8], P1[:, 1168:1176], inc=("dve", 1))
    W("sync", "pe", tbq)
    dma("sync", bstage[0:8, :], bk_d.rearrange("(t p) -> t p", p=128), ("lb", 16))
    bk_lb = prog.cnt["lb"]

    wq_tr, wq_evac = build_wt(wq_lw)
    W("tensor", "lb", bk_lb)
    W("tensor", "dve", ebq)
    tbk = tr(P1[:, 1168:1176], bstage[0:8, :], idf[0:8, 0:8], inc=("pe", 1))
    W("vector", "pe", tbk)
    ebk = dcopy(bkcol[:, 0:8], P1[:, 1168:1176], inc=("dve", 1))

    W("sync", "pe", tbk)
    comb = 0
    first = True
    for e in EXPERTS:
        slot = b1stage[0:8, 0:128] if first else b1stage[0:8, 128:256]
        dma("sync", slot, b2_d[e, :].rearrange("(t p) -> t p", p=128), ("lb", 16))
        l_ = prog.cnt["lb"]
        W("vector", "lb", l_)
        W("vector", "dve", comb)
        m_ = dts(slot, slot, GATES[e], ALU.mult, inc=("dve", 1))
        if first:
            comb = m_
            first = False
        else:
            W("vector", "dve", m_)
            comb = dtt(b1stage[0:8, 0:128], b1stage[0:8, 0:128], b1stage[0:8, 128:256],
                       ALU.add, inc=("dve", 1))
        if e != EXPERTS[-1]:
            W("sync", "dve", comb)
    W("sync", "dve", comb)
    dma("sync", bsc_d.rearrange("(t p) -> t p", p=128), b1stage[0:8, 0:128], ("lb", 16))
    bounce_lb = prog.cnt["lb"]

    W("gpsimd", "pe", wq_tr)
    wk_lw = wload(wk_d)

    def proj_fm(dest, dest_off, bcol, ln_evacs, w_evac, entry_act=0):
        prev_ev = {0: entry_act, 1: entry_act}
        last_mm = 0
        last_ev = 0
        for sc in range(4):
            for ot in range(8):
                reg = (sc * 8 + ot) % 2
                W("tensor", "dve", w_evac)
                W("tensor", "dve", ln_evacs[sc * 4 + 3])
                W("tensor", "act", prev_ev[reg])
                for dc in range(NDC):
                    last_mm = mm(P1[:, reg * 512:(reg + 1) * 512],
                                 c2[:, dc * 1024 + ot * 128: dc * 1024 + (ot + 1) * 128],
                                 xnv[:, dc * 2048 + sc * 512: dc * 2048 + (sc + 1) * 512],
                                 start=(dc == 0), stop=(dc == NDC - 1),
                                 inc=("pe", 1) if dc == NDC - 1 else None)
                W("scalar", "pe", last_mm)
                prev_ev[reg] = act(dest[:, ot * 2048 + sc * 512: ot * 2048 + (sc + 1) * 512],
                                   P1[:, reg * 512:(reg + 1) * 512], AF.Identity,
                                   bias=bcol[:, ot:ot + 1], scale=1.0, inc=("act", 1))
                last_ev = prev_ev[reg]
        return last_mm, last_ev

    W("scalar", "dve", ebq)
    q_mm, q_ev = proj_fm(qa, 0, bqcol, ln1_evac, wq_evac)

    wk_tr, wk_evac = build_wt(wk_lw)
    W("gpsimd", "pe", wk_tr)
    wv_lw = wload(wv_d)

    W("scalar", "dve", ebk)
    k_mm, k_ev = proj_fm(ka, 0, bkcol, ln1_evac, wk_evac, entry_act=q_ev)

    wv_tr, wv_evac = build_wt(wv_lw)
    W("gpsimd", "pe", wv_tr)
    wo_lw = wload(wo_d)

    vregions = [(P1, 0), (P1, 512), (P0, 0), (P0, 512)]
    v_prev_ev = {0: 0, 1: 0, 2: 0, 3: 0}
    v_last_ev = 0
    v_last_mm = 0
    for st in range(NST):
        grp = []
        for half in range(2):
            r = (st * 2 + half) % 4
            pbuf, poff = vregions[r]
            W("tensor", "dve", wv_evac)
            W("tensor", "dve", v_prev_ev[r])
            W("tensor", "lr", borow_lr)
            if st == 0 and half == 0:
                W("tensor", "pe", k_mm)
                W("tensor", "act", k_ev)
            for dc in range(NDC):
                mm(pbuf[:, poff:poff + 512],
                   xnv[:, dc * 2048 + st * 128: dc * 2048 + (st + 1) * 128],
                   c2[:, dc * 1024 + half * 512: dc * 1024 + half * 512 + 512],
                   start=(dc == 0), stop=False)
            v_last_mm = mm(pbuf[:, poff:poff + 512], ones_r[:],
                           bvrow[:, half * 512:half * 512 + 512],
                           start=False, stop=True, inc=("pe", 1))
            grp.append((r, pbuf, poff))
        for half in range(2):
            r, pbuf, poff = grp[half]
            W("vector", "pe", v_last_mm)
            dest = xnv[:, :].rearrange("p (dc c) -> p dc c", c=2048)[
                :, 4 * half:4 * half + 4, st * 128:(st + 1) * 128]
            src = pbuf[:, poff:poff + 512].rearrange("p (dc c) -> p dc c", c=128)
            v_prev_ev[r] = dcopy(dest, src, inc=("dve", 1))
            v_last_ev = v_prev_ev[r]

    wo_tr, wo_evac = build_wt(wo_lw)
    if KPHASE == 2:
        early_store(prog.cnt["dve"])
        return

    scr_free = {0: ln1_tr, 1: wo_tr}

    def scr_ap(b, lo, hi):
        return atok[:, lo:hi] if b == 0 else c1[:, 4096 + lo:4096 + hi]

    def norms_prep(src_t, mask_coloff, psum_dve_entry):
        last_mm = 0
        for g in range(8):
            b = g % 2
            halves = []
            for hf in range(2):
                W("vector", "pe", scr_free[b])
                W("vector", "dve", v_last_ev)
                lo = g * 2048 + hf * 1024
                halves.append(dtt(scr_ap(b, hf * 1024, hf * 1024 + 1024),
                                  src_t[:, lo:lo + 1024], src_t[:, lo:lo + 1024],
                                  ALU.mult, inc=("dve", 1)))
            if g == 0:
                W("tensor", "dve", psum_dve_entry)
            for c4 in range(4):
                W("tensor", "dve", halves[c4 // 2])
                out_ap = (P0[0:16, c4 * 512:(c4 + 1) * 512] if c4 < 3
                          else P1[0:16, 0:512])
                last_mm = mm(out_ap,
                             qkm[:, mask_coloff + g * 16: mask_coloff + (g + 1) * 16],
                             scr_ap(b, c4 * 512, (c4 + 1) * 512),
                             start=(g == 0), stop=(g == 7),
                             inc=("pe", 1) if c4 == 3 else None)
            scr_free[b] = last_mm
        return last_mm

    qn_mm = norms_prep(qa, 0, v_last_ev)
    W("vector", "pe", qn_mm)
    dcopy(qnkn[0:16, 0:1536], P0[0:16, 0:1536], inc=("dve", 1))
    qn_ev = dcopy(qnkn[0:16, 1536:2048], P1[0:16, 0:512], inc=("dve", 1))
    kn_mm = norms_prep(ka, 128, qn_ev)
    W("vector", "pe", kn_mm)
    kn_p0 = dcopy(qnkn[0:16, 2048:2048 + 1536], P0[0:16, 0:1536], inc=("dve", 1))
    kn_ev = dcopy(qnkn[0:16, 2048 + 1536:4096], P1[0:16, 0:512], inc=("dve", 1))
    W("vector", "dve", kn_ev)
    r1 = dred(kcol16[0:16, 0:1], qnkn[0:16, 0:2048], ALU.max, inc=("dve", 1))
    W("vector", "dve", r1)
    r2 = dred(kapcol[0:16, 0:1], qnkn[0:16, 2048:4096], ALU.max, inc=("dve", 1))
    W("vector", "dve", r2)
    r3 = dtt(kcol16[0:16, 0:1], kcol16[0:16, 0:1], kapcol[0:16, 0:1], ALU.add,
             inc=("dve", 1))
    W("vector", "dve", r3)
    ng = dts(kcol16[0:16, 0:1], kcol16[0:16, 0:1], -1.0, ALU.mult, inc=("dve", 1))
    W("tensor", "dve", ng)
    t1 = tr(P1[0:1, 1168:1184], kcol16[0:16, 0:1], idf[0:16, 0:16], inc=("pe", 1))
    W("vector", "pe", t1)
    c1m = dcopy(ksrow[:, :], P1[0:1, 1168:1184], inc=("dve", 1))
    W("tensor", "dve", c1m)
    bc = mm(P1[:, 1280:1296], onesf_col[:, :], ksrow[:, :],
            start=True, stop=True, inc=("pe", 1))
    W("vector", "pe", bc)
    bias_ev = dcopy(bias_sb[:, 0:16], P1[:, 1280:1296], inc=("dve", 1))

    p_sb = [c1[:, 0:2048], c1[:, 2048:4096]]
    pT_sb = [c1[:, 4096:6144], c1[:, 6144:8192]]
    att_slot = [0, 64]
    CHUNK = 1536

    exp_done = [(0, 0, 0, 0)] * (NH + 2)
    pT_free = {0: 0, 1: 0}
    p_dve_free = {0: 0, 1: 0}
    attslot_free = {0: ("dve", 0), 1: ("dve", kn_p0)}
    ae_last = 0
    exp_bank = {0: ("dve", kn_p0), 1: ("dve", kn_p0), 2: ("dve", kn_p0)}
    atok_free = {0: kn_mm, 1: kn_mm}
    wo_done_prev = 0
    resid_done_prev = 0
    resid_of = {}
    ln2stt = {}
    ln2stt = {}
    last_exp = 0
    last_exp2 = 0
    qk_last = 0
    qk_of_h = {}

    def stage_A1(qt, h):
        nonlocal qk_last
        L = (qt + 1) * 128
        L1 = min(L, CHUNK)
        two = L > CHUNK
        dbuf = L <= 512
        soff = (h % 2) * 1024 if dbuf else 0
        par = h % 2
        g = h // 2
        hp = (h % 2) * 64
        bias_ap = bias_sb[:, h:h + 1]
        rhsq = qa[hp:hp + 64, g * 2048 + qt * 128: g * 2048 + (qt + 1) * 128]

        def st_tiles(lo, hi, off, war):
            nonlocal qk_last
            W("tensor", war[0], war[1])
            for jc in range(lo // 128, hi // 128):
                qk_last = mm(P0[:, off + (jc * 128 - lo): off + (jc * 128 - lo) + 128],
                             ka[hp:hp + 64, g * 2048 + jc * 128: g * 2048 + (jc + 1) * 128],
                             rhsq, start=True, stop=True,
                             inc=("pe", 1) if (jc + 1) * 128 >= hi else None)
            return qk_last

        def exp_chunk(lo, hi, off):
            W("scalar", "pe", qk_last)
            W("scalar", "dve", bias_ev)
            W("scalar", "dve", p_dve_free[par])
            return act(pT_sb[par][:, lo:hi], P0[:, off:off + (hi - lo)], AF.Exp,
                       bias=bias_ap, scale=0.125, inc=("act", 1))

        W("tensor", "act", q_ev)
        W("tensor", "act", k_ev)
        if dbuf:
            bk = par * 2
            war = exp_bank[bk]
            W("tensor", war[0], war[1])
            for jc in range(qt):
                mm(P0[:, soff + jc * 128: soff + (jc + 1) * 128],
                   ka[hp:hp + 64, g * 2048 + jc * 128: g * 2048 + (jc + 1) * 128],
                   rhsq, start=True, stop=True)
            dreg = P0[:, soff + qt * 128: soff + (qt + 1) * 128]
            mm(dreg, ka[hp:hp + 64, g * 2048 + qt * 128: g * 2048 + (qt + 1) * 128],
               rhsq, start=True, stop=False)
            qk_last = mm(dreg, negut[:, :], idb[:, :], start=False, stop=True,
                         inc=("pe", 1))
            qk_of_h[(qt, h)] = qk_last
            e1 = exp_chunk(0, L, soff)
            exp_bank[bk] = ("act", e1)
            exp_done[h] = (e1, e1, e1, 0)
            return None
        eA = 0
        e = 0
        for ci in range((L1 + 511) // 512):
            lo = ci * 512
            hi = min(lo + 512, L1)
            st_tiles(lo, hi, lo, exp_bank[ci])
            qk_of_h[(qt, h)] = qk_last
            e = exp_chunk(lo, hi, lo)
            exp_bank[ci] = ("act", e)
            if ci == 0:
                eA = e
        if not two:
            exp_done[h] = (e, e, eA, 0)
            return None
        return (rhsq, bias_ap, eA, e)

    def stage_A2(qt, h, carry):
        nonlocal qk_last
        if carry is None:
            return
        rhsq, bias_ap, eA, eB = carry
        L = (qt + 1) * 128
        par = h % 2
        g = h // 2
        hp = (h % 2) * 64
        L2 = L - CHUNK
        W("tensor", exp_bank[0][0], exp_bank[0][1])
        qk2 = 0
        for jc in range(L2 // 128):
            qk2 = mm(P0[:, jc * 128:(jc + 1) * 128],
                     ka[hp:hp + 64, g * 2048 + CHUNK + jc * 128:
                        g * 2048 + CHUNK + (jc + 1) * 128],
                     rhsq, start=True, stop=True,
                     inc=("pe", 1) if (jc + 1) * 128 >= L2 else None)
        qk_last = qk2
        qk_of_h[(qt, h)] = qk_last
        W("scalar", "pe", qk2)
        W("scalar", "dve", p_dve_free[par])
        exm2 = act(pT_sb[par][:, CHUNK:L], P0[:, 0:L2], AF.Exp,
                   bias=bias_ap, scale=0.125, inc=("act", 1))
        exp_bank[0] = ("act", exm2)
        exp_done[h] = (exm2, exm2, eA, exm2)

    bstate = {}

    def stage_B1(qt, h):
        if (qt + 1) * 128 <= 512:
            bstate[h] = 0
            return
        par = h % 2
        ex_last = exp_done[h][0]
        dcol = qt * 128
        pT_diag = pT_sb[par][:, dcol:dcol + 128]
        W("vector", "act", ex_last)
        W("vector", "pool", consts_pool)
        pm = dtt(pT_diag, pT_diag, trimask[:], ALU.mult, inc=("dve", 1))
        p_dve_free[par] = pm
        bstate[h] = pm

    def stage_B2(qt, h):
        nonlocal ae_last
        L = qt + 1
        par = h % 2
        small = L * 128 <= 512
        ex_last, _, exm1, exm2 = exp_done[h]
        pm = bstate.pop(h)
        W("tensor", "act", ex_last)
        W("tensor", "dve", pm)
        if small:
            W("tensor", "dve", attslot_free[par][1])
        else:
            W("tensor", "dve", ae_last)
        pv_last = 0

        def vfrag(jc):
            o = (h // 2) * 2048 + jc * 128 + (h % 2) * 64
            return xnv[:, o:o + 64]

        if small and par == 1:
            for jc in range(L):
                mm(P0[:, 512:576], pT_sb[1][:, jc * 128:(jc + 1) * 128], vfrag(jc),
                   start=(jc == 0), stop=(jc == L - 1))
            for jc in range(L):
                pv_last = mm(P0[:, 576:577], pT_sb[1][:, jc * 128:(jc + 1) * 128],
                             ones_c[:, :], start=(jc == 0), stop=(jc == L - 1),
                             inc=("pe", 1) if jc == L - 1 else None)
            sums_ap = P0[:, 576:577]
            pv_ap = P0[:, 512:576]
        else:
            a0 = att_slot[par]
            sc0 = 1024 + par
            for jc in range(L):
                mm(P2[:, a0:a0 + 64],
                   pT_sb[par][:, jc * 128:(jc + 1) * 128], vfrag(jc),
                   start=(jc == 0), stop=(jc == L - 1))
                pv_last = mm(P1[:, sc0:sc0 + 1],
                             pT_sb[par][:, jc * 128:(jc + 1) * 128], ones_c[:, :],
                             start=(jc == 0), stop=(jc == L - 1),
                             inc=("pe", 1) if jc == L - 1 else None)
            sums_ap = P1[:, sc0:sc0 + 1]
            pv_ap = P2[:, a0:a0 + 64]
        pT_free[par] = pv_last
        W("vector", "pe", pv_last)
        W("vector", "dve", attslot_free[par][1])
        rc = drecip(swc[:, 2 + par:3 + par], sums_ap, inc=("dve", 1))
        W("vector", "dve", rc)
        if h == 0:
            W("vector", "pe", atok_free[qt % 2])
        ae = dts(atok[:, (qt % 2) * 1024 + h * 64: (qt % 2) * 1024 + (h + 1) * 64],
                 pv_ap, swc[:, 2 + par:3 + par], ALU.mult, inc=("dve", 1))
        attslot_free[par] = ("dve", ae)
        ae_last = ae
        return ae

    def emit_tail(qt, last_ae):
        nonlocal wo_done_prev, resid_done_prev
        at_last, atev = pt_group(
            [atok[:, (qt % 2) * 1024 + dc * 128:(qt % 2) * 1024 + (dc + 1) * 128]
             for dc in range(NDC)],
            lambda PTx, n: (attT[:, :], PTx[:, :]),
            waits=[("dve", last_ae), ("pe", wo_done_prev)], evac="auto")
        atok_free[qt % 2] = at_last
        wo_last = 0
        W("tensor", atev[0], atev[1])
        W("tensor", "dve", wo_evac)
        W("tensor", "dve", resid_done_prev)
        W("tensor", "lr", borow_lr)
        for half in range(2):
            for dc in range(NDC):
                mm(P1[:, half * 512:(half + 1) * 512],
                   attT[:, dc * 128:(dc + 1) * 128],
                   c2[:, dc * 1024 + half * 512: dc * 1024 + half * 512 + 512],
                   start=(dc == 0), stop=False)
            wo_last = mm(P1[:, half * 512:(half + 1) * 512], ones_r[:],
                         borow[:, half * 512:half * 512 + 512], start=False, stop=True,
                         inc=("pe", 1))
        wo_done_prev = wo_last
        W("vector", "pe", wo_last)
        rs = 0
        for half in range(2):
            rs = dtt(x_tm[:, qt * 1024 + half * 512: qt * 1024 + half * 512 + 512],
                     x_tm[:, qt * 1024 + half * 512: qt * 1024 + half * 512 + 512],
                     P1[:, half * 512:(half + 1) * 512], ALU.add, inc=("dve", 1))
        resid_done_prev = rs
        resid_of[qt] = rs

    steps = [(qt, h) for qt in range(KQT0, min(NQT, KQT)) for h in range(KNH)]
    last_ae_of = {}
    b1_emitted = set()
    stt_tile = 0
    stt_phase = 0

    def do_B1(s):
        if s not in b1_emitted:
            b1_emitted.add(s)
            stage_B1(*s)

    def do_B2(s):
        if s not in last_ae_of:
            last_ae_of[s] = stage_B2(*s)
            if s[1] == KNH - 1:
                emit_tail(s[0], last_ae_of[s])

    for idx, (qt, h) in enumerate(steps):
        if KNH == NH and stt_tile <= 14 and stt_tile in resid_of and \
                ((qt >= 5 and h in (5, 11)) or qt == NQT - 1):
            T = stt_tile
            if stt_phase == 0:
                W("vector", "dve", resid_of[T])
                dred(lncols[:, SC + T:SC + T + 1],
                     x_tm[:, T * 1024:(T + 1) * 1024], ALU.add)
                stt_phase = 1
            else:
                W("vector", "dve", resid_of[T])
                W("vector", "dve", ln2stt.get(T - 1, 0))
                ln2stt[T] = dstt(c1[:, 0:1024], x_tm[:, T * 1024:(T + 1) * 1024],
                                 1.0, x_tm[:, T * 1024:(T + 1) * 1024],
                                 ALU.mult, ALU.mult,
                                 accum=lncols[:, SQ + T:SQ + T + 1], inc=("dve", 1))
                stt_phase = 0
                stt_tile += 1
        if idx >= 2:
            do_B2(steps[idx - 2])
        if (qt, h) == (4, 0) and KNH == NH and KQT0 == 0:
            do_B1((3, 15))
            do_B2((3, 15))
            exp_bank[1] = ("dve", last_ae_of[(3, 15)])
        carry = stage_A1(qt, h)
        if idx >= 1:
            do_B1(steps[idx - 1])
        stage_A2(qt, h, carry)
    n = len(steps)
    if n >= 2:
        do_B2(steps[n - 2])
    do_B1(steps[n - 1])
    do_B2(steps[n - 1])

    attn_pe_done = wo_done_prev
    attn_dve_done = resid_done_prev
    if KPHASE == 4:
        early_store(prog.cnt["dve"])
        return

    if 14 in ln2stt:
        ln2stt[15] = ln_stats(15, 0, resid_of[15], ln2stt[14], wait_lx=False,
                              sbuf_discard=True)
    ln2_evac, ln2_tr = layer_norm(attn_pe_done, attn_dve_done, wait_lx=False,
                                  entry_act=prog.cnt["act"],
                                  stats_done=ln2stt.get(15, None))

    if KPHASE == 5:
        early_store(prog.cnt["dve"])
        return

    h_phase_done = {0: qk_last}
    y_phase_done = {}
    w1_loaded = {}
    w2_loaded = {}
    b1_done = {}
    prev_t_b1 = tbk
    W("gpsimd", "pe", attn_pe_done)
    W("gpsimd", "lb", bounce_lb)
    dma("gpsimd", b2t[:, :], bsc_d[None, :].broadcast_to((128, DM)), ("lr", 16))
    b2t_lr = prog.cnt["lr"]

    def emit_b1(ei, e):
        nonlocal prev_t_b1
        bpar = ei % 2
        W("sync", "pe", prev_t_b1)
        W("sync", "lb", bounce_lb)
        dma("sync", b1stage[0:16, bpar * 128:(bpar + 1) * 128],
            b1_d[e, :].rearrange("(t p) -> t p", p=128), ("lb", 16))
        b1_lb = prog.cnt["lb"]
        W("tensor", "lb", b1_lb)
        W("tensor", "dve", b1_done.get(ei - 1, 0))
        if ei == 0:
            W("tensor", "dve", ae_last)
        t_b1 = tr(P1[:, 1168 + bpar * 16: 1184 + bpar * 16],
                  b1stage[0:16, bpar * 128:(bpar + 1) * 128], idf[0:16, 0:16],
                  inc=("pe", 1))
        prev_t_b1 = t_b1
        W("vector", "pe", t_b1)
        W("vector", "act", b1col_readers.get(ei - 2, 0))
        b1_done[ei] = dts(b1col[:, bpar * 16:(bpar + 1) * 16],
                          P1[:, 1168 + bpar * 16: 1184 + bpar * 16], GATES[e], ALU.mult,
                          inc=("dve", 1))

    b1col_readers = {}
    emit_b1(0, EXPERTS[0])

    badd = 0
    for st in range(NST):
        W("vector", "lr", b2t_lr)
        W("vector", "dve", ln2_evac[-1])
        badd = dtt(x_tm[:, st * 1024:(st + 1) * 1024],
                   x_tm[:, st * 1024:(st + 1) * 1024], b2t[:, :], ALU.add,
                   inc=("dve", 1))

    yregion_free = {r: ln2_evac[-1] for r in range(3)}

    for ei, e in enumerate(EXPERTS):
        if ei == 0 and (NQT - 1, NH - 1) in qk_of_h:
            for dc in range(NDC):
                W("gpsimd", "pe", qk_of_h[(NQT - 1, 2 * dc + 1)])
                dma("gpsimd", qa[:, dc * 2048:(dc + 1) * 2048],
                    w1_d[e, dc * 128:(dc + 1) * 128, :], ("lw1", 16))
            w1_loaded[ei] = prog.cnt["lw1"]
            for hc in range(NHC):
                W("gpsimd", "pe", qk_of_h[(NQT - 1, (hc // 2) * 2 + 1)])
                dma("gpsimd", ka[:, hc * 1024:(hc + 1) * 1024],
                    w2_d[e, hc * 128:(hc + 1) * 128, :], ("lw2", 16))
            w2_loaded[ei] = prog.cnt["lw2"]
        else:
            W("gpsimd", "pe", h_phase_done[ei])
            dma("gpsimd", qa[:, :].rearrange("p (dc h) -> p dc h", h=2048),
                w1_d[e].rearrange("(dc p) h -> p dc h", p=128), ("lw1", 16))
            w1_loaded[ei] = prog.cnt["lw1"]
            W("gpsimd", "pe", y_phase_done.get(ei - 1, qk_last))
            dma("gpsimd", ka[:, :].rearrange("p (hc o) -> p hc o", o=1024),
                w2_d[e].rearrange("(hc p) o -> p hc o", p=128), ("lw2", 16))
            w2_loaded[ei] = prog.cnt["lw2"]
        bpar = ei % 2
        if ei > 0:
            emit_b1(ei, e)

        hp_prev = {0: 0, 1: 0}
        w2_prev_groups = 0
        relu_last = 0
        relu_of = {}
        last_h_mm = 0
        last_y_mm = 0
        resid_last = 0
        for sc in range(4):
            for ht in range(NHC):
                reg = ht % 2
                W("tensor", "lw1", w1_loaded[ei])
                W("tensor", "act", hp_prev[reg])
                W("tensor", "dve", ln2_evac[sc * 4 + 3])
                if ei == 0 and sc == 0:
                    W("tensor", "dve", bias_ev)
                for dc in range(NDC):
                    last_h_mm = mm(P1[:, reg * 512:(reg + 1) * 512],
                                   qa[:, dc * 2048 + ht * 128: dc * 2048 + (ht + 1) * 128],
                                   xnv[:, dc * 2048 + sc * 512: dc * 2048 + (sc + 1) * 512],
                                   start=(dc == 0), stop=(dc == NDC - 1),
                                   inc=("pe", 1) if dc == NDC - 1 else None)
                W("scalar", "pe", last_h_mm)
                W("scalar", "pe", w2_prev_groups)
                W("scalar", "dve", b1_done[ei])
                hp_prev[reg] = act(c1[:, ht * 512:(ht + 1) * 512],
                                   P1[:, reg * 512:(reg + 1) * 512], AF.Relu,
                                   bias=b1col[:, bpar * 16 + ht: bpar * 16 + ht + 1],
                                   scale=GATES[e], inc=("act", 1))
                relu_last = hp_prev[reg]
                relu_of[ht] = relu_last
            for stl in range(4):
                st = sc * 4 + stl
                for half in range(2):
                    r = (stl * 2 + half) % 3
                    poff = r * 512
                    W("tensor", "lw2", w2_loaded[ei])
                    W("tensor", "dve", yregion_free[r])
                    for hc in range(NHC):
                        if hc % 4 == 0:
                            W("tensor", "act", relu_of[min(hc + 3, NHC - 1)])
                        last_y_mm = mm(
                            P0[:, poff:poff + 512],
                            c1[:, hc * 512 + stl * 128: hc * 512 + (stl + 1) * 128],
                            ka[:, hc * 1024 + half * 512: hc * 1024 + half * 512 + 512],
                            start=(hc == 0), stop=(hc == NHC - 1),
                            inc=("pe", 1) if hc == NHC - 1 else None)
                    W("vector", "pe", last_y_mm)
                    W("vector", "dve", badd)
                    resid_last = dtt(
                        x_tm[:, st * 1024 + half * 512: st * 1024 + half * 512 + 512],
                        x_tm[:, st * 1024 + half * 512: st * 1024 + half * 512 + 512],
                        P0[:, poff:poff + 512], ALU.add, inc=("dve", 1))
                    yregion_free[r] = resid_last
            w2_prev_groups = last_y_mm
            if ei == len(EXPERTS) - 1:
                for stl in range(4):
                    st = sc * 4 + stl
                    W("sync", "dve", resid_last)
                    dma("sync", y_d.rearrange("(t p) d -> p t d", p=128)[:, st:st + 1, :],
                        x_tm[:, st * 1024:(st + 1) * 1024].rearrange(
                            "p (a d) -> p a d", a=1), ("st", 16))
        h_phase_done[ei + 1] = last_h_mm
        y_phase_done[ei] = last_y_mm
        b2_mm_done = last_y_mm
        b1col_readers[ei] = relu_last

    W("sync", "st", prog.cnt["st"])


_NC_CACHE = None


def kernel(L0_attn_x, L0_attn_Wq, L0_attn_bq, L0_attn_Wk, L0_attn_bk, L0_attn_Wv,
           L0_attn_bv, L0_attn_Wo, L0_attn_bo, L0_ffn_w_gate, L0_ffn_w_noise,
           L0_ffn_W1, L0_ffn_b1, L0_ffn_W2, L0_ffn_b2):
    del L0_ffn_w_gate, L0_ffn_w_noise
    global _NC_CACHE
    if _NC_CACHE is None:
        _NC_CACHE = build_program()
    nc = _NC_CACHE
    x = np.ascontiguousarray(np.asarray(L0_attn_x, dtype=np.float32))
    common = {
        "wq": np.ascontiguousarray(np.asarray(L0_attn_Wq, np.float32)),
        "wk": np.ascontiguousarray(np.asarray(L0_attn_Wk, np.float32)),
        "wv": np.ascontiguousarray(np.asarray(L0_attn_Wv, np.float32)),
        "wo": np.ascontiguousarray(np.asarray(L0_attn_Wo, np.float32)),
        "bq": np.ascontiguousarray(np.asarray(L0_attn_bq, np.float32)),
        "bk": np.ascontiguousarray(np.asarray(L0_attn_bk, np.float32)),
        "bv": np.ascontiguousarray(np.asarray(L0_attn_bv, np.float32)),
        "bo": np.ascontiguousarray(np.asarray(L0_attn_bo, np.float32)),
        "w1": np.ascontiguousarray(np.asarray(L0_ffn_W1, np.float32)),
        "b1": np.ascontiguousarray(np.asarray(L0_ffn_b1, np.float32)),
        "w2": np.ascontiguousarray(np.asarray(L0_ffn_W2, np.float32)),
        "b2": np.ascontiguousarray(np.asarray(L0_ffn_b2, np.float32)),
    }
    in_maps = []
    for b in range(8):
        m = dict(common)
        m["x"] = np.ascontiguousarray(x[b])
        in_maps.append(m)
    res = run_bass_kernel_spmd(nc, in_maps, core_ids=list(range(8)))
    out = np.stack([res.results[b]["y"] for b in range(8)], axis=0)
    return out.astype(np.float32)


# version 37
# speedup vs baseline: 1.0181x; 1.0028x over previous
import os
import math
import contextlib
import numpy as np

import concourse.bass as bass
import concourse.mybir as mybir
from concourse.bass_utils import run_bass_kernel_spmd

F32 = mybir.dt.float32
BF16 = mybir.dt.bfloat16
AF = mybir.ActivationFunctionType
ALU = mybir.AluOpType
AX = mybir.AxisListType

S = 2048
DM = 1024
NH = 16
HD = 64
NST = 16
NDC = 8
NQT = 16
HID = 2048
NHC = 16
EPS = 1e-5

_tv = np.array([100.0, 0.0, 0.0, 0.0], np.float32)
_ex = np.exp(_tv - _tv.max(), dtype=np.float32)
_gk = (_ex / _ex.sum(dtype=np.float32)).astype(np.float32)
GATES = {1: float(_gk[0]), 0: float(_gk[1]), 2: float(_gk[2]), 3: float(_gk[3])}
EXPERTS = [0, 1, 2, 3]

ENGS = ("tensor", "scalar", "vector", "sync", "gpsimd")
KPHASE = int(os.environ.get("KPHASE", "99"))
KQT = int(os.environ.get("KQT", "16"))
KQT0 = int(os.environ.get("KQT0", "0"))
KNH = int(os.environ.get("KNH", "16"))
KSKIP = set(os.environ.get("KSKIP", "").split(","))


class Prog:

    def __init__(self):
        self.q = {e: [] for e in ENGS}
        self.cnt = {}
        self.lastwait = {}
        self.sems = None

    def addsem(self, *names):
        for n in names:
            self.cnt[n] = 0

    def op(self, eng, fn, inc=None):
        if inc is None:
            self.q[eng].append(fn)
            return None
        s, a = inc
        self.cnt[s] += a
        tgt = self.cnt[s]

        def run(e, f=fn, s=s, a=a, self_=self):
            ins = f(e)
            ins.then_inc(self_.sems[s], a)

        self.q[eng].append(run)
        return tgt

    def wait(self, eng, sem, val):
        if val is None or val <= 0:
            return
        assert val <= self.cnt[sem], f"wait target {val} > planned {self.cnt[sem]} for {sem}"
        if self.lastwait.get((eng, sem), 0) >= val:
            return
        self.lastwait[(eng, sem)] = val

        def run(e, s=sem, v=val, self_=self):
            e.wait_ge(self_.sems[s], v)

        self.q[eng].append(run)

    def replay(self, eng, e):
        for f in self.q[eng]:
            f(e)


def build_program():
    nc = bass.Bass("TRN2", target_bir_lowering=False, debug=False)

    x_d = nc.dram_tensor("x", [S, DM], F32, kind="ExternalInput").ap()
    wq_d = nc.dram_tensor("wq", [DM, DM], F32, kind="ExternalInput").ap()
    wk_d = nc.dram_tensor("wk", [DM, DM], F32, kind="ExternalInput").ap()
    wv_d = nc.dram_tensor("wv", [DM, DM], F32, kind="ExternalInput").ap()
    wo_d = nc.dram_tensor("wo", [DM, DM], F32, kind="ExternalInput").ap()
    bq_d = nc.dram_tensor("bq", [DM], F32, kind="ExternalInput").ap()
    bk_d = nc.dram_tensor("bk", [DM], F32, kind="ExternalInput").ap()
    bv_d = nc.dram_tensor("bv", [DM], F32, kind="ExternalInput").ap()
    bo_d = nc.dram_tensor("bo", [DM], F32, kind="ExternalInput").ap()
    w1_d = nc.dram_tensor("w1", [16, DM, HID], F32, kind="ExternalInput").ap()
    b1_d = nc.dram_tensor("b1", [16, HID], F32, kind="ExternalInput").ap()
    w2_d = nc.dram_tensor("w2", [16, HID, DM], F32, kind="ExternalInput").ap()
    b2_d = nc.dram_tensor("b2", [16, DM], F32, kind="ExternalInput").ap()
    y_d = nc.dram_tensor("y", [S, DM], F32, kind="ExternalOutput").ap()
    bsc_d = nc.dram_tensor("bsc", [DM], F32, kind="Internal").ap()

    prog = Prog()
    prog.addsem("lx0", "lx1", "lx2", "lx3", "lw", "lw1", "lw2", "lr", "lb", "st", "pev", "pe", "act", "dve", "pool")

    with contextlib.ExitStack() as stack:
        E = stack.enter_context

        SBUF_BASE = 16512
        cur = [0]

        def at(name, shape, dtype, off=None, adv=True):
            nbytes = (shape[1] * (4 if dtype == F32 else 2) + 31) // 32 * 32
            if off is None:
                off = cur[0]
                if adv:
                    cur[0] += nbytes
            h = nc.alloc_sbuf_tensor_at(name, shape, dtype, offset=SBUF_BASE + off)
            return h

        x_tm = at("x_tm", [128, NST * 1024], F32)
        off_xnv = cur[0]
        xnv = at("xnv", [128, 8 * 2048], BF16)
        off_qa = cur[0]
        qa = at("qa", [128, 8 * 2048], BF16)
        off_ka = cur[0]
        ka = at("ka", [128, 16 * 1024], BF16)
        off_c1 = cur[0]
        c1 = at("c1", [128, 8192], BF16)
        qnkn = at("qnkn", [16, 4096], F32, off=off_c1, adv=False)
        cur[0] = off_c1 + 16384
        off_c2 = cur[0]
        c2 = at("c2", [128, 8192], BF16)
        b2t = at("b2t", [128, 1024], F32, off=off_c2, adv=False)
        atok = at("atok", [128, 2048], BF16)
        attT = at("attT", [128, 1024], BF16, off=off_c1 + 12288, adv=False)
        idb = at("idb", [128, 128], BF16)
        idf = at("idf", [16, 16], F32)
        trimask = at("trimask", [128, 128], BF16)
        negut = at("negut", [128, 128], BF16)
        bias_sb = at("bias_sb", [128, 256], F32)
        lncols = at("lncols", [128, 112], F32)
        swc = at("swc", [128, 8], F32)
        bqcol = at("bqcol", [128, 8], F32)
        bkcol = at("bkcol", [128, 8], F32)
        b1col = at("b1col", [128, 32], F32)
        qkm = at("qkm", [128, 256], BF16)
        ones_r = at("ones_r", [1, 128], BF16)
        ones_c = at("ones_c", [128, 1], BF16)
        onesf16 = at("onesf16", [1, 16], F32)
        onesf_col = at("onesf_col", [1, 128], F32)
        bvrow = at("bvrow", [1, 1024], BF16)
        borow = at("borow", [1, 1024], BF16)
        b2row = at("b2row", [1, 1024], BF16)
        ksrow = at("ksrow", [1, 16], F32)
        kcol16 = at("kcol16", [16, 1], F32)
        kmax1 = at("kmax1", [1, 1], F32)
        kneg1 = at("kneg1", [1, 1], F32)
        kapcol = at("kapcol", [16, 1], F32)
        bstage = at("bstage", [16, 128], F32)
        b1stage = at("b1stage", [16, 256], F32)
        assert cur[0] <= 212830, f"sbuf over budget: {cur[0]}"

        P0 = E(nc.psum_tensor("P0", [128, 1536], F32))
        P1 = E(nc.psum_tensor("P1", [128, 1536], F32))
        PTa = E(nc.psum_tensor("PTa", [128, 1024], BF16))
        P2 = E(nc.psum_tensor("P2", [128, 512], F32))

        sems = {}
        for n in ("lx0", "lx1", "lx2", "lx3", "lw", "lw1", "lw2", "lr", "lb", "st",
                  "pev", "pe", "act", "dve", "pool"):
            sems[n] = E(nc.semaphore(n))
        prog.sems = sems

        plan(prog, locals())

        block = E(nc.Block())

        @block.tensor
        def _(e):
            prog.replay("tensor", e)

        @block.scalar
        def _(e):
            prog.replay("scalar", e)

        @block.vector
        def _(e):
            prog.replay("vector", e)

        @block.sync
        def _(e):
            prog.replay("sync", e)

        @block.gpsimd
        def _(e):
            prog.replay("gpsimd", e)

    return nc


def plan(prog, ns):
    P = prog.op
    W = prog.wait
    x_d, y_d, bsc_d = ns["x_d"], ns["y_d"], ns["bsc_d"]
    wq_d, wk_d, wv_d, wo_d = ns["wq_d"], ns["wk_d"], ns["wv_d"], ns["wo_d"]
    bq_d, bk_d, bv_d, bo_d = ns["bq_d"], ns["bk_d"], ns["bv_d"], ns["bo_d"]
    w1_d, b1_d, w2_d, b2_d = ns["w1_d"], ns["b1_d"], ns["w2_d"], ns["b2_d"]
    x_tm, xnv, qa, ka, c1, qnkn, c2 = (
        ns["x_tm"], ns["xnv"], ns["qa"], ns["ka"], ns["c1"], ns["qnkn"], ns["c2"])
    atok, attT, idb, idf, trimask = ns["atok"], ns["attT"], ns["idb"], ns["idf"], ns["trimask"]
    negut = ns["negut"]
    b2t = ns["b2t"]
    bias_sb, lncols, swc = ns["bias_sb"], ns["lncols"], ns["swc"]
    bqcol, bkcol, b1col, qkm = ns["bqcol"], ns["bkcol"], ns["b1col"], ns["qkm"]
    ones_r, onesf16, ones_c = ns["ones_r"], ns["onesf16"], ns["ones_c"]
    onesf_col = ns["onesf_col"]
    bvrow, borow, b2row = ns["bvrow"], ns["borow"], ns["b2row"]
    ksrow, kcol16, kmax1, kneg1, kapcol = (
        ns["ksrow"], ns["kcol16"], ns["kmax1"], ns["kneg1"], ns["kapcol"])
    bstage, b1stage = ns["bstage"], ns["b1stage"]
    P0, P1, PTa, P2 = ns["P0"], ns["P1"], ns["PTa"], ns["P2"]

    def mm(out, lhsT, rhs, start, stop, inc=None):
        return P("tensor",
                 lambda e, o=out, l=lhsT, r=rhs, s=start, t=stop:
                 e.matmul(o, l, r, start=s, stop=t), inc)

    def tr(out, in_, ident, inc=None):
        return P("tensor",
                 lambda e, o=out, i=in_, d=ident: e.transpose(out=o, in_=i, identity=d), inc)

    def act(out, in_, func, bias=0.0, scale=1.0, accum=None, inc=None):
        return P("scalar",
                 lambda e, o=out, i=in_, f=func, b=bias, s=scale, a=accum:
                 e.activation(out=o, in_=i, func=f, bias=b, scale=s, accum_out=a), inc)

    def dcopy(out, in_, inc=None):
        return P("vector", lambda e, o=out, i=in_: e.tensor_copy(out=o, in_=i), inc)

    def acopy(out, in_, inc=None):
        return P("scalar", lambda e, o=out, i=in_: e.copy(out, in_) if False else e.copy(o, i), inc)

    def dtt(out, in0, in1, aop, inc=None):
        return P("vector",
                 lambda e, o=out, a=in0, b=in1, p=aop: e.tensor_tensor(out=o, in0=a, in1=b, op=p),
                 inc)

    def dts(out, in0, scalar, aop, inc=None, accum=None):
        return P("vector",
                 lambda e, o=out, a=in0, s=scalar, p=aop, ac=accum:
                 e.tensor_scalar(out=o, in0=a, scalar1=s, scalar2=None, op0=p, accum_out=ac), inc)

    def dstt(out, in0, scalar, in1, op0, op1, accum=None, inc=None):
        return P("vector",
                 lambda e, o=out, a=in0, s=scalar, b=in1, p0=op0, p1=op1, ac=accum:
                 e.scalar_tensor_tensor(out=o, in0=a, scalar=s, in1=b, op0=p0, op1=p1,
                                        accum_out=ac), inc)

    def dred(out, in_, aop, inc=None):
        return P("vector",
                 lambda e, o=out, i=in_, p=aop: e.tensor_reduce(out=o, in_=i, axis=AX.X, op=p),
                 inc)

    def drecip(out, in_, inc=None):
        return P("vector", lambda e, o=out, i=in_: e.reciprocal(out=o, in_=i), inc)

    def dma(eng, out, in_, inc):
        return P(eng, lambda e, o=out, i=in_: e.dma_start(out=o, in_=i), inc)

    def gmemset(apx, v, inc=None):
        return P("gpsimd", lambda e, a=apx, vv=v: e.memset(a, vv), inc)

    ptring = {"idx": 0, "free": [("dve", 0)], "bank": None}

    def pt_group(srcs, mk_evac, waits=(), evac="vector"):
        b = ptring["idx"]
        ptring["idx"] = (b + 1) % len(ptring["bank"])
        PTx = ptring["bank"][b]
        fs, fv = ptring["free"][b]
        W("tensor", fs, fv)
        for sem, val in waits:
            W("tensor", sem, val)
        t_last = 0
        for i, s in enumerate(srcs):
            t_last = tr(PTx[:, i * 128:(i + 1) * 128], s, idb[:],
                        inc=("pe", 1) if i == len(srcs) - 1 else None)
        out_ap, in_ap = mk_evac(PTx, len(srcs))
        if evac == "act":
            W("scalar", "pe", t_last)
            ev = ("act", acopy(out_ap, in_ap, inc=("act", 1)))
        else:
            W("vector", "pe", t_last)
            ev = ("dve", dcopy(out_ap, in_ap, inc=("dve", 1)))
        ptring["free"][b] = ev
        return t_last, ev

    ptring["bank"] = [PTa]

    m = gmemset(idb[:], 0.0, inc=("pool", 1))
    W("gpsimd", "pool", m)
    P("gpsimd", lambda e: e.affine_select(out=idb[:], in_=idb[:], compare_op=ALU.not_equal,
                                          fill=1.0, base=0, pattern=[[-1, 128]],
                                          channel_multiplier=1))
    m = gmemset(idf[:], 0.0, inc=("pool", 1))
    W("gpsimd", "pool", m)
    P("gpsimd", lambda e: e.affine_select(out=idf[:], in_=idf[:], compare_op=ALU.not_equal,
                                          fill=1.0, base=0, pattern=[[-1, 16]],
                                          channel_multiplier=1))
    m = gmemset(trimask[:], 1.0, inc=("pool", 1))
    W("gpsimd", "pool", m)
    P("gpsimd", lambda e: e.affine_select(out=trimask[:], in_=trimask[:], compare_op=ALU.is_ge,
                                          fill=0.0, base=0, pattern=[[1, 128]],
                                          channel_multiplier=-1))
    m = gmemset(negut[:], -480.0, inc=("pool", 1))
    W("gpsimd", "pool", m)
    P("gpsimd", lambda e: e.affine_select(out=negut[:], in_=negut[:], compare_op=ALU.is_ge,
                                          fill=0.0, base=-1, pattern=[[1, 128]],
                                          channel_multiplier=-1))
    gmemset(ones_c[:], 1.0)
    gmemset(onesf_col[:], 1.0)
    gmemset(ones_r[:], 1.0)
    gmemset(onesf16[:], 1.0)
    m = gmemset(qkm[:], 0.0, inc=("pool", 1))
    W("gpsimd", "pool", m)
    for g in range(8):
        gmemset(qkm[0:64, g * 16 + 2 * g: g * 16 + 2 * g + 1], 0.0625)
        gmemset(qkm[64:128, g * 16 + 2 * g + 1: g * 16 + 2 * g + 2], 0.0625)
        gmemset(qkm[0:64, 128 + g * 16 + 2 * g: 128 + g * 16 + 2 * g + 1], 0.0625)
        last = gmemset(qkm[64:128, 128 + g * 16 + 2 * g + 1: 128 + g * 16 + 2 * g + 2],
                       0.0625, inc=("pool", 1) if g == 7 else None)
    consts_pool = prog.cnt["pool"]

    dma("gpsimd", bvrow[:], bv_d[None, :], ("lr", 16))
    dma("gpsimd", borow[:], bo_d[None, :], ("lr", 16))
    borow_lr = prog.cnt["lr"]

    x_r = x_d.rearrange("(t p) d -> p t d", p=128)
    xload = []
    for t in range(4):
        dma("sync", x_tm[:, t * 4096:(t + 1) * 4096].rearrange("p (a d) -> p a d", d=1024),
            x_r[:, 4 * t:4 * t + 4, :], (f"lx{t}", 16))
        xload.append((f"lx{t}", prog.cnt[f"lx{t}"]))
    dma("sync", bstage[0:8, :], bq_d.rearrange("(t p) -> t p", p=128), ("lb", 16))
    bq_lb = prog.cnt["lb"]

    def wload(wd):
        dma("gpsimd", c1[:, :].rearrange("p (ot d) -> p ot d", d=1024),
            wd.rearrange("(ot p) d -> p ot d", p=128), ("lw", 16))
        return prog.cnt["lw"]

    wq_lw = wload(wq_d)

    SC, SQ, NM, MSQ, VAR, RSTD, LNV = 0, 16, 32, 48, 64, 80, 96

    def ln_stats(st, entry_pe, entry_dve, prev_stt, wait_lx, sbuf_discard=False):
        if wait_lx:
            W("vector", xload[st // 4][0], xload[st // 4][1])
        W("vector", "pe", entry_pe)
        W("vector", "dve", entry_dve)
        dred(lncols[:, SC + st:SC + st + 1], x_tm[:, st * 1024:(st + 1) * 1024], ALU.add)
        W("vector", "dve", prev_stt)
        out_ap = c1[:, 0:1024] if sbuf_discard else P0[:, 0:1024]
        return dstt(out_ap, x_tm[:, st * 1024:(st + 1) * 1024], 1.0,
                    x_tm[:, st * 1024:(st + 1) * 1024], ALU.mult, ALU.mult,
                    accum=lncols[:, SQ + st:SQ + st + 1], inc=("dve", 1))

    def layer_norm(entry_pe, entry_dve, wait_lx, entry_act=0, stats_done=None):
        grouped = stats_done is None
        evac_dve = []
        prev_ts = 0
        prev_evac = 0
        prev_tr = 0
        tr_half = {0: 0, 1: 0}
        prev_stt = 0
        ngrp = 4 if grouped else 1
        gw = 16 // ngrp
        for grp in range(ngrp):
            c0 = grp * gw
            if grouped:
                for st in range(c0, c0 + gw):
                    prev_stt = ln_stats(st, entry_pe, entry_dve, prev_stt, wait_lx)
                sd = prev_stt
            else:
                sd = stats_done
            W("vector", "dve", sd)
            m1 = dts(lncols[:, NM + c0:NM + c0 + gw], lncols[:, SC + c0:SC + c0 + gw],
                     -1.0 / 1024.0, ALU.mult, inc=("dve", 1))
            W("vector", "dve", m1)
            m2 = dstt(lncols[:, MSQ + c0:MSQ + c0 + gw], lncols[:, NM + c0:NM + c0 + gw],
                      1.0, lncols[:, NM + c0:NM + c0 + gw], ALU.mult, ALU.mult,
                      inc=("dve", 1))
            m3 = dts(lncols[:, SQ + c0:SQ + c0 + gw], lncols[:, SQ + c0:SQ + c0 + gw],
                     1.0 / 1024.0, ALU.mult, inc=("dve", 1))
            W("vector", "dve", m3)
            m4 = dstt(lncols[:, VAR + c0:VAR + c0 + gw], lncols[:, SQ + c0:SQ + c0 + gw],
                      EPS, lncols[:, MSQ + c0:MSQ + c0 + gw], ALU.add, ALU.subtract,
                      inc=("dve", 1))
            W("scalar", "dve", m4)
            a1 = act(lncols[:, LNV + c0:LNV + c0 + gw], lncols[:, VAR + c0:VAR + c0 + gw],
                     AF.Ln, bias=0.0, scale=1.0, inc=("act", 1))
            W("scalar", "act", a1)
            a2 = act(lncols[:, RSTD + c0:RSTD + c0 + gw], lncols[:, LNV + c0:LNV + c0 + gw],
                     AF.Exp, bias=0.0, scale=-0.5, inc=("act", 1))
            W("vector", "act", a2)
            m5 = dtt(lncols[:, MSQ + c0:MSQ + c0 + gw], lncols[:, NM + c0:NM + c0 + gw],
                     lncols[:, RSTD + c0:RSTD + c0 + gw], ALU.mult, inc=("dve", 1))
            for st in range(c0, c0 + gw):
                half = st % 2
                xn_tmp = atok[:, half * 1024:(half + 1) * 1024]
                W("scalar", "dve", m5)
                W("scalar", "pe", tr_half[half])
                prev_ts = act(xn_tmp[:, 0:1024], x_tm[:, st * 1024:(st + 1) * 1024],
                              AF.Identity, bias=lncols[:, MSQ + st:MSQ + st + 1],
                              scale=lncols[:, RSTD + st:RSTD + st + 1], inc=("act", 1))
                waits = [("act", prev_ts), ("pool", consts_pool)]
                if entry_act:
                    waits.append(("act", entry_act))
                prev_tr, ev = pt_group(
                    [xn_tmp[:, dc * 128:(dc + 1) * 128] for dc in range(NDC)],
                    lambda PTx, n, st=st: (
                        xnv[:, :].rearrange("p (dc c) -> p dc c", c=2048)[
                            :, :, st * 128:(st + 1) * 128],
                        PTx[:, :].rearrange("p (dc c) -> p dc c", c=128)),
                    waits=waits)
                tr_half[half] = prev_tr
                prev_evac = ev[1]
                evac_dve.append(prev_evac)
        return evac_dve, prev_tr

    def early_store(dve_gate):
        for st in range(NST):
            W("sync", "dve", dve_gate)
            dma("sync", y_d.rearrange("(t p) d -> p t d", p=128)[:, st:st + 1, :],
                x_tm[:, st * 1024:(st + 1) * 1024].rearrange(
                    "p (a d) -> p a d", a=1), ("st", 16))
        W("sync", "st", prog.cnt["st"])

    ln1_evac, ln1_tr = layer_norm(0, 0, wait_lx=True)
    if KPHASE == 1:
        early_store(prog.cnt["dve"])
        return

    def build_wt(load_lw):
        tr_last = 0
        ev = 0
        for dc in range(NDC):
            tr_last, ev = pt_group(
                [c1[:, ot * 1024 + dc * 128: ot * 1024 + (dc + 1) * 128]
                 for ot in range(8)],
                lambda PTx, n, dc=dc: (c2[:, dc * 1024:(dc + 1) * 1024], PTx[:, :]),
                waits=[("lw", load_lw)])
        return tr_last, ev[1]

    W("tensor", "lb", bq_lb)
    tbq = tr(P1[:, 1168:1176], bstage[0:8, :], idf[0:8, 0:8], inc=("pe", 1))
    W("vector", "pe", tbq)
    ebq = dcopy(bqcol[:, 0:8], P1[:, 1168:1176], inc=("dve", 1))
    W("sync", "pe", tbq)
    dma("sync", bstage[0:8, :], bk_d.rearrange("(t p) -> t p", p=128), ("lb", 16))
    bk_lb = prog.cnt["lb"]

    wq_tr, wq_evac = build_wt(wq_lw)
    W("tensor", "lb", bk_lb)
    W("tensor", "dve", ebq)
    tbk = tr(P1[:, 1168:1176], bstage[0:8, :], idf[0:8, 0:8], inc=("pe", 1))
    W("vector", "pe", tbk)
    ebk = dcopy(bkcol[:, 0:8], P1[:, 1168:1176], inc=("dve", 1))

    W("sync", "pe", tbk)
    comb = 0
    first = True
    for e in EXPERTS:
        slot = b1stage[0:8, 0:128] if first else b1stage[0:8, 128:256]
        dma("sync", slot, b2_d[e, :].rearrange("(t p) -> t p", p=128), ("lb", 16))
        l_ = prog.cnt["lb"]
        W("vector", "lb", l_)
        W("vector", "dve", comb)
        m_ = dts(slot, slot, GATES[e], ALU.mult, inc=("dve", 1))
        if first:
            comb = m_
            first = False
        else:
            W("vector", "dve", m_)
            comb = dtt(b1stage[0:8, 0:128], b1stage[0:8, 0:128], b1stage[0:8, 128:256],
                       ALU.add, inc=("dve", 1))
        if e != EXPERTS[-1]:
            W("sync", "dve", comb)
    W("sync", "dve", comb)
    dma("sync", bsc_d.rearrange("(t p) -> t p", p=128), b1stage[0:8, 0:128], ("lb", 16))
    bounce_lb = prog.cnt["lb"]

    W("gpsimd", "pe", wq_tr)
    wk_lw = wload(wk_d)

    def proj_fm(dest, dest_off, bcol, ln_evacs, w_evac, entry_act=0):
        prev_ev = {0: entry_act, 1: entry_act}
        last_mm = 0
        last_ev = 0
        for sc in range(4):
            for ot in range(8):
                reg = (sc * 8 + ot) % 2
                W("tensor", "dve", w_evac)
                W("tensor", "dve", ln_evacs[sc * 4 + 3])
                W("tensor", "act", prev_ev[reg])
                for dc in range(NDC):
                    last_mm = mm(P1[:, reg * 512:(reg + 1) * 512],
                                 c2[:, dc * 1024 + ot * 128: dc * 1024 + (ot + 1) * 128],
                                 xnv[:, dc * 2048 + sc * 512: dc * 2048 + (sc + 1) * 512],
                                 start=(dc == 0), stop=(dc == NDC - 1),
                                 inc=("pe", 1) if dc == NDC - 1 else None)
                W("scalar", "pe", last_mm)
                prev_ev[reg] = act(dest[:, ot * 2048 + sc * 512: ot * 2048 + (sc + 1) * 512],
                                   P1[:, reg * 512:(reg + 1) * 512], AF.Identity,
                                   bias=bcol[:, ot:ot + 1], scale=1.0, inc=("act", 1))
                last_ev = prev_ev[reg]
        return last_mm, last_ev

    W("scalar", "dve", ebq)
    q_mm, q_ev = proj_fm(qa, 0, bqcol, ln1_evac, wq_evac)

    wk_tr, wk_evac = build_wt(wk_lw)
    W("gpsimd", "pe", wk_tr)
    wv_lw = wload(wv_d)

    W("scalar", "dve", ebk)
    k_mm, k_ev = proj_fm(ka, 0, bkcol, ln1_evac, wk_evac, entry_act=q_ev)

    wv_tr, wv_evac = build_wt(wv_lw)
    W("gpsimd", "pe", wv_tr)
    wo_lw = wload(wo_d)

    vregions = [(P1, 0), (P1, 512), (P0, 0), (P0, 512)]
    v_prev_ev = {0: 0, 1: 0, 2: 0, 3: 0}
    v_last_ev = 0
    v_last_mm = 0
    for st in range(NST):
        grp = []
        for half in range(2):
            r = (st * 2 + half) % 4
            pbuf, poff = vregions[r]
            W("tensor", "dve", wv_evac)
            W("tensor", "dve", v_prev_ev[r])
            W("tensor", "lr", borow_lr)
            if st == 0 and half == 0:
                W("tensor", "pe", k_mm)
                W("tensor", "act", k_ev)
            for dc in range(NDC):
                mm(pbuf[:, poff:poff + 512],
                   xnv[:, dc * 2048 + st * 128: dc * 2048 + (st + 1) * 128],
                   c2[:, dc * 1024 + half * 512: dc * 1024 + half * 512 + 512],
                   start=(dc == 0), stop=False)
            v_last_mm = mm(pbuf[:, poff:poff + 512], ones_r[:],
                           bvrow[:, half * 512:half * 512 + 512],
                           start=False, stop=True, inc=("pe", 1))
            grp.append((r, pbuf, poff))
        for half in range(2):
            r, pbuf, poff = grp[half]
            W("vector", "pe", v_last_mm)
            dest = xnv[:, :].rearrange("p (dc c) -> p dc c", c=2048)[
                :, 4 * half:4 * half + 4, st * 128:(st + 1) * 128]
            src = pbuf[:, poff:poff + 512].rearrange("p (dc c) -> p dc c", c=128)
            v_prev_ev[r] = dcopy(dest, src, inc=("dve", 1))
            v_last_ev = v_prev_ev[r]

    wo_tr, wo_evac = build_wt(wo_lw)
    if KPHASE == 2:
        early_store(prog.cnt["dve"])
        return

    scr_free = {0: ln1_tr, 1: wo_tr}

    def scr_ap(b, lo, hi):
        return atok[:, lo:hi] if b == 0 else c1[:, 4096 + lo:4096 + hi]

    def norms_prep(src_t, mask_coloff, psum_dve_entry):
        last_mm = 0
        for g in range(8):
            b = g % 2
            halves = []
            for hf in range(2):
                W("vector", "pe", scr_free[b])
                W("vector", "dve", v_last_ev)
                lo = g * 2048 + hf * 1024
                halves.append(dtt(scr_ap(b, hf * 1024, hf * 1024 + 1024),
                                  src_t[:, lo:lo + 1024], src_t[:, lo:lo + 1024],
                                  ALU.mult, inc=("dve", 1)))
            if g == 0:
                W("tensor", "dve", psum_dve_entry)
            for c4 in range(4):
                W("tensor", "dve", halves[c4 // 2])
                out_ap = (P0[0:16, c4 * 512:(c4 + 1) * 512] if c4 < 3
                          else P1[0:16, 0:512])
                last_mm = mm(out_ap,
                             qkm[:, mask_coloff + g * 16: mask_coloff + (g + 1) * 16],
                             scr_ap(b, c4 * 512, (c4 + 1) * 512),
                             start=(g == 0), stop=(g == 7),
                             inc=("pe", 1) if c4 == 3 else None)
            scr_free[b] = last_mm
        return last_mm

    qn_mm = norms_prep(qa, 0, v_last_ev)
    W("vector", "pe", qn_mm)
    dcopy(qnkn[0:16, 0:1536], P0[0:16, 0:1536], inc=("dve", 1))
    qn_ev = dcopy(qnkn[0:16, 1536:2048], P1[0:16, 0:512], inc=("dve", 1))
    kn_mm = norms_prep(ka, 128, qn_ev)
    W("vector", "pe", kn_mm)
    kn_p0 = dcopy(qnkn[0:16, 2048:2048 + 1536], P0[0:16, 0:1536], inc=("dve", 1))
    kn_ev = dcopy(qnkn[0:16, 2048 + 1536:4096], P1[0:16, 0:512], inc=("dve", 1))
    W("vector", "dve", kn_ev)
    r1 = dred(kcol16[0:16, 0:1], qnkn[0:16, 0:2048], ALU.max, inc=("dve", 1))
    W("vector", "dve", r1)
    r2 = dred(kapcol[0:16, 0:1], qnkn[0:16, 2048:4096], ALU.max, inc=("dve", 1))
    W("vector", "dve", r2)
    r3 = dtt(kcol16[0:16, 0:1], kcol16[0:16, 0:1], kapcol[0:16, 0:1], ALU.add,
             inc=("dve", 1))
    W("vector", "dve", r3)
    ng = dts(kcol16[0:16, 0:1], kcol16[0:16, 0:1], -1.0, ALU.mult, inc=("dve", 1))
    W("tensor", "dve", ng)
    t1 = tr(P1[0:1, 1168:1184], kcol16[0:16, 0:1], idf[0:16, 0:16], inc=("pe", 1))
    W("vector", "pe", t1)
    c1m = dcopy(ksrow[:, :], P1[0:1, 1168:1184], inc=("dve", 1))
    W("tensor", "dve", c1m)
    bc = mm(P1[:, 1280:1296], onesf_col[:, :], ksrow[:, :],
            start=True, stop=True, inc=("pe", 1))
    W("vector", "pe", bc)
    bias_ev = dcopy(bias_sb[:, 0:16], P1[:, 1280:1296], inc=("dve", 1))

    p_sb = [c1[:, 0:2048], c1[:, 2048:4096]]
    pT_sb = [c1[:, 4096:6144], c1[:, 6144:8192]]
    att_slot = [0, 64]
    CHUNK = 1536

    exp_done = [(0, 0, 0, 0)] * (NH + 2)
    pT_free = {0: 0, 1: 0}
    p_dve_free = {0: 0, 1: 0}
    attslot_free = {0: ("dve", 0), 1: ("dve", kn_p0)}
    ae_last = 0
    exp_bank = {0: ("dve", kn_p0), 1: ("dve", kn_p0), 2: ("dve", kn_p0)}
    atok_free = {0: kn_mm, 1: kn_mm}
    wo_done_prev = 0
    resid_done_prev = 0
    resid_of = {}
    ln2stt = {}
    ln2stt = {}
    last_exp = 0
    last_exp2 = 0
    qk_last = 0
    qk_of_h = {}

    def stage_A1(qt, h):
        nonlocal qk_last
        L = (qt + 1) * 128
        L1 = min(L, CHUNK)
        two = L > CHUNK
        dbuf = L <= 512
        soff = (h % 2) * 1024 if dbuf else 0
        par = h % 2
        g = h // 2
        hp = (h % 2) * 64
        bias_ap = bias_sb[:, h:h + 1]
        rhsq = qa[hp:hp + 64, g * 2048 + qt * 128: g * 2048 + (qt + 1) * 128]

        def st_tiles(lo, hi, off, war):
            nonlocal qk_last
            W("tensor", war[0], war[1])
            for jc in range(lo // 128, hi // 128):
                qk_last = mm(P0[:, off + (jc * 128 - lo): off + (jc * 128 - lo) + 128],
                             ka[hp:hp + 64, g * 2048 + jc * 128: g * 2048 + (jc + 1) * 128],
                             rhsq, start=True, stop=True,
                             inc=("pe", 1) if (jc + 1) * 128 >= hi else None)
            return qk_last

        def exp_chunk(lo, hi, off):
            W("scalar", "pe", qk_last)
            W("scalar", "dve", bias_ev)
            W("scalar", "dve", p_dve_free[par])
            return act(pT_sb[par][:, lo:hi], P0[:, off:off + (hi - lo)], AF.Exp,
                       bias=bias_ap, scale=0.125, inc=("act", 1))

        W("tensor", "act", q_ev)
        W("tensor", "act", k_ev)
        if dbuf:
            bk = par * 2
            war = exp_bank[bk]
            W("tensor", war[0], war[1])
            for jc in range(qt):
                mm(P0[:, soff + jc * 128: soff + (jc + 1) * 128],
                   ka[hp:hp + 64, g * 2048 + jc * 128: g * 2048 + (jc + 1) * 128],
                   rhsq, start=True, stop=True)
            dreg = P0[:, soff + qt * 128: soff + (qt + 1) * 128]
            mm(dreg, ka[hp:hp + 64, g * 2048 + qt * 128: g * 2048 + (qt + 1) * 128],
               rhsq, start=True, stop=False)
            qk_last = mm(dreg, negut[:, :], idb[:, :], start=False, stop=True,
                         inc=("pe", 1))
            qk_of_h[(qt, h)] = qk_last
            e1 = exp_chunk(0, L, soff)
            exp_bank[bk] = ("act", e1)
            exp_done[h] = (e1, e1, e1, 0)
            return None
        eA = 0
        e = 0
        for ci in range((L1 + 511) // 512):
            lo = ci * 512
            hi = min(lo + 512, L1)
            st_tiles(lo, hi, lo, exp_bank[ci])
            qk_of_h[(qt, h)] = qk_last
            e = exp_chunk(lo, hi, lo)
            exp_bank[ci] = ("act", e)
            if ci == 0:
                eA = e
        if not two:
            exp_done[h] = (e, e, eA, 0)
            return None
        return (rhsq, bias_ap, eA, e)

    def stage_A2(qt, h, carry):
        nonlocal qk_last
        if carry is None:
            return
        rhsq, bias_ap, eA, eB = carry
        L = (qt + 1) * 128
        par = h % 2
        g = h // 2
        hp = (h % 2) * 64
        L2 = L - CHUNK
        W("tensor", exp_bank[0][0], exp_bank[0][1])
        qk2 = 0
        for jc in range(L2 // 128):
            qk2 = mm(P0[:, jc * 128:(jc + 1) * 128],
                     ka[hp:hp + 64, g * 2048 + CHUNK + jc * 128:
                        g * 2048 + CHUNK + (jc + 1) * 128],
                     rhsq, start=True, stop=True,
                     inc=("pe", 1) if (jc + 1) * 128 >= L2 else None)
        qk_last = qk2
        qk_of_h[(qt, h)] = qk_last
        W("scalar", "pe", qk2)
        W("scalar", "dve", p_dve_free[par])
        exm2 = act(pT_sb[par][:, CHUNK:L], P0[:, 0:L2], AF.Exp,
                   bias=bias_ap, scale=0.125, inc=("act", 1))
        exp_bank[0] = ("act", exm2)
        exp_done[h] = (exm2, exm2, eA, exm2)

    bstate = {}

    def stage_B1(qt, h):
        if (qt + 1) * 128 <= 512:
            bstate[h] = 0
            return
        par = h % 2
        ex_last = exp_done[h][0]
        dcol = qt * 128
        pT_diag = pT_sb[par][:, dcol:dcol + 128]
        W("vector", "act", ex_last)
        W("vector", "pool", consts_pool)
        pm = dtt(pT_diag, pT_diag, trimask[:], ALU.mult, inc=("dve", 1))
        p_dve_free[par] = pm
        bstate[h] = pm

    def stage_B2(qt, h):
        nonlocal ae_last
        L = qt + 1
        par = h % 2
        small = L * 128 <= 512
        ex_last, _, exm1, exm2 = exp_done[h]
        pm = bstate.pop(h)
        W("tensor", "act", ex_last)
        W("tensor", "dve", pm)
        if small:
            W("tensor", "dve", attslot_free[par][1])
        else:
            W("tensor", "dve", ae_last)
        pv_last = 0

        def vfrag(jc):
            o = (h // 2) * 2048 + jc * 128 + (h % 2) * 64
            return xnv[:, o:o + 64]

        if small and par == 1:
            for jc in range(L):
                mm(P0[:, 512:576], pT_sb[1][:, jc * 128:(jc + 1) * 128], vfrag(jc),
                   start=(jc == 0), stop=(jc == L - 1))
            for jc in range(L):
                pv_last = mm(P0[:, 576:577], pT_sb[1][:, jc * 128:(jc + 1) * 128],
                             ones_c[:, :], start=(jc == 0), stop=(jc == L - 1),
                             inc=("pe", 1) if jc == L - 1 else None)
            sums_ap = P0[:, 576:577]
            pv_ap = P0[:, 512:576]
        else:
            a0 = att_slot[par]
            sc0 = 1024 + par
            for jc in range(L):
                mm(P2[:, a0:a0 + 64],
                   pT_sb[par][:, jc * 128:(jc + 1) * 128], vfrag(jc),
                   start=(jc == 0), stop=(jc == L - 1))
                pv_last = mm(P1[:, sc0:sc0 + 1],
                             pT_sb[par][:, jc * 128:(jc + 1) * 128], ones_c[:, :],
                             start=(jc == 0), stop=(jc == L - 1),
                             inc=("pe", 1) if jc == L - 1 else None)
            sums_ap = P1[:, sc0:sc0 + 1]
            pv_ap = P2[:, a0:a0 + 64]
        pT_free[par] = pv_last
        W("vector", "pe", pv_last)
        W("vector", "dve", attslot_free[par][1])
        rc = drecip(swc[:, 2 + par:3 + par], sums_ap, inc=("dve", 1))
        W("vector", "dve", rc)
        if h == 0:
            W("vector", "pe", atok_free[qt % 2])
        ae = dts(atok[:, (qt % 2) * 1024 + h * 64: (qt % 2) * 1024 + (h + 1) * 64],
                 pv_ap, swc[:, 2 + par:3 + par], ALU.mult, inc=("dve", 1))
        attslot_free[par] = ("dve", ae)
        ae_last = ae
        return ae

    def emit_tail(qt, last_ae):
        nonlocal wo_done_prev, resid_done_prev
        at_last, atev = pt_group(
            [atok[:, (qt % 2) * 1024 + dc * 128:(qt % 2) * 1024 + (dc + 1) * 128]
             for dc in range(NDC)],
            lambda PTx, n: (attT[:, :], PTx[:, :]),
            waits=[("dve", last_ae), ("pe", wo_done_prev)], evac="auto")
        atok_free[qt % 2] = at_last
        wo_last = 0
        W("tensor", atev[0], atev[1])
        W("tensor", "dve", wo_evac)
        W("tensor", "dve", resid_done_prev)
        W("tensor", "lr", borow_lr)
        for half in range(2):
            for dc in range(NDC):
                mm(P1[:, half * 512:(half + 1) * 512],
                   attT[:, dc * 128:(dc + 1) * 128],
                   c2[:, dc * 1024 + half * 512: dc * 1024 + half * 512 + 512],
                   start=(dc == 0), stop=False)
            wo_last = mm(P1[:, half * 512:(half + 1) * 512], ones_r[:],
                         borow[:, half * 512:half * 512 + 512], start=False, stop=True,
                         inc=("pe", 1))
        wo_done_prev = wo_last
        W("vector", "pe", wo_last)
        rs = 0
        for half in range(2):
            rs = dtt(x_tm[:, qt * 1024 + half * 512: qt * 1024 + half * 512 + 512],
                     x_tm[:, qt * 1024 + half * 512: qt * 1024 + half * 512 + 512],
                     P1[:, half * 512:(half + 1) * 512], ALU.add, inc=("dve", 1))
        resid_done_prev = rs
        resid_of[qt] = rs

    steps = [(qt, h) for qt in range(KQT0, min(NQT, KQT)) for h in range(KNH)]
    last_ae_of = {}
    b1_emitted = set()
    stt_tile = 0
    stt_phase = 0

    def do_B1(s):
        if s not in b1_emitted:
            b1_emitted.add(s)
            stage_B1(*s)

    def do_B2(s):
        if s not in last_ae_of:
            last_ae_of[s] = stage_B2(*s)
            if s[1] == KNH - 1:
                emit_tail(s[0], last_ae_of[s])

    for idx, (qt, h) in enumerate(steps):
        if KNH == NH and stt_tile <= 14 and stt_tile in resid_of and \
                ((qt >= 5 and h in (5, 11)) or qt == NQT - 1):
            T = stt_tile
            if stt_phase == 0:
                W("vector", "dve", resid_of[T])
                dred(lncols[:, SC + T:SC + T + 1],
                     x_tm[:, T * 1024:(T + 1) * 1024], ALU.add)
                stt_phase = 1
            else:
                W("vector", "dve", resid_of[T])
                W("vector", "dve", ln2stt.get(T - 1, 0))
                ln2stt[T] = dstt(c1[:, 0:1024], x_tm[:, T * 1024:(T + 1) * 1024],
                                 1.0, x_tm[:, T * 1024:(T + 1) * 1024],
                                 ALU.mult, ALU.mult,
                                 accum=lncols[:, SQ + T:SQ + T + 1], inc=("dve", 1))
                stt_phase = 0
                stt_tile += 1
        if idx >= 2:
            do_B2(steps[idx - 2])
        if (qt, h) == (4, 0) and KNH == NH and KQT0 == 0:
            do_B1((3, 15))
            do_B2((3, 15))
            exp_bank[1] = ("dve", last_ae_of[(3, 15)])
        carry = stage_A1(qt, h)
        if idx >= 1:
            do_B1(steps[idx - 1])
        stage_A2(qt, h, carry)
    n = len(steps)
    if n >= 2:
        do_B2(steps[n - 2])
    do_B1(steps[n - 1])
    do_B2(steps[n - 1])

    attn_pe_done = wo_done_prev
    attn_dve_done = resid_done_prev
    if KPHASE == 4:
        early_store(prog.cnt["dve"])
        return

    if 14 in ln2stt:
        ln2stt[15] = ln_stats(15, 0, resid_of[15], ln2stt[14], wait_lx=False,
                              sbuf_discard=True)
    ln2_evac, ln2_tr = layer_norm(attn_pe_done, attn_dve_done, wait_lx=False,
                                  entry_act=prog.cnt["act"],
                                  stats_done=ln2stt.get(15, None))

    if KPHASE == 5:
        early_store(prog.cnt["dve"])
        return

    h_phase_done = {0: qk_last}
    y_phase_done = {}
    w1_loaded = {}
    w2_loaded = {}
    b1_done = {}
    prev_t_b1 = tbk
    W("gpsimd", "pe", attn_pe_done)
    W("gpsimd", "lb", bounce_lb)
    dma("gpsimd", b2t[:, :], bsc_d[None, :].broadcast_to((128, DM)), ("lr", 16))
    b2t_lr = prog.cnt["lr"]

    def emit_b1(ei, e):
        nonlocal prev_t_b1
        bpar = ei % 2
        W("sync", "pe", prev_t_b1)
        W("sync", "lb", bounce_lb)
        dma("sync", b1stage[0:16, bpar * 128:(bpar + 1) * 128],
            b1_d[e, :].rearrange("(t p) -> t p", p=128), ("lb", 16))
        b1_lb = prog.cnt["lb"]
        W("tensor", "lb", b1_lb)
        W("tensor", "dve", b1_done.get(ei - 1, 0))
        if ei == 0:
            W("tensor", "dve", ae_last)
        t_b1 = tr(P1[:, 1168 + bpar * 16: 1184 + bpar * 16],
                  b1stage[0:16, bpar * 128:(bpar + 1) * 128], idf[0:16, 0:16],
                  inc=("pe", 1))
        prev_t_b1 = t_b1
        W("vector", "pe", t_b1)
        W("vector", "act", b1col_readers.get(ei - 2, 0))
        b1_done[ei] = dts(b1col[:, bpar * 16:(bpar + 1) * 16],
                          P1[:, 1168 + bpar * 16: 1184 + bpar * 16], GATES[e], ALU.mult,
                          inc=("dve", 1))

    b1col_readers = {}
    emit_b1(0, EXPERTS[0])

    badd = 0
    for st in range(NST):
        W("vector", "lr", b2t_lr)
        W("vector", "dve", ln2_evac[-1])
        badd = dtt(x_tm[:, st * 1024:(st + 1) * 1024],
                   x_tm[:, st * 1024:(st + 1) * 1024], b2t[:, :], ALU.add,
                   inc=("dve", 1))

    yregion_free = {r: ln2_evac[-1] for r in range(3)}

    for ei, e in enumerate(EXPERTS):
        if ei == 0 and (NQT - 1, NH - 1) in qk_of_h:
            for dc in range(NDC):
                W("gpsimd", "pe", qk_of_h[(NQT - 1, 2 * dc + 1)])
                dma("gpsimd", qa[:, dc * 2048:(dc + 1) * 2048],
                    w1_d[e, dc * 128:(dc + 1) * 128, :], ("lw1", 16))
            w1_loaded[ei] = prog.cnt["lw1"]
            for hc in range(NHC):
                W("gpsimd", "pe", qk_of_h[(NQT - 1, (hc // 2) * 2 + 1)])
                dma("gpsimd", ka[:, hc * 1024:(hc + 1) * 1024],
                    w2_d[e, hc * 128:(hc + 1) * 128, :], ("lw2", 16))
            w2_loaded[ei] = prog.cnt["lw2"]
        else:
            W("gpsimd", "pe", h_phase_done[ei])
            dma("gpsimd", qa[:, :].rearrange("p (dc h) -> p dc h", h=2048),
                w1_d[e].rearrange("(dc p) h -> p dc h", p=128), ("lw1", 16))
            w1_loaded[ei] = prog.cnt["lw1"]
            W("gpsimd", "pe", y_phase_done.get(ei - 1, qk_last))
            dma("gpsimd", ka[:, :].rearrange("p (hc o) -> p hc o", o=1024),
                w2_d[e].rearrange("(hc p) o -> p hc o", p=128), ("lw2", 16))
            w2_loaded[ei] = prog.cnt["lw2"]
        bpar = ei % 2
        if ei > 0:
            emit_b1(ei, e)

        hp_prev = {0: 0, 1: 0}
        w2_prev_groups = 0
        relu_last = 0
        relu_of = {}
        last_h_mm = 0
        last_y_mm = 0
        resid_last = 0
        for sc in range(4):
            for ht in range(NHC):
                reg = ht % 2
                W("tensor", "lw1", w1_loaded[ei])
                W("tensor", "act", hp_prev[reg])
                W("tensor", "dve", ln2_evac[sc * 4 + 3])
                if ei == 0 and sc == 0:
                    W("tensor", "dve", bias_ev)
                for dc in range(NDC):
                    last_h_mm = mm(P1[:, reg * 512:(reg + 1) * 512],
                                   qa[:, dc * 2048 + ht * 128: dc * 2048 + (ht + 1) * 128],
                                   xnv[:, dc * 2048 + sc * 512: dc * 2048 + (sc + 1) * 512],
                                   start=(dc == 0), stop=(dc == NDC - 1),
                                   inc=("pe", 1) if dc == NDC - 1 else None)
                W("scalar", "pe", last_h_mm)
                W("scalar", "pe", w2_prev_groups)
                W("scalar", "dve", b1_done[ei])
                hp_prev[reg] = act(c1[:, ht * 512:(ht + 1) * 512],
                                   P1[:, reg * 512:(reg + 1) * 512], AF.Relu,
                                   bias=b1col[:, bpar * 16 + ht: bpar * 16 + ht + 1],
                                   scale=GATES[e], inc=("act", 1))
                relu_last = hp_prev[reg]
                relu_of[ht] = relu_last
            for stl in range(4):
                st = sc * 4 + stl
                for half in range(2):
                    r = (stl * 2 + half) % 3
                    poff = r * 512
                    W("tensor", "lw2", w2_loaded[ei])
                    W("tensor", "dve", yregion_free[r])
                    for hc in range(NHC):
                        if hc % 4 == 0:
                            W("tensor", "act", relu_of[min(hc + 3, NHC - 1)])
                        last_y_mm = mm(
                            P0[:, poff:poff + 512],
                            c1[:, hc * 512 + stl * 128: hc * 512 + (stl + 1) * 128],
                            ka[:, hc * 1024 + half * 512: hc * 1024 + half * 512 + 512],
                            start=(hc == 0), stop=(hc == NHC - 1),
                            inc=("pe", 1) if hc == NHC - 1 else None)
                    W("vector", "pe", last_y_mm)
                    W("vector", "dve", badd)
                    resid_last = dtt(
                        x_tm[:, st * 1024 + half * 512: st * 1024 + half * 512 + 512],
                        x_tm[:, st * 1024 + half * 512: st * 1024 + half * 512 + 512],
                        P0[:, poff:poff + 512], ALU.add, inc=("dve", 1))
                    yregion_free[r] = resid_last
                if ei == len(EXPERTS) - 1:
                    W("sync", "dve", resid_last)
                    dma("sync", y_d.rearrange("(t p) d -> p t d", p=128)[:, st:st + 1, :],
                        x_tm[:, st * 1024:(st + 1) * 1024].rearrange(
                            "p (a d) -> p a d", a=1), ("st", 16))
            w2_prev_groups = last_y_mm
        h_phase_done[ei + 1] = last_h_mm
        y_phase_done[ei] = last_y_mm
        b2_mm_done = last_y_mm
        b1col_readers[ei] = relu_last

    W("sync", "st", prog.cnt["st"])


_NC_CACHE = None


def kernel(L0_attn_x, L0_attn_Wq, L0_attn_bq, L0_attn_Wk, L0_attn_bk, L0_attn_Wv,
           L0_attn_bv, L0_attn_Wo, L0_attn_bo, L0_ffn_w_gate, L0_ffn_w_noise,
           L0_ffn_W1, L0_ffn_b1, L0_ffn_W2, L0_ffn_b2):
    del L0_ffn_w_gate, L0_ffn_w_noise
    global _NC_CACHE
    if _NC_CACHE is None:
        _NC_CACHE = build_program()
    nc = _NC_CACHE
    x = np.ascontiguousarray(np.asarray(L0_attn_x, dtype=np.float32))
    common = {
        "wq": np.ascontiguousarray(np.asarray(L0_attn_Wq, np.float32)),
        "wk": np.ascontiguousarray(np.asarray(L0_attn_Wk, np.float32)),
        "wv": np.ascontiguousarray(np.asarray(L0_attn_Wv, np.float32)),
        "wo": np.ascontiguousarray(np.asarray(L0_attn_Wo, np.float32)),
        "bq": np.ascontiguousarray(np.asarray(L0_attn_bq, np.float32)),
        "bk": np.ascontiguousarray(np.asarray(L0_attn_bk, np.float32)),
        "bv": np.ascontiguousarray(np.asarray(L0_attn_bv, np.float32)),
        "bo": np.ascontiguousarray(np.asarray(L0_attn_bo, np.float32)),
        "w1": np.ascontiguousarray(np.asarray(L0_ffn_W1, np.float32)),
        "b1": np.ascontiguousarray(np.asarray(L0_ffn_b1, np.float32)),
        "w2": np.ascontiguousarray(np.asarray(L0_ffn_W2, np.float32)),
        "b2": np.ascontiguousarray(np.asarray(L0_ffn_b2, np.float32)),
    }
    in_maps = []
    for b in range(8):
        m = dict(common)
        m["x"] = np.ascontiguousarray(x[b])
        in_maps.append(m)
    res = run_bass_kernel_spmd(nc, in_maps, core_ids=list(range(8)))
    out = np.stack([res.results[b]["y"] for b in range(8)], axis=0)
    return out.astype(np.float32)


# version 39
# speedup vs baseline: 1.0263x; 1.0080x over previous
import os
import math
import contextlib
import numpy as np

import concourse.bass as bass
import concourse.mybir as mybir
from concourse.bass_utils import run_bass_kernel_spmd

F32 = mybir.dt.float32
BF16 = mybir.dt.bfloat16
AF = mybir.ActivationFunctionType
ALU = mybir.AluOpType
AX = mybir.AxisListType

S = 2048
DM = 1024
NH = 16
HD = 64
NST = 16
NDC = 8
NQT = 16
HID = 2048
NHC = 16
EPS = 1e-5

_tv = np.array([100.0, 0.0, 0.0, 0.0], np.float32)
_ex = np.exp(_tv - _tv.max(), dtype=np.float32)
_gk = (_ex / _ex.sum(dtype=np.float32)).astype(np.float32)
GATES = {1: float(_gk[0]), 0: float(_gk[1]), 2: float(_gk[2]), 3: float(_gk[3])}
EXPERTS = [0, 1, 2, 3]

ENGS = ("tensor", "scalar", "vector", "sync", "gpsimd")
KPHASE = int(os.environ.get("KPHASE", "99"))
KQT = int(os.environ.get("KQT", "16"))
KQT0 = int(os.environ.get("KQT0", "0"))
KNH = int(os.environ.get("KNH", "16"))
KSKIP = set(os.environ.get("KSKIP", "").split(","))


class Prog:

    def __init__(self):
        self.q = {e: [] for e in ENGS}
        self.cnt = {}
        self.lastwait = {}
        self.sems = None

    def addsem(self, *names):
        for n in names:
            self.cnt[n] = 0

    def op(self, eng, fn, inc=None):
        if inc is None:
            self.q[eng].append(fn)
            return None
        s, a = inc
        self.cnt[s] += a
        tgt = self.cnt[s]

        def run(e, f=fn, s=s, a=a, self_=self):
            ins = f(e)
            ins.then_inc(self_.sems[s], a)

        self.q[eng].append(run)
        return tgt

    def wait(self, eng, sem, val):
        if val is None or val <= 0:
            return
        assert val <= self.cnt[sem], f"wait target {val} > planned {self.cnt[sem]} for {sem}"
        if self.lastwait.get((eng, sem), 0) >= val:
            return
        self.lastwait[(eng, sem)] = val

        def run(e, s=sem, v=val, self_=self):
            e.wait_ge(self_.sems[s], v)

        self.q[eng].append(run)

    def replay(self, eng, e):
        for f in self.q[eng]:
            f(e)


def build_program():
    nc = bass.Bass("TRN2", target_bir_lowering=False, debug=False)

    x_d = nc.dram_tensor("x", [S, DM], F32, kind="ExternalInput").ap()
    wq_d = nc.dram_tensor("wq", [DM, DM], F32, kind="ExternalInput").ap()
    wk_d = nc.dram_tensor("wk", [DM, DM], F32, kind="ExternalInput").ap()
    wv_d = nc.dram_tensor("wv", [DM, DM], F32, kind="ExternalInput").ap()
    wo_d = nc.dram_tensor("wo", [DM, DM], F32, kind="ExternalInput").ap()
    bq_d = nc.dram_tensor("bq", [DM], F32, kind="ExternalInput").ap()
    bk_d = nc.dram_tensor("bk", [DM], F32, kind="ExternalInput").ap()
    bv_d = nc.dram_tensor("bv", [DM], F32, kind="ExternalInput").ap()
    bo_d = nc.dram_tensor("bo", [DM], F32, kind="ExternalInput").ap()
    w1_d = nc.dram_tensor("w1", [16, DM, HID], F32, kind="ExternalInput").ap()
    b1_d = nc.dram_tensor("b1", [16, HID], F32, kind="ExternalInput").ap()
    w2_d = nc.dram_tensor("w2", [16, HID, DM], F32, kind="ExternalInput").ap()
    b2_d = nc.dram_tensor("b2", [16, DM], F32, kind="ExternalInput").ap()
    y_d = nc.dram_tensor("y", [S, DM], F32, kind="ExternalOutput").ap()
    bsc_d = nc.dram_tensor("bsc", [DM], F32, kind="Internal").ap()

    prog = Prog()
    prog.addsem("lx0", "lx1", "lx2", "lx3", "lw", "lw1", "lw2", "lr", "lb", "st", "pev", "pe", "act", "dve", "pool")

    with contextlib.ExitStack() as stack:
        E = stack.enter_context

        SBUF_BASE = 16512
        cur = [0]

        def at(name, shape, dtype, off=None, adv=True):
            nbytes = (shape[1] * (4 if dtype == F32 else 2) + 31) // 32 * 32
            if off is None:
                off = cur[0]
                if adv:
                    cur[0] += nbytes
            h = nc.alloc_sbuf_tensor_at(name, shape, dtype, offset=SBUF_BASE + off)
            return h

        x_tm = at("x_tm", [128, NST * 1024], F32)
        off_xnv = cur[0]
        xnv = at("xnv", [128, 8 * 2048], BF16)
        off_qa = cur[0]
        qa = at("qa", [128, 8 * 2048], BF16)
        off_ka = cur[0]
        ka = at("ka", [128, 16 * 1024], BF16)
        off_c1 = cur[0]
        c1 = at("c1", [128, 8192], BF16)
        qnkn = at("qnkn", [16, 4096], F32, off=off_c1, adv=False)
        cur[0] = off_c1 + 16384
        off_c2 = cur[0]
        c2 = at("c2", [128, 8192], BF16)
        b2t = at("b2t", [128, 1024], F32, off=off_c2, adv=False)
        atok = at("atok", [128, 2048], BF16)
        attT = at("attT", [128, 1024], BF16, off=off_c1 + 12288, adv=False)
        idb = at("idb", [128, 128], BF16)
        idf = at("idf", [16, 16], F32)
        trimask = at("trimask", [128, 128], BF16)
        negut = at("negut", [128, 128], BF16)
        bias_sb = at("bias_sb", [128, 256], F32)
        lncols = at("lncols", [128, 112], F32)
        swc = at("swc", [128, 8], F32)
        bqcol = at("bqcol", [128, 8], F32)
        bkcol = at("bkcol", [128, 8], F32)
        b1col = at("b1col", [128, 32], F32)
        qkm = at("qkm", [128, 256], BF16)
        ones_r = at("ones_r", [1, 128], BF16)
        ones_c = at("ones_c", [128, 1], BF16)
        onesf16 = at("onesf16", [1, 16], F32)
        onesf_col = at("onesf_col", [1, 128], F32)
        bvrow = at("bvrow", [1, 1024], BF16)
        borow = at("borow", [1, 1024], BF16)
        b2row = at("b2row", [1, 1024], BF16)
        ksrow = at("ksrow", [1, 16], F32)
        kcol16 = at("kcol16", [16, 1], F32)
        kmax1 = at("kmax1", [1, 1], F32)
        kneg1 = at("kneg1", [1, 1], F32)
        kapcol = at("kapcol", [16, 1], F32)
        bstage = at("bstage", [16, 128], F32)
        b1stage = at("b1stage", [16, 256], F32)
        assert cur[0] <= 212830, f"sbuf over budget: {cur[0]}"

        P0 = E(nc.psum_tensor("P0", [128, 1536], F32))
        P1 = E(nc.psum_tensor("P1", [128, 1536], F32))
        PTa = E(nc.psum_tensor("PTa", [128, 1024], BF16))
        P2 = E(nc.psum_tensor("P2", [128, 512], F32))

        sems = {}
        for n in ("lx0", "lx1", "lx2", "lx3", "lw", "lw1", "lw2", "lr", "lb", "st",
                  "pev", "pe", "act", "dve", "pool"):
            sems[n] = E(nc.semaphore(n))
        prog.sems = sems

        plan(prog, locals())

        block = E(nc.Block())

        @block.tensor
        def _(e):
            prog.replay("tensor", e)

        @block.scalar
        def _(e):
            prog.replay("scalar", e)

        @block.vector
        def _(e):
            prog.replay("vector", e)

        @block.sync
        def _(e):
            prog.replay("sync", e)

        @block.gpsimd
        def _(e):
            prog.replay("gpsimd", e)

    return nc


def plan(prog, ns):
    P = prog.op
    W = prog.wait
    x_d, y_d, bsc_d = ns["x_d"], ns["y_d"], ns["bsc_d"]
    wq_d, wk_d, wv_d, wo_d = ns["wq_d"], ns["wk_d"], ns["wv_d"], ns["wo_d"]
    bq_d, bk_d, bv_d, bo_d = ns["bq_d"], ns["bk_d"], ns["bv_d"], ns["bo_d"]
    w1_d, b1_d, w2_d, b2_d = ns["w1_d"], ns["b1_d"], ns["w2_d"], ns["b2_d"]
    x_tm, xnv, qa, ka, c1, qnkn, c2 = (
        ns["x_tm"], ns["xnv"], ns["qa"], ns["ka"], ns["c1"], ns["qnkn"], ns["c2"])
    atok, attT, idb, idf, trimask = ns["atok"], ns["attT"], ns["idb"], ns["idf"], ns["trimask"]
    negut = ns["negut"]
    b2t = ns["b2t"]
    bias_sb, lncols, swc = ns["bias_sb"], ns["lncols"], ns["swc"]
    bqcol, bkcol, b1col, qkm = ns["bqcol"], ns["bkcol"], ns["b1col"], ns["qkm"]
    ones_r, onesf16, ones_c = ns["ones_r"], ns["onesf16"], ns["ones_c"]
    onesf_col = ns["onesf_col"]
    bvrow, borow, b2row = ns["bvrow"], ns["borow"], ns["b2row"]
    ksrow, kcol16, kmax1, kneg1, kapcol = (
        ns["ksrow"], ns["kcol16"], ns["kmax1"], ns["kneg1"], ns["kapcol"])
    bstage, b1stage = ns["bstage"], ns["b1stage"]
    P0, P1, PTa, P2 = ns["P0"], ns["P1"], ns["PTa"], ns["P2"]

    def mm(out, lhsT, rhs, start, stop, inc=None):
        return P("tensor",
                 lambda e, o=out, l=lhsT, r=rhs, s=start, t=stop:
                 e.matmul(o, l, r, start=s, stop=t), inc)

    def tr(out, in_, ident, inc=None):
        return P("tensor",
                 lambda e, o=out, i=in_, d=ident: e.transpose(out=o, in_=i, identity=d), inc)

    def act(out, in_, func, bias=0.0, scale=1.0, accum=None, inc=None):
        return P("scalar",
                 lambda e, o=out, i=in_, f=func, b=bias, s=scale, a=accum:
                 e.activation(out=o, in_=i, func=f, bias=b, scale=s, accum_out=a), inc)

    def dcopy(out, in_, inc=None):
        return P("vector", lambda e, o=out, i=in_: e.tensor_copy(out=o, in_=i), inc)

    def acopy(out, in_, inc=None):
        return P("scalar", lambda e, o=out, i=in_: e.copy(out, in_) if False else e.copy(o, i), inc)

    def dtt(out, in0, in1, aop, inc=None):
        return P("vector",
                 lambda e, o=out, a=in0, b=in1, p=aop: e.tensor_tensor(out=o, in0=a, in1=b, op=p),
                 inc)

    def dts(out, in0, scalar, aop, inc=None, accum=None):
        return P("vector",
                 lambda e, o=out, a=in0, s=scalar, p=aop, ac=accum:
                 e.tensor_scalar(out=o, in0=a, scalar1=s, scalar2=None, op0=p, accum_out=ac), inc)

    def dstt(out, in0, scalar, in1, op0, op1, accum=None, inc=None):
        return P("vector",
                 lambda e, o=out, a=in0, s=scalar, b=in1, p0=op0, p1=op1, ac=accum:
                 e.scalar_tensor_tensor(out=o, in0=a, scalar=s, in1=b, op0=p0, op1=p1,
                                        accum_out=ac), inc)

    def dred(out, in_, aop, inc=None):
        return P("vector",
                 lambda e, o=out, i=in_, p=aop: e.tensor_reduce(out=o, in_=i, axis=AX.X, op=p),
                 inc)

    def drecip(out, in_, inc=None):
        return P("vector", lambda e, o=out, i=in_: e.reciprocal(out=o, in_=i), inc)

    def dma(eng, out, in_, inc):
        return P(eng, lambda e, o=out, i=in_: e.dma_start(out=o, in_=i), inc)

    def gmemset(apx, v, inc=None):
        return P("gpsimd", lambda e, a=apx, vv=v: e.memset(a, vv), inc)

    ptring = {"idx": 0, "free": [("dve", 0)], "bank": None}

    def pt_group(srcs, mk_evac, waits=(), evac="vector", src_waits=None):
        b = ptring["idx"]
        ptring["idx"] = (b + 1) % len(ptring["bank"])
        PTx = ptring["bank"][b]
        fs, fv = ptring["free"][b]
        W("tensor", fs, fv)
        for sem, val in waits:
            W("tensor", sem, val)
        t_last = 0
        for i, s in enumerate(srcs):
            if src_waits is not None:
                W("tensor", src_waits[i][0], src_waits[i][1])
            t_last = tr(PTx[:, i * 128:(i + 1) * 128], s, idb[:],
                        inc=("pe", 1) if i == len(srcs) - 1 else None)
        out_ap, in_ap = mk_evac(PTx, len(srcs))
        if evac == "act":
            W("scalar", "pe", t_last)
            ev = ("act", acopy(out_ap, in_ap, inc=("act", 1)))
        else:
            W("vector", "pe", t_last)
            ev = ("dve", dcopy(out_ap, in_ap, inc=("dve", 1)))
        ptring["free"][b] = ev
        return t_last, ev

    ptring["bank"] = [PTa]

    m = gmemset(idb[:], 0.0, inc=("pool", 1))
    W("gpsimd", "pool", m)
    P("gpsimd", lambda e: e.affine_select(out=idb[:], in_=idb[:], compare_op=ALU.not_equal,
                                          fill=1.0, base=0, pattern=[[-1, 128]],
                                          channel_multiplier=1))
    m = gmemset(idf[:], 0.0, inc=("pool", 1))
    W("gpsimd", "pool", m)
    P("gpsimd", lambda e: e.affine_select(out=idf[:], in_=idf[:], compare_op=ALU.not_equal,
                                          fill=1.0, base=0, pattern=[[-1, 16]],
                                          channel_multiplier=1))
    m = gmemset(trimask[:], 1.0, inc=("pool", 1))
    W("gpsimd", "pool", m)
    P("gpsimd", lambda e: e.affine_select(out=trimask[:], in_=trimask[:], compare_op=ALU.is_ge,
                                          fill=0.0, base=0, pattern=[[1, 128]],
                                          channel_multiplier=-1))
    m = gmemset(negut[:], -480.0, inc=("pool", 1))
    W("gpsimd", "pool", m)
    P("gpsimd", lambda e: e.affine_select(out=negut[:], in_=negut[:], compare_op=ALU.is_ge,
                                          fill=0.0, base=-1, pattern=[[1, 128]],
                                          channel_multiplier=-1))
    gmemset(ones_c[:], 1.0)
    gmemset(onesf_col[:], 1.0)
    gmemset(ones_r[:], 1.0)
    gmemset(onesf16[:], 1.0)
    m = gmemset(qkm[:], 0.0, inc=("pool", 1))
    W("gpsimd", "pool", m)
    for g in range(8):
        gmemset(qkm[0:64, g * 16 + 2 * g: g * 16 + 2 * g + 1], 0.0625)
        gmemset(qkm[64:128, g * 16 + 2 * g + 1: g * 16 + 2 * g + 2], 0.0625)
        gmemset(qkm[0:64, 128 + g * 16 + 2 * g: 128 + g * 16 + 2 * g + 1], 0.0625)
        last = gmemset(qkm[64:128, 128 + g * 16 + 2 * g + 1: 128 + g * 16 + 2 * g + 2],
                       0.0625, inc=("pool", 1) if g == 7 else None)
    consts_pool = prog.cnt["pool"]

    dma("gpsimd", bvrow[:], bv_d[None, :], ("lr", 16))
    dma("gpsimd", borow[:], bo_d[None, :], ("lr", 16))
    borow_lr = prog.cnt["lr"]

    x_r = x_d.rearrange("(t p) d -> p t d", p=128)
    xload = []
    for t in range(4):
        dma("sync", x_tm[:, t * 4096:(t + 1) * 4096].rearrange("p (a d) -> p a d", d=1024),
            x_r[:, 4 * t:4 * t + 4, :], (f"lx{t}", 16))
        xload.append((f"lx{t}", prog.cnt[f"lx{t}"]))
    dma("sync", bstage[0:8, :], bq_d.rearrange("(t p) -> t p", p=128), ("lb", 16))
    bq_lb = prog.cnt["lb"]

    def wload(wd):
        dma("gpsimd", c1[:, :].rearrange("p (ot d) -> p ot d", d=1024),
            wd.rearrange("(ot p) d -> p ot d", p=128), ("lw", 16))
        return prog.cnt["lw"]

    wq_lw = wload(wq_d)

    SC, SQ, NM, MSQ, VAR, RSTD, LNV = 0, 16, 32, 48, 64, 80, 96

    def ln_stats(st, entry_pe, entry_dve, prev_stt, wait_lx, sbuf_discard=False):
        if wait_lx:
            W("vector", xload[st // 4][0], xload[st // 4][1])
        W("vector", "pe", entry_pe)
        W("vector", "dve", entry_dve)
        dred(lncols[:, SC + st:SC + st + 1], x_tm[:, st * 1024:(st + 1) * 1024], ALU.add)
        W("vector", "dve", prev_stt)
        out_ap = c1[:, 0:1024] if sbuf_discard else P0[:, 0:1024]
        return dstt(out_ap, x_tm[:, st * 1024:(st + 1) * 1024], 1.0,
                    x_tm[:, st * 1024:(st + 1) * 1024], ALU.mult, ALU.mult,
                    accum=lncols[:, SQ + st:SQ + st + 1], inc=("dve", 1))

    def layer_norm(entry_pe, entry_dve, wait_lx, entry_act=0, stats_done=None):
        grouped = stats_done is None
        evac_dve = []
        prev_ts = 0
        prev_evac = 0
        prev_tr = 0
        tr_half = {0: 0, 1: 0}
        prev_stt = 0
        ngrp = 4 if grouped else 1
        gw = 16 // ngrp
        for grp in range(ngrp):
            c0 = grp * gw
            if grouped:
                for st in range(c0, c0 + gw):
                    prev_stt = ln_stats(st, entry_pe, entry_dve, prev_stt, wait_lx)
                sd = prev_stt
            else:
                sd = stats_done
            W("vector", "dve", sd)
            m1 = dts(lncols[:, NM + c0:NM + c0 + gw], lncols[:, SC + c0:SC + c0 + gw],
                     -1.0 / 1024.0, ALU.mult, inc=("dve", 1))
            W("vector", "dve", m1)
            m2 = dstt(lncols[:, MSQ + c0:MSQ + c0 + gw], lncols[:, NM + c0:NM + c0 + gw],
                      1.0, lncols[:, NM + c0:NM + c0 + gw], ALU.mult, ALU.mult,
                      inc=("dve", 1))
            m3 = dts(lncols[:, SQ + c0:SQ + c0 + gw], lncols[:, SQ + c0:SQ + c0 + gw],
                     1.0 / 1024.0, ALU.mult, inc=("dve", 1))
            W("vector", "dve", m3)
            m4 = dstt(lncols[:, VAR + c0:VAR + c0 + gw], lncols[:, SQ + c0:SQ + c0 + gw],
                      EPS, lncols[:, MSQ + c0:MSQ + c0 + gw], ALU.add, ALU.subtract,
                      inc=("dve", 1))
            W("scalar", "dve", m4)
            a1 = act(lncols[:, LNV + c0:LNV + c0 + gw], lncols[:, VAR + c0:VAR + c0 + gw],
                     AF.Ln, bias=0.0, scale=1.0, inc=("act", 1))
            W("scalar", "act", a1)
            a2 = act(lncols[:, RSTD + c0:RSTD + c0 + gw], lncols[:, LNV + c0:LNV + c0 + gw],
                     AF.Exp, bias=0.0, scale=-0.5, inc=("act", 1))
            W("vector", "act", a2)
            m5 = dtt(lncols[:, MSQ + c0:MSQ + c0 + gw], lncols[:, NM + c0:NM + c0 + gw],
                     lncols[:, RSTD + c0:RSTD + c0 + gw], ALU.mult, inc=("dve", 1))
            for st in range(c0, c0 + gw):
                half = st % 2
                xn_tmp = atok[:, half * 1024:(half + 1) * 1024]
                W("scalar", "dve", m5)
                W("scalar", "pe", tr_half[half])
                prev_ts = act(xn_tmp[:, 0:1024], x_tm[:, st * 1024:(st + 1) * 1024],
                              AF.Identity, bias=lncols[:, MSQ + st:MSQ + st + 1],
                              scale=lncols[:, RSTD + st:RSTD + st + 1], inc=("act", 1))
                waits = [("act", prev_ts), ("pool", consts_pool)]
                if entry_act:
                    waits.append(("act", entry_act))
                prev_tr, ev = pt_group(
                    [xn_tmp[:, dc * 128:(dc + 1) * 128] for dc in range(NDC)],
                    lambda PTx, n, st=st: (
                        xnv[:, :].rearrange("p (dc c) -> p dc c", c=2048)[
                            :, :, st * 128:(st + 1) * 128],
                        PTx[:, :].rearrange("p (dc c) -> p dc c", c=128)),
                    waits=waits)
                tr_half[half] = prev_tr
                prev_evac = ev[1]
                evac_dve.append(prev_evac)
        return evac_dve, prev_tr

    def early_store(dve_gate):
        for st in range(NST):
            W("sync", "dve", dve_gate)
            dma("sync", y_d.rearrange("(t p) d -> p t d", p=128)[:, st:st + 1, :],
                x_tm[:, st * 1024:(st + 1) * 1024].rearrange(
                    "p (a d) -> p a d", a=1), ("st", 16))
        W("sync", "st", prog.cnt["st"])

    ln1_evac, ln1_tr = layer_norm(0, 0, wait_lx=True)
    if KPHASE == 1:
        early_store(prog.cnt["dve"])
        return

    def build_wt(load_lw):
        tr_last = 0
        ev = 0
        for dc in range(NDC):
            tr_last, ev = pt_group(
                [c1[:, ot * 1024 + dc * 128: ot * 1024 + (dc + 1) * 128]
                 for ot in range(8)],
                lambda PTx, n, dc=dc: (c2[:, dc * 1024:(dc + 1) * 1024], PTx[:, :]),
                waits=[("lw", load_lw)])
        return tr_last, ev[1]

    W("tensor", "lb", bq_lb)
    tbq = tr(P1[:, 1168:1176], bstage[0:8, :], idf[0:8, 0:8], inc=("pe", 1))
    W("vector", "pe", tbq)
    ebq = dcopy(bqcol[:, 0:8], P1[:, 1168:1176], inc=("dve", 1))
    W("sync", "pe", tbq)
    dma("sync", bstage[0:8, :], bk_d.rearrange("(t p) -> t p", p=128), ("lb", 16))
    bk_lb = prog.cnt["lb"]

    wq_tr, wq_evac = build_wt(wq_lw)
    W("tensor", "lb", bk_lb)
    W("tensor", "dve", ebq)
    tbk = tr(P1[:, 1168:1176], bstage[0:8, :], idf[0:8, 0:8], inc=("pe", 1))
    W("vector", "pe", tbk)
    ebk = dcopy(bkcol[:, 0:8], P1[:, 1168:1176], inc=("dve", 1))

    W("sync", "pe", tbk)
    comb = 0
    first = True
    for e in EXPERTS:
        slot = b1stage[0:8, 0:128] if first else b1stage[0:8, 128:256]
        dma("sync", slot, b2_d[e, :].rearrange("(t p) -> t p", p=128), ("lb", 16))
        l_ = prog.cnt["lb"]
        W("vector", "lb", l_)
        W("vector", "dve", comb)
        m_ = dts(slot, slot, GATES[e], ALU.mult, inc=("dve", 1))
        if first:
            comb = m_
            first = False
        else:
            W("vector", "dve", m_)
            comb = dtt(b1stage[0:8, 0:128], b1stage[0:8, 0:128], b1stage[0:8, 128:256],
                       ALU.add, inc=("dve", 1))
        if e != EXPERTS[-1]:
            W("sync", "dve", comb)
    W("sync", "dve", comb)
    dma("sync", bsc_d.rearrange("(t p) -> t p", p=128), b1stage[0:8, 0:128], ("lb", 16))
    bounce_lb = prog.cnt["lb"]

    W("gpsimd", "pe", wq_tr)
    wk_lw = wload(wk_d)

    def proj_fm(dest, dest_off, bcol, ln_evacs, w_evac, entry_act=0):
        prev_ev = {0: entry_act, 1: entry_act}
        last_mm = 0
        last_ev = 0
        for sc in range(4):
            for ot in range(8):
                reg = (sc * 8 + ot) % 2
                W("tensor", "dve", w_evac)
                W("tensor", "dve", ln_evacs[sc * 4 + 3])
                W("tensor", "act", prev_ev[reg])
                for dc in range(NDC):
                    last_mm = mm(P1[:, reg * 512:(reg + 1) * 512],
                                 c2[:, dc * 1024 + ot * 128: dc * 1024 + (ot + 1) * 128],
                                 xnv[:, dc * 2048 + sc * 512: dc * 2048 + (sc + 1) * 512],
                                 start=(dc == 0), stop=(dc == NDC - 1),
                                 inc=("pe", 1) if dc == NDC - 1 else None)
                W("scalar", "pe", last_mm)
                prev_ev[reg] = act(dest[:, ot * 2048 + sc * 512: ot * 2048 + (sc + 1) * 512],
                                   P1[:, reg * 512:(reg + 1) * 512], AF.Identity,
                                   bias=bcol[:, ot:ot + 1], scale=1.0, inc=("act", 1))
                last_ev = prev_ev[reg]
        return last_mm, last_ev

    W("scalar", "dve", ebq)
    q_mm, q_ev = proj_fm(qa, 0, bqcol, ln1_evac, wq_evac)

    wk_tr, wk_evac = build_wt(wk_lw)
    W("gpsimd", "pe", wk_tr)
    wv_lw = wload(wv_d)

    W("scalar", "dve", ebk)
    k_mm, k_ev = proj_fm(ka, 0, bkcol, ln1_evac, wk_evac, entry_act=q_ev)

    wv_tr, wv_evac = build_wt(wv_lw)
    W("gpsimd", "pe", wv_tr)
    wo_lw = wload(wo_d)

    vregions = [(P1, 0), (P1, 512), (P0, 0), (P0, 512)]
    v_prev_ev = {0: 0, 1: 0, 2: 0, 3: 0}
    v_last_ev = 0
    v_last_mm = 0
    for st in range(NST):
        grp = []
        for half in range(2):
            r = (st * 2 + half) % 4
            pbuf, poff = vregions[r]
            W("tensor", "dve", wv_evac)
            W("tensor", "dve", v_prev_ev[r])
            W("tensor", "lr", borow_lr)
            if st == 0 and half == 0:
                W("tensor", "pe", k_mm)
                W("tensor", "act", k_ev)
            for dc in range(NDC):
                mm(pbuf[:, poff:poff + 512],
                   xnv[:, dc * 2048 + st * 128: dc * 2048 + (st + 1) * 128],
                   c2[:, dc * 1024 + half * 512: dc * 1024 + half * 512 + 512],
                   start=(dc == 0), stop=False)
            v_last_mm = mm(pbuf[:, poff:poff + 512], ones_r[:],
                           bvrow[:, half * 512:half * 512 + 512],
                           start=False, stop=True, inc=("pe", 1))
            grp.append((r, pbuf, poff))
        for half in range(2):
            r, pbuf, poff = grp[half]
            W("vector", "pe", v_last_mm)
            dest = xnv[:, :].rearrange("p (dc c) -> p dc c", c=2048)[
                :, 4 * half:4 * half + 4, st * 128:(st + 1) * 128]
            src = pbuf[:, poff:poff + 512].rearrange("p (dc c) -> p dc c", c=128)
            v_prev_ev[r] = dcopy(dest, src, inc=("dve", 1))
            v_last_ev = v_prev_ev[r]

    wo_tr, wo_evac = build_wt(wo_lw)
    if KPHASE == 2:
        early_store(prog.cnt["dve"])
        return

    scr_free = {0: ln1_tr, 1: wo_tr}

    def scr_ap(b, lo, hi):
        return atok[:, lo:hi] if b == 0 else c1[:, 4096 + lo:4096 + hi]

    def norms_prep(src_t, mask_coloff, psum_dve_entry):
        last_mm = 0
        for g in range(8):
            b = g % 2
            halves = []
            for hf in range(2):
                W("vector", "pe", scr_free[b])
                W("vector", "dve", v_last_ev)
                lo = g * 2048 + hf * 1024
                halves.append(dtt(scr_ap(b, hf * 1024, hf * 1024 + 1024),
                                  src_t[:, lo:lo + 1024], src_t[:, lo:lo + 1024],
                                  ALU.mult, inc=("dve", 1)))
            if g == 0:
                W("tensor", "dve", psum_dve_entry)
            for c4 in range(4):
                W("tensor", "dve", halves[c4 // 2])
                out_ap = (P0[0:16, c4 * 512:(c4 + 1) * 512] if c4 < 3
                          else P1[0:16, 0:512])
                last_mm = mm(out_ap,
                             qkm[:, mask_coloff + g * 16: mask_coloff + (g + 1) * 16],
                             scr_ap(b, c4 * 512, (c4 + 1) * 512),
                             start=(g == 0), stop=(g == 7),
                             inc=("pe", 1) if c4 == 3 else None)
            scr_free[b] = last_mm
        return last_mm

    qn_mm = norms_prep(qa, 0, v_last_ev)
    W("vector", "pe", qn_mm)
    dcopy(qnkn[0:16, 0:1536], P0[0:16, 0:1536], inc=("dve", 1))
    qn_ev = dcopy(qnkn[0:16, 1536:2048], P1[0:16, 0:512], inc=("dve", 1))
    kn_mm = norms_prep(ka, 128, qn_ev)
    W("vector", "pe", kn_mm)
    kn_p0 = dcopy(qnkn[0:16, 2048:2048 + 1536], P0[0:16, 0:1536], inc=("dve", 1))
    kn_ev = dcopy(qnkn[0:16, 2048 + 1536:4096], P1[0:16, 0:512], inc=("dve", 1))
    W("vector", "dve", kn_ev)
    r1 = dred(kcol16[0:16, 0:1], qnkn[0:16, 0:2048], ALU.max, inc=("dve", 1))
    W("vector", "dve", r1)
    r2 = dred(kapcol[0:16, 0:1], qnkn[0:16, 2048:4096], ALU.max, inc=("dve", 1))
    W("vector", "dve", r2)
    r3 = dtt(kcol16[0:16, 0:1], kcol16[0:16, 0:1], kapcol[0:16, 0:1], ALU.add,
             inc=("dve", 1))
    W("vector", "dve", r3)
    ng = dts(kcol16[0:16, 0:1], kcol16[0:16, 0:1], -1.0, ALU.mult, inc=("dve", 1))
    W("tensor", "dve", ng)
    t1 = tr(P1[0:1, 1168:1184], kcol16[0:16, 0:1], idf[0:16, 0:16], inc=("pe", 1))
    W("vector", "pe", t1)
    c1m = dcopy(ksrow[:, :], P1[0:1, 1168:1184], inc=("dve", 1))
    W("tensor", "dve", c1m)
    bc = mm(P1[:, 1280:1296], onesf_col[:, :], ksrow[:, :],
            start=True, stop=True, inc=("pe", 1))
    W("vector", "pe", bc)
    bias_ev = dcopy(bias_sb[:, 0:16], P1[:, 1280:1296], inc=("dve", 1))

    p_sb = [c1[:, 0:2048], c1[:, 2048:4096]]
    pT_sb = [c1[:, 4096:6144], c1[:, 6144:8192]]
    att_slot = [0, 64]
    CHUNK = 1536

    exp_done = [(0, 0, 0, 0)] * (NH + 2)
    pT_free = {0: 0, 1: 0}
    p_dve_free = {0: 0, 1: 0}
    attslot_free = {0: ("dve", 0), 1: ("dve", kn_p0)}
    ae_last = 0
    exp_bank = {0: ("dve", kn_p0), 1: ("dve", kn_p0), 2: ("dve", kn_p0)}
    atok_free = {0: kn_mm, 1: kn_mm}
    wo_done_prev = 0
    resid_done_prev = 0
    resid_of = {}
    ln2stt = {}
    ln2stt = {}
    last_exp = 0
    last_exp2 = 0
    qk_last = 0
    qk_of_h = {}

    def stage_A1(qt, h):
        nonlocal qk_last
        L = (qt + 1) * 128
        L1 = min(L, CHUNK)
        two = L > CHUNK
        dbuf = L <= 512
        soff = (h % 2) * 1024 if dbuf else 0
        par = h % 2
        g = h // 2
        hp = (h % 2) * 64
        bias_ap = bias_sb[:, h:h + 1]
        rhsq = qa[hp:hp + 64, g * 2048 + qt * 128: g * 2048 + (qt + 1) * 128]

        def st_tiles(lo, hi, off, war):
            nonlocal qk_last
            W("tensor", war[0], war[1])
            for jc in range(lo // 128, hi // 128):
                qk_last = mm(P0[:, off + (jc * 128 - lo): off + (jc * 128 - lo) + 128],
                             ka[hp:hp + 64, g * 2048 + jc * 128: g * 2048 + (jc + 1) * 128],
                             rhsq, start=True, stop=True,
                             inc=("pe", 1) if (jc + 1) * 128 >= hi else None)
            return qk_last

        def exp_chunk(lo, hi, off):
            W("scalar", "pe", qk_last)
            W("scalar", "dve", bias_ev)
            W("scalar", "dve", p_dve_free[par])
            return act(pT_sb[par][:, lo:hi], P0[:, off:off + (hi - lo)], AF.Exp,
                       bias=bias_ap, scale=0.125, inc=("act", 1))

        W("tensor", "act", q_ev)
        W("tensor", "act", k_ev)
        if dbuf:
            bk = par * 2
            war = exp_bank[bk]
            W("tensor", war[0], war[1])
            for jc in range(qt):
                mm(P0[:, soff + jc * 128: soff + (jc + 1) * 128],
                   ka[hp:hp + 64, g * 2048 + jc * 128: g * 2048 + (jc + 1) * 128],
                   rhsq, start=True, stop=True)
            dreg = P0[:, soff + qt * 128: soff + (qt + 1) * 128]
            mm(dreg, ka[hp:hp + 64, g * 2048 + qt * 128: g * 2048 + (qt + 1) * 128],
               rhsq, start=True, stop=False)
            qk_last = mm(dreg, negut[:, :], idb[:, :], start=False, stop=True,
                         inc=("pe", 1))
            qk_of_h[(qt, h)] = qk_last
            e1 = exp_chunk(0, L, soff)
            exp_bank[bk] = ("act", e1)
            exp_done[h] = (e1, e1, e1, 0)
            return None
        eA = 0
        e = 0
        for ci in range((L1 + 511) // 512):
            lo = ci * 512
            hi = min(lo + 512, L1)
            st_tiles(lo, hi, lo, exp_bank[ci])
            qk_of_h[(qt, h)] = qk_last
            e = exp_chunk(lo, hi, lo)
            exp_bank[ci] = ("act", e)
            if ci == 0:
                eA = e
        if not two:
            exp_done[h] = (e, e, eA, 0)
            return None
        return (rhsq, bias_ap, eA, e)

    def stage_A2(qt, h, carry):
        nonlocal qk_last
        if carry is None:
            return
        rhsq, bias_ap, eA, eB = carry
        L = (qt + 1) * 128
        par = h % 2
        g = h // 2
        hp = (h % 2) * 64
        L2 = L - CHUNK
        W("tensor", exp_bank[0][0], exp_bank[0][1])
        qk2 = 0
        for jc in range(L2 // 128):
            qk2 = mm(P0[:, jc * 128:(jc + 1) * 128],
                     ka[hp:hp + 64, g * 2048 + CHUNK + jc * 128:
                        g * 2048 + CHUNK + (jc + 1) * 128],
                     rhsq, start=True, stop=True,
                     inc=("pe", 1) if (jc + 1) * 128 >= L2 else None)
        qk_last = qk2
        qk_of_h[(qt, h)] = qk_last
        W("scalar", "pe", qk2)
        W("scalar", "dve", p_dve_free[par])
        exm2 = act(pT_sb[par][:, CHUNK:L], P0[:, 0:L2], AF.Exp,
                   bias=bias_ap, scale=0.125, inc=("act", 1))
        exp_bank[0] = ("act", exm2)
        exp_done[h] = (exm2, exm2, eA, exm2)

    bstate = {}

    def stage_B1(qt, h):
        if (qt + 1) * 128 <= 512:
            bstate[h] = 0
            return
        par = h % 2
        ex_last = exp_done[h][0]
        dcol = qt * 128
        pT_diag = pT_sb[par][:, dcol:dcol + 128]
        W("vector", "act", ex_last)
        W("vector", "pool", consts_pool)
        pm = dtt(pT_diag, pT_diag, trimask[:], ALU.mult, inc=("dve", 1))
        p_dve_free[par] = pm
        bstate[h] = pm

    def stage_B2(qt, h):
        nonlocal ae_last
        L = qt + 1
        par = h % 2
        small = L * 128 <= 512
        ex_last, _, exm1, exm2 = exp_done[h]
        pm = bstate.pop(h)
        W("tensor", "act", ex_last)
        W("tensor", "dve", pm)
        if small:
            W("tensor", "dve", attslot_free[par][1])
        else:
            W("tensor", "dve", ae_last)
        pv_last = 0

        def vfrag(jc):
            o = (h // 2) * 2048 + jc * 128 + (h % 2) * 64
            return xnv[:, o:o + 64]

        if small and par == 1:
            for jc in range(L):
                mm(P0[:, 512:576], pT_sb[1][:, jc * 128:(jc + 1) * 128], vfrag(jc),
                   start=(jc == 0), stop=(jc == L - 1))
            for jc in range(L):
                pv_last = mm(P0[:, 576:577], pT_sb[1][:, jc * 128:(jc + 1) * 128],
                             ones_c[:, :], start=(jc == 0), stop=(jc == L - 1),
                             inc=("pe", 1) if jc == L - 1 else None)
            sums_ap = P0[:, 576:577]
            pv_ap = P0[:, 512:576]
        else:
            a0 = att_slot[par]
            sc0 = 1024 + par
            for jc in range(L):
                mm(P2[:, a0:a0 + 64],
                   pT_sb[par][:, jc * 128:(jc + 1) * 128], vfrag(jc),
                   start=(jc == 0), stop=(jc == L - 1))
                pv_last = mm(P1[:, sc0:sc0 + 1],
                             pT_sb[par][:, jc * 128:(jc + 1) * 128], ones_c[:, :],
                             start=(jc == 0), stop=(jc == L - 1),
                             inc=("pe", 1) if jc == L - 1 else None)
            sums_ap = P1[:, sc0:sc0 + 1]
            pv_ap = P2[:, a0:a0 + 64]
        pT_free[par] = pv_last
        W("vector", "pe", pv_last)
        W("vector", "dve", attslot_free[par][1])
        rc = drecip(swc[:, 2 + par:3 + par], sums_ap, inc=("dve", 1))
        W("vector", "dve", rc)
        if h == 0:
            W("vector", "pe", atok_free[qt % 2])
        ae = dts(atok[:, (qt % 2) * 1024 + h * 64: (qt % 2) * 1024 + (h + 1) * 64],
                 pv_ap, swc[:, 2 + par:3 + par], ALU.mult, inc=("dve", 1))
        attslot_free[par] = ("dve", ae)
        ae_last = ae
        return ae

    def emit_tail(qt, last_ae):
        nonlocal wo_done_prev, resid_done_prev
        at_last, atev = pt_group(
            [atok[:, (qt % 2) * 1024 + dc * 128:(qt % 2) * 1024 + (dc + 1) * 128]
             for dc in range(NDC)],
            lambda PTx, n: (attT[:, :], PTx[:, :]),
            waits=[("pe", wo_done_prev)], evac="auto",
            src_waits=[("dve", last_ae_of.get((qt, 2 * dc + 1), last_ae))
                       for dc in range(NDC)])
        atok_free[qt % 2] = at_last
        wo_last = 0
        W("tensor", atev[0], atev[1])
        W("tensor", "dve", wo_evac)
        W("tensor", "dve", resid_done_prev)
        W("tensor", "lr", borow_lr)
        for half in range(2):
            for dc in range(NDC):
                mm(P1[:, half * 512:(half + 1) * 512],
                   attT[:, dc * 128:(dc + 1) * 128],
                   c2[:, dc * 1024 + half * 512: dc * 1024 + half * 512 + 512],
                   start=(dc == 0), stop=False)
            wo_last = mm(P1[:, half * 512:(half + 1) * 512], ones_r[:],
                         borow[:, half * 512:half * 512 + 512], start=False, stop=True,
                         inc=("pe", 1))
        wo_done_prev = wo_last
        W("vector", "pe", wo_last)
        rs = 0
        for half in range(2):
            rs = dtt(x_tm[:, qt * 1024 + half * 512: qt * 1024 + half * 512 + 512],
                     x_tm[:, qt * 1024 + half * 512: qt * 1024 + half * 512 + 512],
                     P1[:, half * 512:(half + 1) * 512], ALU.add, inc=("dve", 1))
        resid_done_prev = rs
        resid_of[qt] = rs

    steps = [(qt, h) for qt in range(KQT0, min(NQT, KQT)) for h in range(KNH)]
    last_ae_of = {}
    b1_emitted = set()
    stt_tile = 0
    stt_phase = 0

    def do_B1(s):
        if s not in b1_emitted:
            b1_emitted.add(s)
            stage_B1(*s)

    def do_B2(s):
        if s not in last_ae_of:
            last_ae_of[s] = stage_B2(*s)
            if s[1] == KNH - 1:
                emit_tail(s[0], last_ae_of[s])

    for idx, (qt, h) in enumerate(steps):
        if KNH == NH and stt_tile <= 14 and stt_tile in resid_of and \
                ((qt >= 5 and h in (5, 11)) or qt == NQT - 1):
            T = stt_tile
            if stt_phase == 0:
                W("vector", "dve", resid_of[T])
                dred(lncols[:, SC + T:SC + T + 1],
                     x_tm[:, T * 1024:(T + 1) * 1024], ALU.add)
                stt_phase = 1
            else:
                W("vector", "dve", resid_of[T])
                W("vector", "dve", ln2stt.get(T - 1, 0))
                ln2stt[T] = dstt(c1[:, 0:1024], x_tm[:, T * 1024:(T + 1) * 1024],
                                 1.0, x_tm[:, T * 1024:(T + 1) * 1024],
                                 ALU.mult, ALU.mult,
                                 accum=lncols[:, SQ + T:SQ + T + 1], inc=("dve", 1))
                stt_phase = 0
                stt_tile += 1
        if idx >= 2:
            do_B2(steps[idx - 2])
        if (qt, h) == (4, 0) and KNH == NH and KQT0 == 0:
            do_B1((3, 15))
            do_B2((3, 15))
            exp_bank[1] = ("dve", last_ae_of[(3, 15)])
        carry = stage_A1(qt, h)
        if idx >= 1:
            do_B1(steps[idx - 1])
        stage_A2(qt, h, carry)
    n = len(steps)
    if n >= 2:
        do_B2(steps[n - 2])
    do_B1(steps[n - 1])
    do_B2(steps[n - 1])

    attn_pe_done = wo_done_prev
    attn_dve_done = resid_done_prev
    if KPHASE == 4:
        early_store(prog.cnt["dve"])
        return

    if 14 in ln2stt:
        ln2stt[15] = ln_stats(15, 0, resid_of[15], ln2stt[14], wait_lx=False,
                              sbuf_discard=True)
    ln2_evac, ln2_tr = layer_norm(attn_pe_done, attn_dve_done, wait_lx=False,
                                  entry_act=prog.cnt["act"],
                                  stats_done=ln2stt.get(15, None))

    if KPHASE == 5:
        early_store(prog.cnt["dve"])
        return

    h_phase_done = {0: qk_last}
    y_phase_done = {}
    w1_loaded = {}
    w2_loaded = {}
    b1_done = {}
    prev_t_b1 = tbk
    W("gpsimd", "pe", attn_pe_done)
    W("gpsimd", "lb", bounce_lb)
    dma("gpsimd", b2t[:, :], bsc_d[None, :].broadcast_to((128, DM)), ("lr", 16))
    b2t_lr = prog.cnt["lr"]

    def emit_b1(ei, e):
        nonlocal prev_t_b1
        bpar = ei % 2
        W("sync", "pe", prev_t_b1)
        W("sync", "lb", bounce_lb)
        dma("sync", b1stage[0:16, bpar * 128:(bpar + 1) * 128],
            b1_d[e, :].rearrange("(t p) -> t p", p=128), ("lb", 16))
        b1_lb = prog.cnt["lb"]
        W("tensor", "lb", b1_lb)
        W("tensor", "dve", b1_done.get(ei - 1, 0))
        if ei == 0:
            W("tensor", "dve", ae_last)
        t_b1 = tr(P1[:, 1168 + bpar * 16: 1184 + bpar * 16],
                  b1stage[0:16, bpar * 128:(bpar + 1) * 128], idf[0:16, 0:16],
                  inc=("pe", 1))
        prev_t_b1 = t_b1
        W("vector", "pe", t_b1)
        W("vector", "act", b1col_readers.get(ei - 2, 0))
        b1_done[ei] = dts(b1col[:, bpar * 16:(bpar + 1) * 16],
                          P1[:, 1168 + bpar * 16: 1184 + bpar * 16], GATES[e], ALU.mult,
                          inc=("dve", 1))

    b1col_readers = {}
    emit_b1(0, EXPERTS[0])

    badd = 0
    for st in range(NST):
        W("vector", "lr", b2t_lr)
        W("vector", "dve", ln2_evac[-1])
        badd = dtt(x_tm[:, st * 1024:(st + 1) * 1024],
                   x_tm[:, st * 1024:(st + 1) * 1024], b2t[:, :], ALU.add,
                   inc=("dve", 1))

    yregion_free = {r: ln2_evac[-1] for r in range(3)}

    for ei, e in enumerate(EXPERTS):
        if ei == 0 and (NQT - 1, NH - 1) in qk_of_h:
            for dc in range(NDC):
                W("gpsimd", "pe", qk_of_h[(NQT - 1, 2 * dc + 1)])
                dma("gpsimd", qa[:, dc * 2048:(dc + 1) * 2048],
                    w1_d[e, dc * 128:(dc + 1) * 128, :], ("lw1", 16))
            w1_loaded[ei] = prog.cnt["lw1"]
            for hc in range(NHC):
                W("gpsimd", "pe", qk_of_h[(NQT - 1, (hc // 2) * 2 + 1)])
                dma("gpsimd", ka[:, hc * 1024:(hc + 1) * 1024],
                    w2_d[e, hc * 128:(hc + 1) * 128, :], ("lw2", 16))
            w2_loaded[ei] = prog.cnt["lw2"]
        else:
            W("gpsimd", "pe", h_phase_done[ei])
            dma("gpsimd", qa[:, :].rearrange("p (dc h) -> p dc h", h=2048),
                w1_d[e].rearrange("(dc p) h -> p dc h", p=128), ("lw1", 16))
            w1_loaded[ei] = prog.cnt["lw1"]
            W("gpsimd", "pe", y_phase_done.get(ei - 1, qk_last))
            dma("gpsimd", ka[:, :].rearrange("p (hc o) -> p hc o", o=1024),
                w2_d[e].rearrange("(hc p) o -> p hc o", p=128), ("lw2", 16))
            w2_loaded[ei] = prog.cnt["lw2"]
        bpar = ei % 2
        if ei > 0:
            emit_b1(ei, e)

        hp_prev = {0: 0, 1: 0}
        w2_prev_groups = 0
        relu_last = 0
        relu_of = {}
        last_h_mm = 0
        last_y_mm = 0
        resid_last = 0
        for sc in range(4):
            for ht in range(NHC):
                reg = ht % 2
                W("tensor", "lw1", w1_loaded[ei])
                W("tensor", "act", hp_prev[reg])
                W("tensor", "dve", ln2_evac[sc * 4 + 3])
                if ei == 0 and sc == 0:
                    W("tensor", "dve", bias_ev)
                for dc in range(NDC):
                    last_h_mm = mm(P1[:, reg * 512:(reg + 1) * 512],
                                   qa[:, dc * 2048 + ht * 128: dc * 2048 + (ht + 1) * 128],
                                   xnv[:, dc * 2048 + sc * 512: dc * 2048 + (sc + 1) * 512],
                                   start=(dc == 0), stop=(dc == NDC - 1),
                                   inc=("pe", 1) if dc == NDC - 1 else None)
                W("scalar", "pe", last_h_mm)
                W("scalar", "pe", w2_prev_groups)
                W("scalar", "dve", b1_done[ei])
                hp_prev[reg] = act(c1[:, ht * 512:(ht + 1) * 512],
                                   P1[:, reg * 512:(reg + 1) * 512], AF.Relu,
                                   bias=b1col[:, bpar * 16 + ht: bpar * 16 + ht + 1],
                                   scale=GATES[e], inc=("act", 1))
                relu_last = hp_prev[reg]
                relu_of[ht] = relu_last
            for stl in range(4):
                st = sc * 4 + stl
                for half in range(2):
                    r = (stl * 2 + half) % 3
                    poff = r * 512
                    W("tensor", "lw2", w2_loaded[ei])
                    W("tensor", "dve", yregion_free[r])
                    for hc in range(NHC):
                        if hc % 4 == 0:
                            W("tensor", "act", relu_of[min(hc + 3, NHC - 1)])
                        last_y_mm = mm(
                            P0[:, poff:poff + 512],
                            c1[:, hc * 512 + stl * 128: hc * 512 + (stl + 1) * 128],
                            ka[:, hc * 1024 + half * 512: hc * 1024 + half * 512 + 512],
                            start=(hc == 0), stop=(hc == NHC - 1),
                            inc=("pe", 1) if hc == NHC - 1 else None)
                    W("vector", "pe", last_y_mm)
                    W("vector", "dve", badd)
                    resid_last = dtt(
                        x_tm[:, st * 1024 + half * 512: st * 1024 + half * 512 + 512],
                        x_tm[:, st * 1024 + half * 512: st * 1024 + half * 512 + 512],
                        P0[:, poff:poff + 512], ALU.add, inc=("dve", 1))
                    yregion_free[r] = resid_last
                if ei == len(EXPERTS) - 1:
                    W("sync", "dve", resid_last)
                    dma("sync", y_d.rearrange("(t p) d -> p t d", p=128)[:, st:st + 1, :],
                        x_tm[:, st * 1024:(st + 1) * 1024].rearrange(
                            "p (a d) -> p a d", a=1), ("st", 16))
            w2_prev_groups = last_y_mm
        h_phase_done[ei + 1] = last_h_mm
        y_phase_done[ei] = last_y_mm
        b2_mm_done = last_y_mm
        b1col_readers[ei] = relu_last

    W("sync", "st", prog.cnt["st"])


_NC_CACHE = None


def kernel(L0_attn_x, L0_attn_Wq, L0_attn_bq, L0_attn_Wk, L0_attn_bk, L0_attn_Wv,
           L0_attn_bv, L0_attn_Wo, L0_attn_bo, L0_ffn_w_gate, L0_ffn_w_noise,
           L0_ffn_W1, L0_ffn_b1, L0_ffn_W2, L0_ffn_b2):
    del L0_ffn_w_gate, L0_ffn_w_noise
    global _NC_CACHE
    if _NC_CACHE is None:
        _NC_CACHE = build_program()
    nc = _NC_CACHE
    x = np.ascontiguousarray(np.asarray(L0_attn_x, dtype=np.float32))
    common = {
        "wq": np.ascontiguousarray(np.asarray(L0_attn_Wq, np.float32)),
        "wk": np.ascontiguousarray(np.asarray(L0_attn_Wk, np.float32)),
        "wv": np.ascontiguousarray(np.asarray(L0_attn_Wv, np.float32)),
        "wo": np.ascontiguousarray(np.asarray(L0_attn_Wo, np.float32)),
        "bq": np.ascontiguousarray(np.asarray(L0_attn_bq, np.float32)),
        "bk": np.ascontiguousarray(np.asarray(L0_attn_bk, np.float32)),
        "bv": np.ascontiguousarray(np.asarray(L0_attn_bv, np.float32)),
        "bo": np.ascontiguousarray(np.asarray(L0_attn_bo, np.float32)),
        "w1": np.ascontiguousarray(np.asarray(L0_ffn_W1, np.float32)),
        "b1": np.ascontiguousarray(np.asarray(L0_ffn_b1, np.float32)),
        "w2": np.ascontiguousarray(np.asarray(L0_ffn_W2, np.float32)),
        "b2": np.ascontiguousarray(np.asarray(L0_ffn_b2, np.float32)),
    }
    in_maps = []
    for b in range(8):
        m = dict(common)
        m["x"] = np.ascontiguousarray(x[b])
        in_maps.append(m)
    res = run_bass_kernel_spmd(nc, in_maps, core_ids=list(range(8)))
    out = np.stack([res.results[b]["y"] for b in range(8)], axis=0)
    return out.astype(np.float32)
